# version 04
# baseline (speedup 1.0000x reference)
import bisect
import numpy as np
import concourse.bass as bass
import concourse.mybir as mybir
from concourse.bass_utils import run_bass_kernel_spmd
from contextlib import ExitStack

F32 = mybir.dt.float32
BF16 = mybir.dt.bfloat16
I32 = mybir.dt.int32
I8 = mybir.dt.int8
XAX = mybir.AxisListType.X
Alu = mybir.AluOpType
Act = mybir.ActivationFunctionType

S = 2048
D = 1024
ND = D // 128
FS = (slice(None), slice(None))


def _f(a, b):
    return (slice(None), slice(a, b))


def _p(a, b):
    return (slice(a, b), slice(None))


class TB:
    __slots__ = ("ap0", "name", "writer", "readers", "shape", "dtype", "tiny")

    def __init__(self, handle, name, shape, dtype):
        self.ap0 = handle.ap() if hasattr(handle, "ap") else handle
        self.name = name
        self.shape = tuple(shape)
        self.dtype = dtype
        self.writer = None
        self.readers = []
        self.tiny = len(shape) >= 2 and shape[1] <= 32

    def __getitem__(self, sl):
        return self.ap0[sl]

    @property
    def full(self):
        return self.ap0[tuple(slice(None) for _ in self.shape)]

    def i32(self, sl):
        return self.ap0.bitcast(I32)[sl]


class Lane:
    _next_id = 0

    def __init__(self, step):
        self.step = step
        self.sem = None
        self.id = Lane._next_id
        Lane._next_id += 1


NDMALANES = 6


class Stream:
    def __init__(self, kb, name, engine_key, dma):
        self.kb = kb
        self.name = name
        self.engine_key = engine_key
        self.dma = dma
        self.async_ = name == "VR"
        self.step = 16 if dma else 1
        self.n = 0
        self.inc_idx = []
        self.lanes = [Lane(self.step) for _ in range(NDMALANES if dma else 1)]

    def cover(self, idx):
        if self.dma:
            ln = self.lanes[idx % NDMALANES]
            return (ln, (idx // NDMALANES + 1) * self.step)
        ln = self.lanes[0]
        j = bisect.bisect_left(self.inc_idx, idx)
        if j < len(self.inc_idx):
            return (ln, (j + 1) * self.step)
        last_op = self.kb.last_op[self]
        assert last_op.sidx >= idx
        last_op.inc = True
        self.inc_idx.append(last_op.sidx)
        return (ln, len(self.inc_idx) * self.step)


class Op:
    __slots__ = ("fn", "stream", "sidx", "waits", "inc", "tag")

    def __init__(self, fn, stream, sidx, tag):
        self.fn = fn
        self.stream = stream
        self.sidx = sidx
        self.waits = []
        self.inc = False
        self.tag = tag


class KB:
    ENGINES = ("tensor", "vector", "scalar", "gpsimd", "sync")

    def __init__(self, nc, stack):
        self.nc = nc
        self.stack = stack
        self.eops = {e: [] for e in self.ENGINES}
        self.streams = {}
        self.last_op = {}
        self.n_ops = 0
        for name, ekey, dma in [
            ("T", "tensor", False), ("V", "vector", False), ("SC", "scalar", False),
            ("G", "gpsimd", False), ("SD", "sync", True), ("GD", "gpsimd", True),
            ("CC", "gpsimd", False), ("VR", "vector", False),
        ]:
            self.streams[name] = Stream(self, name, ekey, dma)
        self._uid = 0

    def uid(self, p="t"):
        self._uid += 1
        return f"{p}{self._uid}"

    def sb(self, name, shape, dtype=F32):
        h = self.stack.enter_context(self.nc.sbuf_tensor(name, list(shape), dtype))
        return TB(h, name, shape, dtype)

    def ps(self, name):
        h = self.stack.enter_context(self.nc.psum_tensor(name, [128, 512], F32))
        return TB(h, name, (128, 512), F32)

    def dram(self, name, shape, dtype=F32, kind="Internal", addr_space="Local"):
        h = self.nc.dram_tensor(name, list(shape), dtype, kind=kind, addr_space=addr_space)
        return TB(h, name, shape, dtype)

    def const(self, name, arr):
        h = self.nc.inline_tensor(np.ascontiguousarray(arr), name=name)
        return TB(h, name, arr.shape, None)

    def op(self, stream, fn, reads=(), writes=(), tag=""):
        if isinstance(stream, str):
            stream = self.streams[stream]
        if stream.name == "V" and any(getattr(b, "tiny", False) for b in writes):
            stream = self.streams["VR"]
        o = Op(fn, stream, stream.n, tag)
        stream.n += 1
        self.n_ops += 1
        waits = {}

        def need(src, idx):
            if src is stream and not stream.dma and not stream.async_:
                return
            if src is stream and stream.dma and (stream.n - 1 - idx) < NDMALANES:
                if (stream.n - 1) % NDMALANES == idx % NDMALANES:
                    return
            ln, v = src.cover(idx)
            if waits.get(ln, 0) < v:
                waits[ln] = v

        for b in reads:
            if b.writer is not None:
                need(*b.writer)
        for b in writes:
            if b.writer is not None:
                need(*b.writer)
            for r in b.readers:
                need(*r)
        if stream.dma:
            o.inc = True
            ln = stream.lanes[o.sidx % NDMALANES]
            prev_uses = o.sidx // NDMALANES
            if prev_uses >= 1:
                if waits.get(ln, 0) < prev_uses * stream.step:
                    waits[ln] = prev_uses * stream.step
        elif stream.async_:
            o.inc = True
            stream.inc_idx.append(o.sidx)
        o.waits = list(waits.items())
        self.eops[stream.engine_key].append(o)
        self.last_op[stream] = o
        me = (stream, o.sidx)
        for b in writes:
            b.writer = me
            b.readers = []
        for b in reads:
            b.readers.append(me)
        return o

    def dma(self, dst, dsl, src, ssl, stream="SD", tag="dma"):
        d = dst[dsl] if dsl is not None else dst.full
        s = src[ssl] if ssl is not None else src.full
        return self.op(stream, lambda e: e.dma_start(out=d, in_=s),
                       reads=[src], writes=[dst], tag=tag)

    def dma_ap(self, dst, d_ap, src, s_ap, stream="SD", tag="dma", slow=False):
        if slow:
            return self.op(stream,
                           lambda e: e.dma_start(out=d_ap, in_=s_ap,
                                                 allow_slow_non_contiguous=True),
                           reads=[src], writes=[dst], tag=tag)
        return self.op(stream, lambda e: e.dma_start(out=d_ap, in_=s_ap),
                       reads=[src], writes=[dst], tag=tag)

    def dma_acc(self, dst, dsl, src, ssl, tag="dmacc"):
        d = dst[dsl] if dsl is not None else dst.full
        s = src[ssl] if ssl is not None else src.full
        return self.op("GD", lambda e: e.dma_start(out=d, in_=s, accum_op=Alu.add),
                       reads=[src, dst], writes=[dst], tag=tag)

    def v(self, fn, reads, writes, stream="V", tag="v"):
        return self.op(stream, fn, reads=reads, writes=writes, tag=tag)

    def mm(self, psum, psl, lhsT, lsl, rhs, rsl, start, stop, tag="mm"):
        o = psum[psl] if psl is not None else psum.full
        l = lhsT[lsl] if lsl is not None else lhsT.full
        r = rhs[rsl] if rsl is not None else rhs.full
        return self.op("T", lambda e: e.matmul(o, l, r, start=start, stop=stop),
                       reads=[lhsT, rhs], writes=[psum], tag=tag)

    def finalize(self, block):
        for s in self.streams.values():
            for i, ln in enumerate(s.lanes):
                ln.sem = self.stack.enter_context(self.nc.semaphore(f"sem_{s.name}{i}"))

        def mk(ekey):
            def run(eng):
                waited = {}
                for o in self.eops[ekey]:
                    for (ln, val) in o.waits:
                        if waited.get(ln.id, 0) < val:
                            eng.wait_ge(ln.sem, val)
                            waited[ln.id] = val
                    try:
                        inst = o.fn(eng)
                    except Exception:
                        print(f"[finalize] failing op: stream={o.stream.name} "
                              f"sidx={o.sidx} tag={o.tag}")
                        raise
                    if o.inc:
                        st = o.stream
                        if st.dma:
                            ln = st.lanes[o.sidx % NDMALANES]
                            inst.then_inc(ln.sem, st.step)
                        else:
                            inst.then_inc(st.lanes[0].sem, st.step)
                if ekey == "sync":
                    for s in self.streams.values():
                        if s.dma:
                            for i, ln in enumerate(s.lanes):
                                uses = (s.n - 1 - i) // NDMALANES + 1 if s.n > i else 0
                                if uses > 0:
                                    eng.wait_ge(ln.sem, uses * s.step)
                        elif s.name == "CC" and s.inc_idx:
                            eng.wait_ge(s.lanes[0].sem, len(s.inc_idx) * s.step)
            return run

        block.tensor(mk("tensor"))
        block.vector(mk("vector"))
        block.scalar(mk("scalar"))
        block.gpsimd(mk("gpsimd"))
        block.sync(mk("sync"))


class MK:
    def __init__(self, kb: KB, n_layers, core_ids, dbg=False):
        self.kb = kb
        self.n_layers = n_layers
        self.core_ids = core_ids
        self.dbg = dbg
        self.dbg_names = []
        self._vg = 0
        self.psum = [kb.ps(f"pbank{i}") for i in range(8)]
        self._ps = 0
        self.ones = kb.sb("fx_ones", (1, 128))
        kb.dma(self.ones, None, kb.const("c_ones", np.ones((1, 128), np.float32)),
               (slice(0, 1), slice(None)))
        self.onesc = kb.sb("fx_onesc", (128, 1))
        kb.dma(self.onesc, None, kb.const("c_onesc", np.ones((128, 1), np.float32)), FS)
        self.ones65 = kb.sb("fx_ones65", (65, 128))
        kb.dma(self.ones65, None, kb.const("c_ones65", np.ones((65, 128), np.float32)), FS)
        self.ident = kb.sb("fx_ident", (128, 128))
        kb.dma(self.ident, None, kb.const("c_ident", np.eye(128, dtype=np.float32)), FS)
        self.rows = kb.sb("fx_rows", (1, 2 * S))
        self.junk = kb.sb("fx_junk", (128, 4))
        self.CVALS = [1e-5, 1e-6, 1e-8, 0.5, 0.125, -1.0, 2.0, 1.0 / 127.0, -0.5, 0.17677669529663687, 0.25,
                      1.0, 0.35355339059327373]
        self.ccols = kb.sb("fx_cc", (128, len(self.CVALS)))
        kb.dma(self.ccols, None, kb.const(
            "c_cvals", np.tile(np.asarray(self.CVALS, np.float32)[None, :], (128, 1))), FS)
        self.cols = kb.sb("fx_cols", (128, 256))
        self.cols.tiny = True
        self._col_n = 0
        self.pool = [kb.sb(f"pool{i}", (128, S)) for i in range(17)]
        self._free = list(range(17))
        self.slab = [kb.sb(f"slab{i}", (128, 4096)) for i in range(2)]
        self.cmsrow = kb.sb("fx_cms", (1, 129))
        self.xd = kb.dram("xd", (D, S))
        self.xnd = kb.dram("xnd", (D, S))
        self.attd = kb.dram("attd", (D, S))
        self.x_in = kb.dram("x_f", (D, S), kind="ExternalInput")
        self.y_out = kb.dram("y", (D, S), kind="ExternalOutput")

    def take(self, n=1):
        assert len(self._free) >= n, f"pool exhausted (want {n}, have {len(self._free)})"
        out = [self.pool[self._free.pop()] for _ in range(n)]
        return out if n > 1 else out[0]

    def rel(self, *tiles):
        for ts in tiles:
            if isinstance(ts, list):
                self.rel(*ts)
            else:
                i = int(ts.name[4:])
                assert i not in self._free
                self._free.append(i)

    def pb(self):
        b = self.psum[self._ps % 8]
        self._ps += 1
        return b

    def colslot(self, n=1):
        c = self._col_n
        self._col_n += n
        assert self._col_n <= 256
        return c

    def tap(self, name, tb, sl=None, shape=None):
        if not self.dbg:
            return
        shp = shape or tb.shape
        d = self.kb.dram("dbg_" + name, shp, kind="ExternalOutput")
        if tb.ap0.tensor.__class__.__name__.startswith("DRam") if False else str(type(tb.ap0)).find("x") >= -1:
            pass
        self.kb.dma(d, None, tb, sl, tag="tap")
        self.dbg_names.append("dbg_" + name)

    def vg(self):
        self._vg += 1
        return "V" if self._vg % 2 else "G"

    def cc(self, v, pdim=128):
        i = self.CVALS.index(v)
        return self.ccols[(slice(0, pdim), slice(i, i + 1))]

    def evac(self, out_tb, osl, psum_tb, psl, tag="ev"):
        self._vg += 1
        if self._vg % 2:
            self.kb.v(lambda e, o=out_tb[osl], i=psum_tb[psl]: e.tensor_copy(o, i),
                      reads=[psum_tb], writes=[out_tb], stream="V", tag=tag)
        else:
            self.kb.op("SC", lambda e, o=out_tb[osl], i=psum_tb[psl]: e.copy(o, i),
                       reads=[psum_tb], writes=[out_tb], tag=tag)

    def ew2(self, op, out_tb, osl, a_tb, asl, b_tb, bsl, stream=None, tag="ew"):
        st = stream or self.vg()
        self.kb.v(lambda e, o=out_tb[osl], a=a_tb[asl], b=b_tb[bsl]: e.tensor_tensor(o, a, b, op),
                  reads=[a_tb, b_tb], writes=[out_tb], stream=st, tag=tag)

    def ts(self, out_tb, osl, in_tb, isl, s1, op0, s2=None, op1=None, stream=None, tag="ts"):
        st = stream or "V"
        if s2 is None:
            self.kb.v(lambda e, o=out_tb[osl], i=in_tb[isl]: e.tensor_single_scalar(o, i, s1, op0),
                      reads=[in_tb], writes=[out_tb], stream=st, tag=tag)
        else:
            self.kb.v(lambda e, o=out_tb[osl], i=in_tb[isl]: e.tensor_scalar(o, i, s1, s2, op0, op1),
                      reads=[in_tb], writes=[out_tb], stream=st, tag=tag)

    def ts_col(self, out_tb, osl, in_tb, isl, col_tb, cslot, op0, s2=None, op1=None,
               stream=None, tag="tsc"):
        st = stream or "V"
        cap = col_tb[(slice(None), slice(cslot, cslot + 1))]
        if s2 is None:
            self.kb.v(lambda e, o=out_tb[osl], i=in_tb[isl]: e.tensor_single_scalar(o, i, cap, op0),
                      reads=[in_tb, col_tb], writes=[out_tb], stream=st, tag=tag)
        else:
            self.kb.v(lambda e, o=out_tb[osl], i=in_tb[isl]: e.tensor_scalar(o, i, cap, s2, op0, op1),
                      reads=[in_tb, col_tb], writes=[out_tb], stream=st, tag=tag)

    def stt_col(self, out_tb, osl, in0_tb, i0sl, col_tb, cslot, in1_tb, i1sl, op0, op1,
                stream=None, tag="stt"):
        st = stream or "V"
        cap = col_tb[(slice(None), slice(cslot, cslot + 1))]
        self.kb.v(lambda e, o=out_tb[osl], a=in0_tb[i0sl], b=in1_tb[i1sl]:
                  e.scalar_tensor_tensor(o, a, cap, b, op0, op1),
                  reads=[in0_tb, in1_tb, col_tb], writes=[out_tb], stream=st, tag=tag)

    def stt(self, out_tb, osl, in0_tb, i0sl, scalar, in1_tb, i1sl, op0, op1,
            stream=None, tag="stt"):
        st = stream or "V"
        self.kb.v(lambda e, o=out_tb[osl], a=in0_tb[i0sl], b=in1_tb[i1sl]:
                  e.scalar_tensor_tensor(o, a, scalar, b, op0, op1),
                  reads=[in0_tb, in1_tb], writes=[out_tb], stream=st, tag=tag)

    def sc_settle(self, out_tb, osl, pdim=128, tag="stl"):
        last = (slice(0, min(pdim, 128)),
                slice(osl[1].stop - 1 if osl[1].stop is not None else out_tb.shape[1] - 1,
                      osl[1].stop if osl[1].stop is not None else out_tb.shape[1]))
        self.kb.op("SC", lambda e, i=out_tb[last]:
                   e.activation(self.junk[(slice(0, min(pdim, 128)), slice(0, 1))], i,
                                Act.Identity),
                   reads=[out_tb], writes=[self.junk], tag=tag)

    def _act_needs_settle(self, out_tb, osl, in_tb, isl):
        w = (osl[1].stop - (osl[1].start or 0)) if osl[1].stop is not None else out_tb.shape[1]
        return w <= 64 or (out_tb is in_tb and osl == isl)

    def act(self, out_tb, osl, in_tb, isl, func, bias=0.0, scale=1.0, pdim=128, tag="act"):
        reads = [in_tb]
        if isinstance(bias, float) and bias not in (0.0,):
            bias = self.cc(bias, pdim)
            reads.append(self.ccols)
        if isinstance(scale, float) and scale not in (1.0,):
            scale = self.cc(scale, pdim)
            reads.append(self.ccols)
        self.kb.op("SC", lambda e, o=out_tb[osl], i=in_tb[isl], b=bias, s=scale:
                   e.activation(o, i, func, bias=b, scale=s),
                   reads=reads, writes=[out_tb], tag=tag)
        if self._act_needs_settle(out_tb, osl, in_tb, isl):
            self.sc_settle(out_tb, osl, pdim=pdim, tag=tag + "s")

    def act_col(self, out_tb, osl, in_tb, isl, func, bias_cs=None, scale_cs=None,
                accum_cs=None, col_tb=None, tag="actc"):
        ct = col_tb if col_tb is not None else self.cols
        kw = {}
        reads = [in_tb]
        outs = [out_tb]
        bias = 0.0 if bias_cs is None else ct[(slice(None), slice(bias_cs, bias_cs + 1))]
        scale = 1.0 if scale_cs is None else ct[(slice(None), slice(scale_cs, scale_cs + 1))]
        if bias_cs is not None or scale_cs is not None:
            reads.append(ct)
        if accum_cs is not None:
            kw["accum_out"] = ct[(slice(None), slice(accum_cs, accum_cs + 1))]
            outs.append(ct)
        self.kb.op("SC", lambda e, o=out_tb[osl], i=in_tb[isl], b=bias, s=scale, kw=kw:
                   e.activation(o, i, func, bias=b, scale=s, **kw),
                   reads=reads, writes=outs, tag=tag)
        if accum_cs is not None:
            self.sc_settle(ct, (slice(None), slice(accum_cs, accum_cs + 1)), tag=tag + "s")
        elif self._act_needs_settle(out_tb, osl, in_tb, isl):
            self.sc_settle(out_tb, osl, tag=tag + "s")

    def bcast_row(self, row_off, n, out_tb, out_off, row_tb=None, tag="bc"):
        kb = self.kb
        rt = row_tb if row_tb is not None else self.rows
        for c0 in range(0, n, 512):
            w = min(512, n - c0)
            p = self.pb()
            kb.mm(p, (slice(None), slice(0, w)), self.ones, (slice(0, 1), slice(None)),
                  rt, (slice(0, 1), slice(row_off + c0, row_off + c0 + w)),
                  start=True, stop=True, tag=tag)
            self.evac(out_tb, _f(out_off + c0, out_off + c0 + w), p, _f(0, w), tag=tag + "e")

    def row_to_cols(self, row_off, n, cslot, tag="r2c"):
        kb = self.kb
        d = kb.dram(kb.uid("r2c"), (n,))
        kb.dma_ap(d, d.ap0[(slice(None),)],
                  self.rows, self.rows.ap0[(slice(0, 1), slice(row_off, row_off + n))],
                  stream="GD", tag=tag + "w")
        nj = n // 128
        kb.dma_ap(self.cols, self.cols.ap0[(slice(None), slice(cslot, cslot + nj))],
                  d, d.ap0.rearrange("(j p) -> p j", p=128),
                  stream="GD", tag=tag + "r", slow=True)

    def cols_to_row(self, cslot, nj, row_off, tag="c2r"):
        kb = self.kb
        d = kb.dram(kb.uid("c2r"), (nj * 128,))
        kb.dma_ap(d, d.ap0.rearrange("(j p) -> p j", p=128),
                  self.cols, self.cols.ap0[(slice(None), slice(cslot, cslot + nj))],
                  stream="GD", tag=tag + "w", slow=True)
        kb.dma_ap(self.rows, self.rows.ap0[(slice(0, 1), slice(row_off, row_off + nj * 128))],
                  d, d.ap0[(slice(None),)], stream="GD", tag=tag + "r")

    def transpose128(self, in_tb, isl, out_tb, osl, pr=128, pc=128, kp=128, tag="tr"):
        kb = self.kb
        p = self.pb()
        kb.op("T", lambda e: e.transpose(p[(slice(0, pc), slice(0, pr))], in_tb[isl],
                                         self.ident[(slice(0, kp), slice(0, kp))]),
              reads=[in_tb, self.ident], writes=[p], tag=tag)
        self.evac(out_tb, osl, p, (slice(0, pc), slice(0, pr)), tag=tag + "e")

    def colsum(self, tiles, out_off, scale, sq=False, tag="cs"):
        kb = self.kb
        sqt = self.take() if sq else None
        for c0 in range(0, S, 512):
            p = self.pb()
            for k, t in enumerate(tiles):
                if sq:
                    dsl = _f((k % 4) * 512, (k % 4) * 512 + 512)
                    self.act(sqt, dsl, t, _f(c0, c0 + 512), Act.Square, tag=tag + "sq")
                    rt, rsl = sqt, dsl
                else:
                    rt, rsl = t, _f(c0, c0 + 512)
                kb.mm(p, (slice(0, 1), slice(None)), self.onesc, FS,
                      rt, rsl, start=(k == 0), stop=(k == len(tiles) - 1), tag=tag)
            self.kb.v(lambda e, o=self.rows[(slice(0, 1), slice(out_off + c0, out_off + c0 + 512))],
                      i=p[(slice(0, 1), slice(0, 512))], sc=scale: e.tensor_scalar_mul(o, i, sc),
                      reads=[p], writes=[self.rows], stream="V", tag=tag + "ev")
        if sq:
            self.rel(sqt)

    def ln(self, src, dst, eps, tag, g_cs=None, b_cs=None, store=None):
        kb = self.kb
        MU, RS = 0, S
        self.colsum(src, MU, 1.0 / D, sq=False, tag=tag + "mu")
        self.colsum(src, RS, 1.0 / D, sq=True, tag=tag + "ss")
        mus = (slice(0, 1), slice(MU, MU + S))
        rss = (slice(0, 1), slice(RS, RS + S))
        mub = self.take()
        rsb = self.take()
        t2 = (slice(0, 1), slice(0, S))
        self.ew2(Alu.mult, mub, t2, self.rows, mus, self.rows, mus, stream="V",
                 tag=tag + "m2")
        self.ew2(Alu.subtract, self.rows, rss, self.rows, rss, mub, t2, stream="V",
                 tag=tag + "var")
        self.act(self.rows, rss, self.rows, rss, Act.Sqrt, bias=float(eps), pdim=1, tag=tag + "sd")
        kb.v(lambda e: e.reciprocal(self.rows[rss], self.rows[rss]),
             reads=[self.rows], writes=[self.rows], stream="VR", tag=tag + "rc")
        self.bcast_row(MU, S, mub, 0, tag=tag + "bm")
        self.bcast_row(RS, S, rsb, 0, tag=tag + "br")
        for k in range(ND):
            st = self.vg()
            d = dst[k % len(dst)] if store is not None else dst[k]
            self.ew2(Alu.subtract, d, FS, src[k], FS, mub, FS, stream=st, tag=tag + "s")
            self.ew2(Alu.mult, d, FS, d, FS, rsb, FS, stream=st, tag=tag + "m")
            if g_cs is not None:
                self.act_col(d, FS, d, FS, Act.Identity, bias_cs=b_cs + k,
                             scale_cs=g_cs + k, tag=tag + "aff")
            if store is not None:
                self.kb.dma(store, _p(k * 128, (k + 1) * 128), d, FS, tag=tag + "st")
        self.rel(mub, rsb)

    def load_col(self, dram_tb, row0, cslot, n=1, tag="lc"):
        for j in range(n):
            self.kb.dma(self.cols, (slice(None), slice(cslot + j, cslot + j + 1)),
                        dram_tb, (slice(row0 + j * 128, row0 + (j + 1) * 128), slice(0, 1)),
                        tag=tag)

def _fq_round(mk, out_tb, osl, src_tb, ssl, si_cs, scr_tb, scrsl, lo, hi, tag="fq"):
    kb = mk.kb
    if si_cs is None:
        kb.v(lambda e, o=scr_tb.i32(scrsl), i=src_tb[ssl]: e.tensor_copy(o, i),
             reads=[src_tb], writes=[scr_tb], stream="V", tag=tag + "q")
        kb.v(lambda e, o=out_tb[osl], i=scr_tb.i32(scrsl): e.tensor_copy(o, i),
             reads=[scr_tb], writes=[out_tb], stream="V", tag=tag + "r0")
        kb.v(lambda e, o=scr_tb[scrsl], a=src_tb[ssl], b=out_tb[osl]:
             e.tensor_sub(o, a, b),
             reads=[src_tb, out_tb], writes=[scr_tb], stream="V", tag=tag + "d")
    else:
        cap = mk.cols[(slice(None), slice(si_cs, si_cs + 1))]
        kb.v(lambda e, o=scr_tb.i32(scrsl), i=src_tb[ssl]: e.tensor_single_scalar(o, i, cap, Alu.mult),
             reads=[src_tb, mk.cols], writes=[scr_tb], stream="V", tag=tag + "q")
        kb.v(lambda e, o=out_tb[osl], i=scr_tb.i32(scrsl): e.tensor_copy(o, i),
             reads=[scr_tb], writes=[out_tb], stream="V", tag=tag + "r0")
        kb.v(lambda e, o=scr_tb[scrsl], a=src_tb[ssl], b=out_tb[osl]:
             e.scalar_tensor_tensor(o, a, cap, b, Alu.mult, Alu.subtract),
             reads=[src_tb, out_tb, mk.cols], writes=[scr_tb], stream="V", tag=tag + "d")
    mk.stt(out_tb, osl, scr_tb, scrsl, 0.5, out_tb, osl, Alu.is_gt, Alu.add, tag=tag + "c1")
    mk.stt(out_tb, osl, scr_tb, scrsl, -0.5, out_tb, osl, Alu.is_ge, Alu.add, tag=tag + "c2")
    mk.ts(out_tb, osl, out_tb, osl, hi + 1.0, Alu.min, lo + 1.0, Alu.max, tag=tag + "cl")
    mk.ts(out_tb, osl, out_tb, osl, -1.0, Alu.add, tag=tag + "m1")


def _colmax_to_scalar(mk, cslot, out_cslot, tag="cms"):
    kb = mk.kb
    p = mk.pb()
    kb.op("T", lambda e: e.transpose(p[(slice(0, 1), slice(0, 128))],
                                     mk.cols[(slice(None), slice(cslot, cslot + 1))],
                                     mk.ident.full),
          reads=[mk.cols, mk.ident], writes=[p], tag=tag + "t")
    row = mk.cmsrow
    kb.v(lambda e: e.tensor_copy(row[(slice(0, 1), slice(0, 128))],
                                 p[(slice(0, 1), slice(0, 128))]),
         reads=[p], writes=[row], stream="V", tag=tag + "e")
    kb.v(lambda e: e.reduce_max(row[(slice(0, 1), slice(128, 129))],
                                row[(slice(0, 1), slice(0, 128))], axis=XAX),
         reads=[row], writes=[row], stream="VR", tag=tag + "r")
    return row


def _allreduce_max(mk, row_tb, row_off, out_cslot, tag="cc"):
    kb = mk.kb
    cin = kb.dram(kb.uid("ccin"), (1, 1))
    cout = kb.dram(kb.uid("ccout"), (1, 1))
    kb.dma(cin, (slice(0, 1), slice(0, 1)), row_tb,
           (slice(0, 1), slice(row_off, row_off + 1)), stream="GD", tag=tag + "w")
    kb.op("CC", lambda e: e.collective_compute(
        "AllReduce", Alu.max, ins=[cin.full], outs=[cout.full],
        replica_groups=[mk.core_ids]), reads=[cin], writes=[cout], tag=tag)
    kb.dma_ap(mk.cols, mk.cols.ap0[(slice(None), slice(out_cslot, out_cslot + 1))],
              cout, cout.ap0[(slice(0, 1), slice(0, 1))].to_broadcast([128, 1]),
              stream="SD", tag=tag + "b")


MK.fq = _fq_round
MK.colmax_to_scalar = _colmax_to_scalar
MK.allreduce_max = _allreduce_max


def emit_layer0(mk: MK, x8):
    kb = mk.kb
    mk._col_n = 0
    fc1_q = kb.dram("L0_fc1_q", (D, 4096), I32, kind="ExternalInput")
    fc1_sb = kb.dram("L0_fc1_sb", (D, 4096), kind="ExternalInput")
    fc1_b = kb.dram("L0_fc1_b", (4096, 1), kind="ExternalInput")
    fc2_q = kb.dram("L0_fc2_q", (4096, D), I32, kind="ExternalInput")
    fc2_sb = kb.dram("L0_fc2_sb", (4096, D), kind="ExternalInput")
    fc2_b = kb.dram("L0_fc2_b", (D, 1), kind="ExternalInput")
    ln_g = kb.dram("L0_ln_g", (D, 1), kind="ExternalInput")
    ln_b = kb.dram("L0_ln_b", (D, 1), kind="ExternalInput")
    etd = kb.dram("L0_etd", (S, S))
    xtd = kb.dram("L0_xtd", (S, D))
    hd = kb.dram("L0_hd", (D, S))
    w1fd = kb.dram("L0_w1f", (D, 4096))
    w2fd = kb.dram("L0_w2f", (4096, D))

    for k in range(ND):
        kb.dma(mk.xd, _p(k * 128, (k + 1) * 128), x8[k], FS, tag="x_st")
    xn8 = x8
    mk.ln(x8, xn8, 1e-5, "l0ln")
    for k in range(ND):
        kb.dma(mk.xnd, _p(k * 128, (k + 1) * 128), xn8[k], FS, tag="xn_st")

    g_cs = mk.colslot(8)
    b_cs = mk.colslot(8)
    mk.load_col(ln_g, 0, g_cs, 8, tag="l0g")
    mk.load_col(ln_b, 0, b_cs, 8, tag="l0b")
    h2 = mk.take(2)
    mk.ln(xn8, h2, 1e-5, "l0ln2", g_cs=g_cs, b_cs=b_cs, store=hd)
    mk.rel(h2)

    NRM = 0
    mk.colsum(xn8, NRM, 1.0, sq=True, tag="a0n")
    nsl = (slice(0, 1), slice(NRM, NRM + S))
    mk.act(mk.rows, nsl, mk.rows, nsl, Act.Sqrt, pdim=1, tag="a0sq")
    kb.v(lambda e: e.reciprocal(mk.rows[nsl], mk.rows[nsl]), reads=[mk.rows],
         writes=[mk.rows], stream="VR", tag="a0rc")
    rb = mk.take()
    mk.bcast_row(NRM, S, rb, 0, tag="a0rb")
    r_cs = mk.colslot(16)
    mk.row_to_cols(NRM, S, r_cs, tag="a0r2c")
    m_cs = mk.colslot(16)
    s_cs = mk.colslot(16)

    sim = mk.take()
    et = mk.take()
    for a in range(16):
        for c0 in range(0, S, 512):
            p = mk.pb()
            for k in range(ND):
                kb.mm(p, (slice(None), slice(0, 512)),
                      xn8[k], _f(a * 128, (a + 1) * 128),
                      xn8[k], _f(c0, c0 + 512),
                      start=(k == 0), stop=(k == ND - 1), tag="a0S")
            fsl = _f(c0, c0 + 512)
            mk.ts_col(sim, fsl, rb, fsl, mk.cols, r_cs + a, Alu.mult, 1e8, Alu.min,
                      stream="V", tag="a0f")
            mk.kb.v(lambda e, o=sim[fsl], i=p[(slice(None), slice(0, 512))]:
                    e.tensor_mul(o, o, i), reads=[p, sim], writes=[sim], stream="V",
                    tag="a0sm")
        kb.v(lambda e, o=mk.cols[(slice(None), slice(m_cs + a, m_cs + a + 1))], i=sim.full:
             e.reduce_max(o, i, axis=XAX, negate=True),
             reads=[sim], writes=[mk.cols], stream="VR", tag="a0mx")
        mk.act_col(sim, FS, sim, FS, Act.Exp, bias_cs=m_cs + a, accum_cs=s_cs + a, tag="a0e")
        for b in range(16):
            mk.transpose128(sim, _f(b * 128, (b + 1) * 128), et, _f(b * 128, (b + 1) * 128),
                            tag="a0tr")
        kb.dma_ap(etd, etd.ap0[:, a * 128:(a + 1) * 128].rearrange("(b p) i -> p b i", p=128),
                  et, et.ap0.rearrange("p (b i) -> p b i", i=128), tag="a0etst")
    xtk = mk.take()
    for t in range(16):
        for k in range(ND):
            mk.transpose128(xn8[k], _f(t * 128, (t + 1) * 128), xtk,
                            _f(k * 128, (k + 1) * 128), tag="a0xt")
        kb.dma(xtd, _p(t * 128, (t + 1) * 128), xtk, _f(0, D), tag="a0xtst")
    mk.rel(sim, et, xtk, rb, xn8)

    xtok = mk.take(8)
    for u in range(8):
        kb.dma_ap(xtok[u], xtok[u].ap0.rearrange("p (c d) -> p c d", d=D),
                  xtd, xtd.ap0[2 * u * 128:(2 * u + 2) * 128, :].rearrange(
                      "(c p) d -> p c d", p=128), tag="a0xtld")
    att8 = mk.take(8)
    etb = mk.take()
    for a in range(16):
        kb.dma_ap(etb, etb.ap0.rearrange("p (b i) -> p b i", i=128),
                  etd, etd.ap0[:, a * 128:(a + 1) * 128].rearrange("(b p) i -> p b i", p=128),
                  tag="a0etld")
        for m in range(ND):
            p = mk.pb()
            for j in range(16):
                kb.mm(p, (slice(None), slice(0, 128)),
                      xtok[j // 2], _f((j % 2) * D + m * 128, (j % 2) * D + (m + 1) * 128),
                      etb, _f(j * 128, (j + 1) * 128),
                      start=(j == 0), stop=(j == 15), tag="a0AV")
            mk.evac(att8[m], _f(a * 128, (a + 1) * 128), p, (slice(None), slice(0, 128)),
                    tag="a0AVe")
    mk.rel(xtok, etb)
    rs_cs = mk.colslot(16)
    kb.v(lambda e: e.reciprocal(mk.cols[(slice(None), slice(rs_cs, rs_cs + 16))],
                                mk.cols[(slice(None), slice(s_cs, s_cs + 16))]),
         reads=[mk.cols], writes=[mk.cols], stream="VR", tag="a0rs")
    mk.cols_to_row(rs_cs, 16, 0, tag="a0c2r")
    sbb = mk.take()
    mk.bcast_row(0, S, sbb, 0, tag="a0sbb")
    for m in range(ND):
        mk.ew2(Alu.mult, att8[m], FS, att8[m], FS, sbb, FS, tag="a0nm")
        kb.dma(mk.attd, _p(m * 128, (m + 1) * 128), att8[m], FS, tag="a0st")
    mk.rel(att8, sbb)

    sa, sb_ = mk.slab
    for k in range(ND):
        kb.dma(sa, FS, fc1_q, _p(k * 128, (k + 1) * 128), stream="GD", tag="f0qld")
        kb.dma(sb_, FS, fc1_sb, _p(k * 128, (k + 1) * 128), tag="f0sld")
        mk.stt(sa, FS, sa, FS, -128.0, sb_, FS, Alu.add, Alu.mult, stream="V", tag="f0dq")
        kb.dma(w1fd, _p(k * 128, (k + 1) * 128), sa, FS, tag="f0wst")
    for k2 in range(32):
        qs = _f(0, D)
        ss = _f(D, 2 * D)
        kb.dma(sa, qs, fc2_q, _p(k2 * 128, (k2 + 1) * 128), stream="GD", tag="f0q2")
        kb.dma(sa, ss, fc2_sb, _p(k2 * 128, (k2 + 1) * 128), tag="f0s2")
        mk.stt(sa, qs, sa, qs, -128.0, sa, ss, Alu.add, Alu.mult, stream="V", tag="f0dq2")
        kb.dma(w2fd, _p(k2 * 128, (k2 + 1) * 128), sa, qs, tag="f0w2st")

    b1_cs = mk.colslot(32)
    mk.load_col(fc1_b, 0, b1_cs, 32, tag="f0b1")
    b2_cs = mk.colslot(8)
    mk.load_col(fc2_b, 0, b2_cs, 8, tag="f0b2")
    b2h_cs = mk.colslot(8)
    kb.v(lambda e: e.tensor_scalar_mul(mk.cols[(slice(None), slice(b2h_cs, b2h_cs + 8))],
                                       mk.cols[(slice(None), slice(b2_cs, b2_cs + 8))], 0.5),
         reads=[mk.cols], writes=[mk.cols], stream="V", tag="f0bh")

    hch = mk.take(2)
    y4 = mk.take(8)
    wsl = mk.take(2)
    w2s = mk.take(2)
    zt = mk.take(2)
    for n in range(4):
        nsl_ = slice(n * 512, (n + 1) * 512)
        for k in range(ND):
            kb.dma(hch[k // 4], _f((k % 4) * 512, (k % 4) * 512 + 512),
                   hd, (slice(k * 128, (k + 1) * 128), nsl_), tag="f0hld")
        for mi in range(32):
            ws = wsl[mi % 2]
            kb.dma_ap(ws, ws.ap0[:, 0:1024].rearrange("p (k m) -> p k m", m=128),
                      w1fd, w1fd.ap0[:, mi * 128:(mi + 1) * 128].rearrange(
                          "(k p) m -> p k m", p=128), tag="f0wld")
            p = mk.pb()
            for k in range(ND):
                kb.mm(p, (slice(None), slice(0, 512)),
                      ws, _f(k * 128, (k + 1) * 128),
                      hch[k // 4], _f((k % 4) * 512, (k % 4) * 512 + 512),
                      start=(k == 0), stop=(k == ND - 1), tag="f0mm1")
            mk.act_col(y4[mi // 4], _f((mi % 4) * 512, (mi % 4) * 512 + 512),
                       p, (slice(None), slice(0, 512)), Act.Silu, bias_cs=b1_cs + mi,
                       tag="f0silu")
        for mi in range(32):
            w2t = w2s[mi % 2]
            kb.dma(w2t, _f(0, D), w2fd, _p(mi * 128, (mi + 1) * 128), tag="f0w2ld")
            for o in range(8):
                kb.mm(mk.psum[o], (slice(None), slice(0, 512)),
                      w2t, _f(o * 128, (o + 1) * 128),
                      y4[mi // 4], _f((mi % 4) * 512, (mi % 4) * 512 + 512),
                      start=(mi == 0), stop=(mi == 31), tag="f0mm2")
        for o in range(8):
            bap = mk.cols[(slice(None), slice(b2h_cs + o, b2h_cs + o + 1))]
            kb.op("SC", lambda e, o_=zt[o // 4][_f((o % 4) * 512, (o % 4) * 512 + 512)],
                  i=mk.psum[o][(slice(None), slice(0, 512))], b=bap, s=mk.cc(0.5):
                  e.activation(o_, i, Act.Identity, bias=b, scale=s),
                  reads=[mk.psum[o], mk.cols, mk.ccols], writes=[zt[o // 4]], tag="f0zev")
        for o in range(8):
            kb.dma_acc(mk.attd, (slice(o * 128, (o + 1) * 128), nsl_),
                       zt[o // 4], _f((o % 4) * 512, (o % 4) * 512 + 512), tag="f0zacc")
    mk._ps = 0
    mk.rel(hch, y4, wsl, w2s, zt)

    xnew = mk.take(8)
    ta = mk.take()
    tb2 = mk.take()
    for m in range(ND):
        msl = _p(m * 128, (m + 1) * 128)
        kb.dma(xnew[m], FS, mk.xd, msl, tag="c0x")
        kb.dma(ta, FS, mk.attd, msl, tag="c0a")
        kb.dma(tb2, FS, mk.xnd, msl, tag="c0n")
        mk.ew2(Alu.add, xnew[m], FS, xnew[m], FS, ta, FS, stream="V", tag="c0s1")
        mk.stt(xnew[m], FS, tb2, FS, 2.0, xnew[m], FS, Alu.mult, Alu.add, stream="V",
               tag="c0s2")
    mk.rel(ta, tb2)
    return xnew



def emit_layer1(mk: MK, x8):
    kb = mk.kb
    mk._col_n = 0
    WqT = kb.dram("L1_WqT", (D, D), kind="ExternalInput")
    WkT = kb.dram("L1_WkT", (D, D), kind="ExternalInput")
    WvT = kb.dram("L1_WvT", (D, D), kind="ExternalInput")
    WpT = kb.dram("L1_WpT", (D, D), kind="ExternalInput")
    bp = kb.dram("L1_bp", (D, 1), kind="ExternalInput")
    bng = kb.dram("L1_bng", (D, 1), kind="ExternalInput")
    bnb = kb.dram("L1_bnb", (D, 1), kind="ExternalInput")
    bnm = kb.dram("L1_bnm", (D, 1), kind="ExternalInput")
    bnv = kb.dram("L1_bnv", (D, 1), kind="ExternalInput")
    clip_x = kb.dram("L1_clip_x", (1, 1), kind="ExternalInput")
    clip_d = kb.dram("L1_clip_d", (1, 1), kind="ExternalInput")
    I1 = 2816
    NI = I1 // 128
    wgN = kb.dram("L1_wgN", (I1, D), kind="ExternalInput")
    wuN = kb.dram("L1_wuN", (I1, D), kind="ExternalInput")
    wdN = kb.dram("L1_wdN", (D, I1), kind="ExternalInput")
    wgT = kb.dram("L1_wgT", (D, I1), kind="ExternalInput")
    wuT = kb.dram("L1_wuT", (D, I1), kind="ExternalInput")
    wdT = kb.dram("L1_wdT", (I1, D), kind="ExternalInput")
    iwg_d = kb.dram("L1_iwg", (D, I1))
    iwu_d = kb.dram("L1_iwu", (D, I1))
    iwd_d = kb.dram("L1_iwd", (I1, D))
    xrd = kb.dram("L1_xrd", (D, 512))
    hmd = kb.dram("L1_hmd", (I1, S))
    ihd = kb.dram("L1_ihd", (I1, S))

    CS = lambda a, n=1: (slice(None), slice(a, a + n))

    for k in range(ND):
        kb.dma(mk.xd, _p(k * 128, (k + 1) * 128), x8[k], FS, tag="l1x_st")
    xn8 = x8
    mk.ln(x8, xn8, 1e-5, "l1ln")
    for k in range(ND):
        kb.dma(mk.xnd, _p(k * 128, (k + 1) * 128), xn8[k], FS, tag="l1xn_st")

    gm_cs = mk.colslot(1)
    gtmp = kb.sb(kb.uid("gm"), (128, 1))
    for k in range(ND):
        if k == 0:
            kb.v(lambda e: e.tensor_reduce(mk.cols[CS(gm_cs)], xn8[0].full, axis=XAX,
                                           op=Alu.max, apply_absolute_value=True),
                 reads=[xn8[0]], writes=[mk.cols], stream="VR", tag="l1gm")
        else:
            kb.v(lambda e, i=xn8[k].full: e.tensor_reduce(gtmp.full, i, axis=XAX,
                                                          op=Alu.max, apply_absolute_value=True),
                 reads=[xn8[k]], writes=[gtmp], stream="VR", tag="l1gm")
            kb.v(lambda e: e.tensor_tensor(mk.cols[CS(gm_cs)], mk.cols[CS(gm_cs)],
                                           gtmp.full, Alu.max),
                 reads=[gtmp, mk.cols], writes=[mk.cols], stream="V", tag="l1gmx")
    grow = mk.colmax_to_scalar(gm_cs, None, tag="l1cms")
    gx_cs = mk.colslot(1)
    mk.allreduce_max(grow, 128, gx_cs, tag="l1cc1")
    sx_cs = mk.colslot(1)
    six_cs = mk.colslot(1)
    kb.dma_ap(mk.cols, mk.cols.ap0[CS(sx_cs)],
              clip_x, clip_x.ap0[(slice(0, 1), slice(0, 1))].to_broadcast([128, 1]),
              stream="SD", tag="l1clipx")
    mk.act_col(mk.cols, CS(sx_cs), mk.cols, CS(sx_cs), Act.Sigmoid, tag="l1sig")
    kb.v(lambda e: e.tensor_mul(mk.cols[CS(sx_cs)], mk.cols[CS(sx_cs)], mk.cols[CS(gx_cs)]),
         reads=[mk.cols], writes=[mk.cols], stream="V", tag="l1sx1")
    mk.ts(mk.cols, CS(sx_cs), mk.cols, CS(sx_cs), 1.0 / 127.0, Alu.mult, stream="V", tag="l1sx2")
    kb.v(lambda e: e.reciprocal(mk.cols[CS(six_cs)], mk.cols[CS(sx_cs)]),
         reads=[mk.cols], writes=[mk.cols], stream="VR", tag="l1six")

    bnr = mk.colslot(8)
    bnbi = mk.colslot(8)
    tmpc = mk.colslot(8)
    mk.load_col(bnv, 0, bnr, 8, tag="l1bnv")
    mk.load_col(bng, 0, tmpc, 8, tag="l1bng")
    mk.ts(mk.cols, CS(bnr, 8), mk.cols, CS(bnr, 8), 1e-6, Alu.add, stream="V", tag="l1bn1")
    mk.act_col(mk.cols, CS(bnr, 8), mk.cols, CS(bnr, 8), Act.Sqrt, tag="l1bn2")
    kb.v(lambda e: e.reciprocal(mk.cols[CS(bnr, 8)], mk.cols[CS(bnr, 8)]),
         reads=[mk.cols], writes=[mk.cols], stream="VR", tag="l1bn3")
    kb.v(lambda e: e.tensor_mul(mk.cols[CS(bnr, 8)], mk.cols[CS(bnr, 8)], mk.cols[CS(tmpc, 8)]),
         reads=[mk.cols], writes=[mk.cols], stream="V", tag="l1bn4")
    mk.load_col(bnm, 0, tmpc, 8, tag="l1bnm")
    kb.v(lambda e: e.tensor_mul(mk.cols[CS(bnbi, 8)], mk.cols[CS(tmpc, 8)], mk.cols[CS(bnr, 8)]),
         reads=[mk.cols], writes=[mk.cols], stream="V", tag="l1bn5")
    mk.load_col(bnb, 0, tmpc, 8, tag="l1bnb")
    kb.v(lambda e: e.tensor_sub(mk.cols[CS(bnbi, 8)], mk.cols[CS(tmpc, 8)], mk.cols[CS(bnbi, 8)]),
         reads=[mk.cols], writes=[mk.cols], stream="V", tag="l1bn6")

    mk.tap("l1cols", mk.cols)
    xr2 = mk.take(2)
    for k in range(ND):
        xt = xr2[k // 4]
        osl = (slice(None), slice((k % 4) * 512, (k % 4) * 512 + 512))
        aps = [xn8[k].ap0[(slice(None), slice(j, S, 4))] for j in range(4)]
        st = mk.vg()
        kb.v(lambda e, o=xt[osl], a=aps[0], b=aps[1]: e.tensor_add(o, a, b),
             reads=[xn8[k]], writes=[xt], stream=st, tag="l1p1")
        kb.v(lambda e, o=xt[osl], a=aps[2]: e.tensor_add(o, o, a),
             reads=[xn8[k], xt], writes=[xt], stream=st, tag="l1p2")
        kb.v(lambda e, o=xt[osl], a=aps[3]: e.tensor_add(o, o, a),
             reads=[xn8[k], xt], writes=[xt], stream=st, tag="l1p3")
        mk.ts(xt, osl, xt, osl, 0.25, Alu.mult, stream=st, tag="l1p4")
        if k == 0:
            mk.tap("l1xraw", xt, sl=osl, shape=(128, 512))
        mk.act_col(xt, osl, xt, osl, Act.Identity, bias_cs=bnbi + k, scale_cs=bnr + k,
                   tag="l1bna")
        kb.dma(xrd, _p(k * 128, (k + 1) * 128), xr2[k // 4], osl, tag="l1xr_st")
    mk.rel(xr2)

    mk.tap("l1xrd", xrd)
    q8 = mk.take(8)
    wsl = mk.take(1)
    for m in range(ND):
        half = (m % 2) * 1024
        kb.dma_ap(wsl, wsl.ap0[(slice(None), slice(half, half + 1024))].rearrange(
            "p (k m) -> p k m", m=128),
            WqT, WqT.ap0[:, m * 128:(m + 1) * 128].rearrange("(k p) m -> p k m", p=128),
            tag="l1wq")
        for n in range(4):
            p = mk.pb()
            for k in range(ND):
                kb.mm(p, (slice(None), slice(0, 512)),
                      wsl, _f(half + k * 128, half + (k + 1) * 128),
                      xn8[k], _f(n * 512, (n + 1) * 512),
                      start=(k == 0), stop=(k == ND - 1), tag="l1qmm")
            mk.evac(q8[m], _f(n * 512, (n + 1) * 512), p, (slice(None), slice(0, 512)),
                    tag="l1qev")
    mk.rel(xn8, wsl)

    for m in range(ND):
        mk.tap(f"l1q{m}", q8[m])
    xr2 = mk.take(2)
    for k in range(ND):
        kb.dma(xr2[k // 4], (slice(None), slice((k % 4) * 512, (k % 4) * 512 + 512)),
               xrd, _p(k * 128, (k + 1) * 128), tag="l1xr_ld")
    ktok = mk.take(2)
    vtok = mk.take(2)
    for wT, dst, tg in [(WkT, ktok, "l1k"), (WvT, vtok, "l1v")]:
        for c in range(ND):
            kb.dma(mk.slab[c // 4], (slice(None), slice((c % 4) * 1024, (c % 4) * 1024 + 1024)),
                   wT, _p(c * 128, (c + 1) * 128), tag=tg + "w")
        for tb in range(4):
            for j2 in range(2):
                p = mk.pb()
                for c in range(ND):
                    kb.mm(p, (slice(None), slice(0, 512)),
                          xr2[c // 4], _f((c % 4) * 512 + tb * 128,
                                          (c % 4) * 512 + (tb + 1) * 128),
                          mk.slab[c // 4], _f((c % 4) * 1024 + j2 * 512,
                                              (c % 4) * 1024 + (j2 + 1) * 512),
                          start=(c == 0), stop=(c == ND - 1), tag=tg)
                mk.evac(dst[tb // 2], _f((tb % 2) * 1024 + j2 * 512,
                                         (tb % 2) * 1024 + (j2 + 1) * 512),
                        p, (slice(None), slice(0, 512)), tag=tg + "e")
    for u in range(2):
        mk.tap(f"l1ktok{u}", ktok[u])
        mk.tap(f"l1vtok{u}", vtok[u])
    mk.rel(xr2)

    kvbd = mk.take(1)
    for g in range(ND):
        bd = (slice(None), slice((g % 2) * 1024, (g % 2) * 1024 + 128))
        kb.v(lambda e, o=kvbd[(slice(None), slice((g % 2) * 1024, (g % 2) * 1024 + 128))]:
             e.memset(o, 0.0), reads=[], writes=[kvbd], stream="V", tag="l1bz")
        for j in range(4):
            h = 4 * g + j
            p = mk.pb()
            for c in range(4):
                u, off = c // 2, (c % 2) * 1024
                kb.mm(p, (slice(0, 32), slice(0, 32)),
                      ktok[u], _f(off + h * 32, off + (h + 1) * 32),
                      vtok[u], _f(off + h * 32, off + (h + 1) * 32),
                      start=(c == 0), stop=(c == 3), tag="l1kv")
            kb.op("SC", lambda e, o=kvbd[(slice(32 * j, 32 * j + 32),
                                          slice((g % 2) * 1024 + 32 * j,
                                                (g % 2) * 1024 + 32 * j + 32))],
                  i=p[(slice(0, 32), slice(0, 32))], s=mk.cc(0.17677669529663687, 32):
                  e.activation(o, i, Act.Identity, scale=s),
                  reads=[p, mk.ccols], writes=[kvbd], tag="l1kve")
        mk.sc_settle(kvbd, (slice(None), slice((g % 2) * 1024 + 127, (g % 2) * 1024 + 128)),
                     tag="l1kvs")
        for n in range(4):
            p = mk.pb()
            kb.mm(p, (slice(None), slice(0, 512)),
                  kvbd, (slice(None), slice((g % 2) * 1024, (g % 2) * 1024 + 128)),
                  q8[g], _f(n * 512, (n + 1) * 512),
                  start=True, stop=True, tag="l1ao")
            mk.evac(q8[g], _f(n * 512, (n + 1) * 512), p, (slice(None), slice(0, 512)),
                    tag="l1aoev")
    for g in range(ND):
        mk.tap(f"l1ao{g}", q8[g])
    mk.rel(ktok, vtok, kvbd)

    bp_cs = mk.colslot(8)
    mk.load_col(bp, 0, bp_cs, 8, tag="l1bp")
    wsl = mk.take(1)
    am2 = mk.take(2)
    for m in range(ND):
        half = (m % 2) * 1024
        kb.dma_ap(wsl, wsl.ap0[(slice(None), slice(half, half + 1024))].rearrange(
            "p (k m) -> p k m", m=128),
            WpT, WpT.ap0[:, m * 128:(m + 1) * 128].rearrange("(k p) m -> p k m", p=128),
            tag="l1wp")
        at = am2[m % 2]
        for n in range(4):
            p = mk.pb()
            for k in range(ND):
                kb.mm(p, (slice(None), slice(0, 512)),
                      wsl, _f(half + k * 128, half + (k + 1) * 128),
                      q8[k], _f(n * 512, (n + 1) * 512),
                      start=(k == 0), stop=(k == ND - 1), tag="l1pmm")
            mk.act_col(at, _f(n * 512, (n + 1) * 512), p, (slice(None), slice(0, 512)),
                       Act.Identity, bias_cs=bp_cs + m, tag="l1pev")
        kb.dma(mk.attd, _p(m * 128, (m + 1) * 128), at, FS, tag="l1att_st")
    mk.tap("l1att", mk.attd)
    mk.rel(q8, wsl, am2)

    def w_scales(wN, n_i, tg):
        sc = mk.colslot(((n_i + 127) // 128))
        inv = mk.colslot(((n_i + 127) // 128))
        wt = mk.take(1) if wN.shape[1] == D else None
        for it in range(n_i // 128):
            if wN.shape[1] == D:
                buf, bsl = wt, (slice(None), slice(0, D))
                kb.dma(buf, bsl, wN, _p(it * 128, (it + 1) * 128), tag=tg + "ld")
            else:
                buf, bsl = mk.slab[0], (slice(None), slice(0, I1))
                kb.dma(buf, bsl, wN, _p(it * 128, (it + 1) * 128), tag=tg + "ld")
            kb.v(lambda e, o=mk.cols[CS(sc + it)], i=buf[bsl]:
                 e.tensor_reduce(o, i, axis=XAX, op=Alu.max, apply_absolute_value=True),
                 reads=[buf], writes=[mk.cols], stream="VR", tag=tg + "mx")
        nslots = n_i // 128
        mk.ts(mk.cols, CS(sc, nslots), mk.cols, CS(sc, nslots), 1.0 / 127.0, Alu.mult,
              stream="V", tag=tg + "sc")
        kb.v(lambda e: e.reciprocal(mk.cols[CS(inv, nslots)], mk.cols[CS(sc, nslots)]),
             reads=[mk.cols], writes=[mk.cols], stream="VR", tag=tg + "iv")
        if wt is not None:
            mk.rel(wt)
        return sc, inv

    g_sc, g_inv = w_scales(wgN, I1, "l1sg")
    u_sc, u_inv = w_scales(wuN, I1, "l1su")
    d_sc, d_inv = w_scales(wdN, D, "l1sd")

    def w_round(wT, inv_cs, n_free, iw_d, nk, tg):
        nj = n_free // 128
        mk.cols_to_row(inv_cs, nj, 0, tag=tg + "c2r")
        ib = mk.take(2)
        mk.bcast_row(0, n_free, ib[0], 0, tag=tg + "bc") if n_free <= S else None
        if n_free > S:
            mk.bcast_row(0, S, ib[0], 0, tag=tg + "bc")
            mk.bcast_row(S, n_free - S, ib[1], 0, tag=tg + "bc2")
        scr = mk.take(1)
        out = mk.take(1) if n_free <= S else mk.slab[1]
        for kkt in range(nk):
            buf = mk.slab[0]
            kb.dma(buf, (slice(None), slice(0, n_free)), wT, _p(kkt * 128, (kkt + 1) * 128),
                   tag=tg + "ld")
            for c0 in range(0, n_free, S):
                w_ = min(S, n_free - c0)
                ibt = ib[c0 // S]
                kb.v(lambda e, o=out[(slice(None), slice(c0, c0 + w_))] if n_free > S
                     else out[(slice(None), slice(0, w_))],
                     a=buf[(slice(None), slice(c0, c0 + w_))], b=ibt[(slice(None), slice(0, w_))]:
                     e.tensor_mul(o, a, b),
                     reads=[buf, ibt], writes=[out], stream="V", tag=tg + "u")
            for c0 in range(0, n_free, S):
                w_ = min(S, n_free - c0)
                osl = (slice(None), slice(c0, c0 + w_)) if n_free > S else (slice(None), slice(0, w_))
                mk.fq(buf, (slice(None), slice(c0, c0 + w_)), out, osl, None,
                      scr, (slice(None), slice(0, w_)), -128.0, 127.0, tag=tg + "fq")
            kb.dma(iw_d, _p(kkt * 128, (kkt + 1) * 128), buf, (slice(None), slice(0, n_free)),
                   tag=tg + "st")
        mk.rel(ib, scr)
        if n_free <= S:
            mk.rel(out)

    w_round(wgT, g_inv, I1, iwg_d, ND, "l1rg")
    w_round(wuT, u_inv, I1, iwu_d, ND, "l1ru")
    w_round(wdT, d_inv, D, iwd_d, NI, "l1rd")

    gs2 = mk.colslot(NI)
    us2 = mk.colslot(NI)
    kb.v(lambda e: e.tensor_single_scalar(mk.cols[CS(gs2, NI)], mk.cols[CS(g_sc, NI)],
                                          mk.cols[CS(sx_cs)], Alu.mult),
         reads=[mk.cols], writes=[mk.cols], stream="V", tag="l1gs2")
    kb.v(lambda e: e.tensor_single_scalar(mk.cols[CS(us2, NI)], mk.cols[CS(u_sc, NI)],
                                          mk.cols[CS(sx_cs)], Alu.mult),
         reads=[mk.cols], writes=[mk.cols], stream="V", tag="l1us2")

    mk.tap("l1iwg", iwg_d)
    ixq = mk.take(8)
    ld = mk.take(1)
    scr = mk.take(1)
    for k in range(ND):
        kb.dma(ld, FS, mk.xnd, _p(k * 128, (k + 1) * 128), tag="l1xq_ld")
        mk.fq(ixq[k], FS, ld, FS, six_cs, scr, FS, -128.0, 127.0, tag="l1xq")
    mk.rel(ld, scr)
    mk.tap("l1ixq0", ixq[0])

    hmx_cs = mk.colslot(1)
    kb.v(lambda e: e.memset(mk.cols[CS(hmx_cs)], 0.0), reads=[], writes=[mk.cols],
         stream="V", tag="l1hm0")
    wsl = mk.take(1)
    g6 = mk.take(6)
    tmp = mk.take(1)
    hred = kb.sb(kb.uid("hred"), (128, 1))
    for n in range(4):
        nsl_ = slice(n * 512, (n + 1) * 512)
        for it in range(NI):
            half = (it % 2) * 1024
            slot = (g6[it // 4], (slice(None), slice((it % 4) * 512, (it % 4) * 512 + 512)))
            for wdr, iw_dr, scs, isfirst in [(None, iwg_d, gs2, True), (None, iwu_d, us2, False)]:
                kb.dma_ap(wsl, wsl.ap0[(slice(None), slice(half, half + 1024))].rearrange(
                    "p (k m) -> p k m", m=128),
                    iw_dr, iw_dr.ap0[:, it * 128:(it + 1) * 128].rearrange(
                        "(k p) m -> p k m", p=128), tag="l1iwld")
                p = mk.pb()
                for k in range(ND):
                    kb.mm(p, (slice(None), slice(0, 512)),
                          wsl, _f(half + k * 128, half + (k + 1) * 128),
                          ixq[k], _f(n * 512, (n + 1) * 512),
                          start=(k == 0), stop=(k == ND - 1), tag="l1gumm")
                if isfirst:
                    mk.act_col(slot[0], slot[1], p, (slice(None), slice(0, 512)),
                               Act.Silu, scale_cs=scs + it, tag="l1gev")
                else:
                    mk.act_col(tmp, _f(0, 512), p, (slice(None), slice(0, 512)),
                               Act.Identity, scale_cs=scs + it, tag="l1uev")
            kb.v(lambda e, o=slot[0][slot[1]], b=tmp[_f(0, 512)]: e.tensor_mul(o, o, b),
                 reads=[tmp, slot[0]], writes=[slot[0]], stream="V", tag="l1h")
            kb.v(lambda e, o=hred.full, i=slot[0][slot[1]]:
                 e.tensor_reduce(o, i, axis=XAX, op=Alu.max, apply_absolute_value=True),
                 reads=[slot[0]], writes=[hred], stream="VR", tag="l1hmr")
            kb.v(lambda e: e.tensor_tensor(mk.cols[CS(hmx_cs)], mk.cols[CS(hmx_cs)],
                                           hred.full, Alu.max),
                 reads=[hred, mk.cols], writes=[mk.cols], stream="V", tag="l1hmx")
            kb.dma(hmd, (slice(it * 128, (it + 1) * 128), nsl_), slot[0], slot[1],
                   tag="l1h_st")
    mk.rel(ixq, wsl, g6, tmp)

    hrow = mk.colmax_to_scalar(hmx_cs, None, tag="l1cms2")
    hx_cs = mk.colslot(1)
    mk.allreduce_max(hrow, 128, hx_cs, tag="l1cc2")
    s2_cs = mk.colslot(1)
    si2_cs = mk.colslot(1)
    kb.dma_ap(mk.cols, mk.cols.ap0[CS(s2_cs)],
              clip_d, clip_d.ap0[(slice(0, 1), slice(0, 1))].to_broadcast([128, 1]),
              stream="SD", tag="l1clipd")
    mk.act_col(mk.cols, CS(s2_cs), mk.cols, CS(s2_cs), Act.Sigmoid, tag="l1sig2")
    kb.v(lambda e: e.tensor_mul(mk.cols[CS(s2_cs)], mk.cols[CS(s2_cs)], mk.cols[CS(hx_cs)]),
         reads=[mk.cols], writes=[mk.cols], stream="V", tag="l1s21")
    mk.ts(mk.cols, CS(s2_cs), mk.cols, CS(s2_cs), 1.0 / 127.0, Alu.mult, stream="V", tag="l1s22")
    kb.v(lambda e: e.reciprocal(mk.cols[CS(si2_cs)], mk.cols[CS(s2_cs)]),
         reads=[mk.cols], writes=[mk.cols], stream="VR", tag="l1si2")
    ds2 = mk.colslot(8)
    kb.v(lambda e: e.tensor_single_scalar(mk.cols[CS(ds2, 8)], mk.cols[CS(d_sc, 8)],
                                          mk.cols[CS(s2_cs)], Alu.mult),
         reads=[mk.cols], writes=[mk.cols], stream="V", tag="l1ds2")

    mk.tap("l1hmd", hmd)
    mk.tap("l1ffncols", mk.cols)
    qt = mk.take(2)
    scr = mk.take(1)
    for it in range(NI):
        b0, b1 = qt[0], qt[1]
        kb.dma(b0, FS, hmd, _p(it * 128, (it + 1) * 128), tag="l1ih_ld")
        mk.fq(b1, FS, b0, FS, si2_cs, scr, FS, -128.0, 127.0, tag="l1ih")
        kb.dma(ihd, _p(it * 128, (it + 1) * 128), b1, FS, tag="l1ih_st")
    mk.rel(qt, scr)

    mk.tap("l1ihd", ihd)
    ih6 = mk.take(6)
    zt = mk.take(2)
    for n in range(4):
        nsl_ = slice(n * 512, (n + 1) * 512)
        for it in range(NI):
            kb.dma(ih6[it // 4], (slice(None), slice((it % 4) * 512, (it % 4) * 512 + 512)),
                   ihd, (slice(it * 128, (it + 1) * 128), nsl_), tag="l1dh_ld")
        for o in range(8):
            kb.dma_ap(mk.slab[0], mk.slab[0].ap0[(slice(None), slice(0, NI * 128))].rearrange(
                "p (k m) -> p k m", m=128),
                iwd_d, iwd_d.ap0[:, o * 128:(o + 1) * 128].rearrange("(k p) m -> p k m", p=128),
                tag="l1dwld")
            p = mk.pb()
            for it in range(NI):
                kb.mm(p, (slice(None), slice(0, 512)),
                      mk.slab[0], _f(it * 128, (it + 1) * 128),
                      ih6[it // 4], _f((it % 4) * 512, (it % 4) * 512 + 512),
                      start=(it == 0), stop=(it == NI - 1), tag="l1dmm")
            mk.act_col(zt[o // 4], _f((o % 4) * 512, (o % 4) * 512 + 512),
                       p, (slice(None), slice(0, 512)), Act.Identity,
                       scale_cs=ds2 + o, tag="l1zev")
        for o in range(8):
            kb.dma_acc(mk.attd, (slice(o * 128, (o + 1) * 128), nsl_),
                       zt[o // 4], _f((o % 4) * 512, (o % 4) * 512 + 512), tag="l1zacc")
    mk.rel(ih6, zt)

    xnew = mk.take(8)
    ta = mk.take(1)
    for m in range(ND):
        msl = _p(m * 128, (m + 1) * 128)
        kb.dma(xnew[m], FS, mk.xd, msl, tag="l1cx")
        kb.dma(ta, FS, mk.attd, msl, tag="l1ca")
        mk.ew2(Alu.add, xnew[m], FS, xnew[m], FS, ta, FS, stream="V", tag="l1cs")
    mk.rel(ta)
    return xnew


def emit_layer2(mk: MK, x8):
    kb = mk.kb
    mk._col_n = 0
    CS = lambda a, n=1: (slice(None), slice(a, a + n))
    WqdN = kb.dram("L2_WqdN", (D, D), kind="ExternalInput")
    WkdN = kb.dram("L2_WkdN", (D, D), kind="ExternalInput")
    WvdN = kb.dram("L2_WvdN", (D, D), kind="ExternalInput")
    upqc = kb.dram("L2_upqc", (64, 32), kind="ExternalInput")
    upqe = kb.dram("L2_upqe", (64, 32), kind="ExternalInput")
    upkc = kb.dram("L2_upkc", (64, 32), kind="ExternalInput")
    upke = kb.dram("L2_upke", (64, 32), kind="ExternalInput")
    upv = kb.dram("L2_upv", (64, 64), kind="ExternalInput")
    upqeT = kb.dram("L2_upqeT", (32, 64), kind="ExternalInput")
    upkeT = kb.dram("L2_upkeT", (32, 64), kind="ExternalInput")
    WcN = kb.dram("L2_WcN", (D, D), kind="ExternalInput")
    gwT = kb.dram("L2_gwT", (D, 8), kind="ExternalInput")
    bbal = kb.dram("L2_bb", (8, 1), kind="ExternalInput")
    w1T = kb.dram("L2_w1T", (8 * D, 2048), kind="ExternalInput")
    w2T = kb.dram("L2_w2T", (8 * 2048, D), kind="ExternalInput")
    qlatd = kb.dram("L2_qlat", (D, S))
    klatd = kb.dram("L2_klat", (D, S))
    vlatd = kb.dram("L2_vlat", (D, S))
    aod = kb.dram("L2_aod", (D, S))
    combd = kb.dram("L2_combd", (8, S))
    inv = 1.0 / 10000.0 ** (np.arange(0, 32, 2, dtype=np.float64) / 32.0)
    fr = np.arange(S, dtype=np.float64)[None, :] * inv[:, None]
    cosT = (np.tile(np.cos(fr), (2, 1)) * (8.0 ** -0.5)).astype(np.float32)
    sinT = (np.tile(np.sin(fr), (2, 1)) * (8.0 ** -0.5)).astype(np.float32)
    pm = np.zeros((32, 32), np.float32)
    for i in range(16):
        pm[2 * i + 1, 2 * i] = -1.0
        pm[2 * i, 2 * i + 1] = 1.0
    cos_s = kb.sb("l2cos", (32, S))
    sin_s = kb.sb("l2sin", (32, S))
    kb.dma(cos_s, None, kb.const("c_l2cos", cosT), FS, tag="l2tc")
    kb.dma(sin_s, None, kb.const("c_l2sin", sinT), FS, tag="l2ts")
    pm_s = kb.sb("l2pm", (32, 32))
    kb.dma(pm_s, None, kb.const("c_l2pm", pm), FS, tag="l2tp")
    up_s = {}
    for nm, dr, shp in [("qc", upqc, (64, 32)), ("qe", upqe, (64, 32)),
                        ("kc", upkc, (64, 32)), ("ke", upke, (64, 32)),
                        ("v", upv, (64, 64)), ("qeT", upqeT, (32, 64)),
                        ("keT", upkeT, (32, 64))]:
        t = kb.sb("l2up_" + nm, shp)
        kb.dma(t, None, dr, FS, tag="l2up")
        up_s[nm] = t
    for nm, src in [("qpe", "qeT"), ("kpe", "keT")]:
        t = kb.sb("l2up_" + nm, (64, 32))
        p = mk.pb()
        kb.mm(p, (slice(0, 64), slice(0, 32)), up_s[src], FS, pm_s, FS,
              start=True, stop=True, tag="l2pe")
        kb.v(lambda e, o=t.full, i=p[(slice(0, 64), slice(0, 32))]: e.tensor_copy(o, i),
             reads=[p], writes=[t], stream="V", tag="l2pee")
        up_s[nm] = t

    for k in range(ND):
        kb.dma(mk.xd, _p(k * 128, (k + 1) * 128), x8[k], FS, tag="l2x_st")
    xn8 = x8
    mk.ln(x8, xn8, 1e-5, "l2ln")
    for k in range(ND):
        kb.dma(mk.xnd, _p(k * 128, (k + 1) * 128), xn8[k], FS, tag="l2xn_st")

    bb_cs = mk.colslot(1)
    kb.dma(mk.cols, (slice(0, 8), slice(bb_cs, bb_cs + 1)), bbal, (slice(0, 8), slice(0, 1)),
           tag="l2bb")
    gw_s = kb.sb("l2gw", (128, 64))
    for k in range(ND):
        kb.dma(gw_s, (slice(None), slice(k * 8, (k + 1) * 8)), gwT,
               _p(k * 128, (k + 1) * 128), tag="l2gw")
    gfm = mk.take(1)
    for n in range(4):
        p = mk.pb()
        for k in range(ND):
            kb.mm(p, (slice(0, 8), slice(0, 512)), gw_s, _f(k * 8, (k + 1) * 8),
                  xn8[k], _f(n * 512, (n + 1) * 512), start=(k == 0), stop=(k == ND - 1),
                  tag="l2gmm")
        kb.op("SC", lambda e, o=gfm[(slice(0, 8), slice(n * 512, (n + 1) * 512))],
              i=p[(slice(0, 8), slice(0, 512))],
              b=mk.cols[(slice(0, 8), slice(bb_cs, bb_cs + 1))]:
              e.activation(o, i, Act.Sigmoid, bias=b),
              reads=[p, mk.cols], writes=[gfm], tag="l2sig")
    gsc = mk.colslot(8)
    m1s = mk.colslot(1); eqs = mk.colslot(8); cnts = mk.colslot(1)
    msks = mk.colslot(8); m2as = mk.colslot(1); inds = mk.colslot(1)
    t2s = mk.colslot(1); m2s = mk.colslot(1); gms = mk.colslot(8)
    dens = mk.colslot(1); rdens = mk.colslot(1); cbs = mk.colslot(8)
    fms = mk.colslot(128)
    for t in range(16):
        tsl = slice(t * 128, (t + 1) * 128)
        p = mk.pb()
        kb.op("T", lambda e, o=p[(slice(None), slice(0, 8))], i=gfm[(slice(0, 8), tsl)]:
              e.transpose(o, i, mk.ident[(slice(0, 8), slice(0, 8))]),
              reads=[gfm, mk.ident], writes=[p], tag="l2gt")
        kb.v(lambda e, o=mk.cols[CS(gsc, 8)], i=p[(slice(None), slice(0, 8))]:
             e.tensor_copy(o, i), reads=[p], writes=[mk.cols], stream="V", tag="l2gte")
        sc_ = CS(gsc, 8)
        kb.v(lambda e: e.reduce_max(mk.cols[CS(m1s)], mk.cols[sc_], axis=XAX),
             reads=[mk.cols], writes=[mk.cols], stream="VR", tag="l2m1")
        mk.ts_col(mk.cols, CS(eqs, 8), mk.cols, sc_, mk.cols, m1s, Alu.is_equal, tag="l2eq")
        kb.v(lambda e: e.tensor_reduce(mk.cols[CS(cnts)], mk.cols[CS(eqs, 8)], axis=XAX,
                                       op=Alu.add), reads=[mk.cols], writes=[mk.cols],
             stream="VR", tag="l2cnt")
        mk.stt(mk.cols, CS(msks, 8), mk.cols, CS(eqs, 8), -1e30, mk.cols, sc_,
               Alu.mult, Alu.add, tag="l2msk")
        kb.v(lambda e: e.reduce_max(mk.cols[CS(m2as)], mk.cols[CS(msks, 8)], axis=XAX),
             reads=[mk.cols], writes=[mk.cols], stream="VR", tag="l2m2a")
        mk.ts(mk.cols, CS(inds), mk.cols, CS(cnts), 2.0, Alu.is_ge, tag="l2ind")
        mk.ew2(Alu.mult, mk.cols, CS(t2s), mk.cols, CS(inds), mk.cols, CS(m1s),
               stream="V", tag="l2t2")
        mk.ew2(Alu.max, mk.cols, CS(m2s), mk.cols, CS(m2as), mk.cols, CS(t2s),
               stream="V", tag="l2m2")
        mk.ts_col(mk.cols, CS(gms, 8), mk.cols, sc_, mk.cols, m2s, Alu.is_ge, tag="l2gm")
        mk.ew2(Alu.add, mk.cols, CS(dens), mk.cols, CS(m1s), mk.cols, CS(m2s),
               stream="V", tag="l2dn")
        mk.ts(mk.cols, CS(dens), mk.cols, CS(dens), 1e-6, Alu.add, tag="l2dn2")
        kb.v(lambda e: e.reciprocal(mk.cols[CS(rdens)], mk.cols[CS(dens)]),
             reads=[mk.cols], writes=[mk.cols], stream="VR", tag="l2rd")
        mk.ew2(Alu.mult, mk.cols, CS(cbs, 8), mk.cols, sc_, mk.cols, CS(gms, 8),
               stream="V", tag="l2cb")
        mk.ts_col(mk.cols, CS(cbs, 8), mk.cols, CS(cbs, 8), mk.cols, rdens, Alu.mult,
                  tag="l2cb2")
        p = mk.pb()
        kb.op("T", lambda e, o=p[(slice(0, 8), slice(0, 128))]:
              e.transpose(o, mk.cols[CS(cbs, 8)], mk.ident.full),
              reads=[mk.cols, mk.ident], writes=[p], tag="l2ct")
        kb.v(lambda e, o=mk.cols[(slice(0, 8), slice(fms, fms + 128))],
             i=p[(slice(0, 8), slice(0, 128))]: e.tensor_copy(o, i),
             reads=[p], writes=[mk.cols], stream="V", tag="l2cfe")
        kb.dma(combd, (slice(0, 8), tsl), mk.cols, (slice(0, 8), slice(fms, fms + 128)),
               tag="l2cst")
    mk.rel(gfm)

    wsl = mk.take(1)
    am2 = mk.take(2)
    for wN, outd, tg in [(WqdN, qlatd, "l2q"), (WkdN, klatd, "l2k"), (WvdN, vlatd, "l2v")]:
        for m in range(ND):
            half = (m % 2) * 1024
            kb.dma_ap(wsl, wsl.ap0[(slice(None), slice(half, half + 1024))].rearrange(
                "p (k m) -> p k m", m=128),
                wN, wN.ap0[:, m * 128:(m + 1) * 128].rearrange("(k p) m -> p k m", p=128),
                tag=tg + "w")
            at = am2[m % 2]
            for n in range(4):
                p = mk.pb()
                for k in range(ND):
                    kb.mm(p, (slice(None), slice(0, 512)),
                          wsl, _f(half + k * 128, half + (k + 1) * 128),
                          xn8[k], _f(n * 512, (n + 1) * 512),
                          start=(k == 0), stop=(k == ND - 1), tag=tg)
                mk.evac(at, _f(n * 512, (n + 1) * 512), p, (slice(None), slice(0, 512)),
                        tag=tg + "e")
            kb.dma(outd, _p(m * 128, (m + 1) * 128), at, FS, tag=tg + "st")
    mk.rel(wsl, am2)

    z6 = mk.take(6)
    hb = mk.take(1)
    wrot = mk.take(2)

    def zsl(mo, n):
        if mo < 6:
            return z6[mo], _f(n * 512, (n + 1) * 512)
        return mk.slab[0], _f((mo - 6) * 2048 + n * 512, (mo - 6) * 2048 + (n + 1) * 512)

    for e in range(8):
        kb.dma_ap(mk.slab[1], mk.slab[1].ap0[(slice(None), slice(0, S))],
                  combd, combd.ap0[(slice(e, e + 1), slice(None))].to_broadcast([128, S]),
                  tag="l2cbld")
        for mi in range(16):
            w = wrot[(e * 16 + mi) % 2]
            kb.dma_ap(w, w.ap0[(slice(None), slice(0, 1024))].rearrange(
                "p (k m) -> p k m", m=128),
                w1T, w1T.ap0[e * D:(e + 1) * D, mi * 128:(mi + 1) * 128].rearrange(
                    "(k p) m -> p k m", p=128), tag="l2w1")
            kb.dma(w, _f(1024, 2048), w2T,
                   _p(e * 2048 + mi * 128, e * 2048 + (mi + 1) * 128), tag="l2w2")
            for n in range(4):
                p = mk.pb()
                for k in range(ND):
                    kb.mm(p, (slice(None), slice(0, 512)),
                          w, _f(k * 128, (k + 1) * 128),
                          xn8[k], _f(n * 512, (n + 1) * 512),
                          start=(k == 0), stop=(k == ND - 1), tag="l2f1")
                mk.act(hb, _f(n * 512, (n + 1) * 512), p, (slice(None), slice(0, 512)),
                       Act.Silu, tag="l2silu")
                kb.v(lambda e_, o=hb[_f(n * 512, (n + 1) * 512)],
                     b=mk.slab[1][_f(n * 512, (n + 1) * 512)]: e_.tensor_mul(o, o, b),
                     reads=[mk.slab[1], hb], writes=[hb], stream="V", tag="l2hcb")
            for mo in range(8):
                for n in range(4):
                    p = mk.pb()
                    kb.mm(p, (slice(None), slice(0, 512)),
                          w, _f(1024 + mo * 128, 1024 + (mo + 1) * 128),
                          hb, _f(n * 512, (n + 1) * 512),
                          start=True, stop=True, tag="l2f2")
                    zt, zs = zsl(mo, n)
                    if e == 0 and mi == 0:
                        kb.v(lambda e_, o=zt[zs], i=p[(slice(None), slice(0, 512))]:
                             e_.tensor_copy(o, i), reads=[p], writes=[zt], stream="V",
                             tag="l2z0")
                    else:
                        kb.v(lambda e_, o=zt[zs], i=p[(slice(None), slice(0, 512))]:
                             e_.tensor_add(o, o, i), reads=[p, zt], writes=[zt], stream="V",
                             tag="l2za")
    for mo in range(8):
        zt, _ = zsl(mo, 0)
        src = _f(0, S) if mo < 6 else _f((mo - 6) * 2048, (mo - 6) * 2048 + S)
        kb.dma(mk.attd, _p(mo * 128, (mo + 1) * 128), zt, src, tag="l2zst")
    mk.rel(z6, hb, wrot, xn8)

    mtile = mk.take(1)
    ms = mk.take(1)
    kb.v(lambda e, o=ms[_f(0, 512)]: e.memset(o, 1.0), reads=[], writes=[ms],
         stream="V", tag="l2m1s")
    for j in range(4):
        kb.op("G", lambda e, o=mtile[_f(j * 512, (j + 1) * 512)], i=ms[_f(0, 512)],
              b=-128 * j:
              e.affine_select(o, i, pattern=[[1, 512]], compare_op=Alu.is_ge,
                              fill=0.0, base=b, channel_multiplier=-1),
              reads=[ms], writes=[mtile], tag="l2mk")
    mk.rel(ms)
    vtok = mk.take(1)
    for c in range(16):
        kb.v(lambda e, o=vtok[_f(c * 65 + 64, c * 65 + 65)]: e.memset(o, 1.0),
             reads=[], writes=[vtok], stream="VR", tag="l2vo")
    et_ab = mk.take(2)
    et2 = mk.take(1)
    ao = mk.take(1)
    ql, kl, vl = mk.take(3)
    qc, qe, qe2, kc, ke, ke2 = mk.take(6)
    vh = mk.take(1)
    P32 = (slice(0, 32), slice(None))
    for h in range(16):
        for lt, src in [(ql, qlatd), (kl, klatd), (vl, vlatd)]:
            kb.dma(lt, (slice(0, 64), slice(None)), src, _p(h * 64, (h + 1) * 64),
                   tag="l2lat")
        for dst, ups, srct, scaled in [
                (qc, "qc", ql, True), (qe, "qe", ql, False), (qe2, "qpe", ql, False),
                (kc, "kc", kl, True), (ke, "ke", kl, False), (ke2, "kpe", kl, False)]:
            for n in range(4):
                p = mk.psum[n % 4]
                kb.mm(p, (slice(0, 32), slice(0, 512)), up_s[ups], None,
                      srct, (slice(0, 64), slice(n * 512, (n + 1) * 512)),
                      start=True, stop=True, tag="l2up")
                osl = (slice(0, 32), slice(n * 512, (n + 1) * 512))
                if scaled:
                    kb.op("SC", lambda e, o=dst[osl], i=p[(slice(0, 32), slice(0, 512))],
                          s=mk.cc(0.35355339059327373, 32):
                          e.activation(o, i, Act.Identity, scale=s),
                          reads=[p, mk.ccols], writes=[dst], tag="l2upc")
                else:
                    kb.v(lambda e, o=dst[osl], i=p[(slice(0, 32), slice(0, 512))]:
                         e.tensor_copy(o, i), reads=[p], writes=[dst], stream="V",
                         tag="l2upe")
        for n in range(4):
            p = mk.psum[n % 4]
            kb.mm(p, (slice(0, 64), slice(0, 512)), up_s["v"], None,
                  vl, (slice(0, 64), slice(n * 512, (n + 1) * 512)),
                  start=True, stop=True, tag="l2vmm")
            mk.evac(vh, (slice(0, 64), slice(n * 512, (n + 1) * 512)),
                    p, (slice(0, 64), slice(0, 512)), tag="l2ve")
        for a, b, s1, s2 in [(qe, qe2, "V", "G"), (ke, ke2, "G", "V")]:
            mk.ew2(Alu.mult, a, P32, a, P32, cos_s, FS, stream=s1, tag="l2r1")
            mk.ew2(Alu.mult, b, P32, b, P32, sin_s, FS, stream=s2, tag="l2r2")
            mk.ew2(Alu.add, a, P32, a, P32, b, P32, stream=s1, tag="l2r3")
        for c in range(16):
            p = mk.psum[4 + c % 2]
            kb.op("T", lambda e, o=p[(slice(None), slice(0, 64))],
                  i=vh[(slice(0, 64), slice(c * 128, (c + 1) * 128))]:
                  e.transpose(o, i, mk.ident[(slice(0, 64), slice(0, 64))]),
                  reads=[vh, mk.ident], writes=[p], tag="l2vt")
            kb.v(lambda e, o=vtok[_f(c * 65, c * 65 + 64)],
                 i=p[(slice(None), slice(0, 64))]: e.tensor_copy(o, i),
                 reads=[p], writes=[vtok], stream="V", tag="l2vte")
        for Qn in range(4):
            pav = mk.psum[6 + Qn % 2]
            nlast = 4 * Qn + 3
            for cb in range(4 * Qn + 4):
                pet = mk.psum[4 + cb % 2]
                kb.mm(pet, (slice(None), slice(0, 512)),
                      kc, (slice(0, 32), slice(cb * 128, (cb + 1) * 128)),
                      qc, (slice(0, 32), slice(Qn * 512, (Qn + 1) * 512)),
                      start=True, stop=False, tag="l2et1")
                kb.mm(pet, (slice(None), slice(0, 512)),
                      ke, (slice(0, 32), slice(cb * 128, (cb + 1) * 128)),
                      qe, (slice(0, 32), slice(Qn * 512, (Qn + 1) * 512)),
                      start=False, stop=True, tag="l2et2")
                et = et_ab[cb % 2]
                kb.op("SC", lambda e, o=et[_f(0, 512)], i=pet[(slice(None), slice(0, 512))]:
                      e.activation(o, i, Act.Exp), reads=[pet], writes=[et], tag="l2ex")
                if cb >= 4 * Qn:
                    j = cb - 4 * Qn
                    kb.v(lambda e, o=et[_f(0, 512)], m=mtile[_f(j * 512, (j + 1) * 512)]:
                         e.tensor_mul(o, o, m), reads=[mtile, et], writes=[et],
                         stream="V", tag="l2dgm")
                kb.mm(pav, (slice(0, 65), slice(0, 512)),
                      vtok, _f(cb * 65, (cb + 1) * 65), et, _f(0, 512),
                      start=(cb == 0), stop=(cb == nlast), tag="l2av")
            kb.v(lambda e, o=et2[(slice(64, 65), slice(0, 512))],
                 i=pav[(slice(64, 65), slice(0, 512))]: e.tensor_copy(o, i),
                 reads=[pav], writes=[et2], stream="V", tag="l2sm")
            kb.v(lambda e, o=et2[(slice(64, 65), slice(0, 512))]: e.reciprocal(o, o),
                 reads=[et2], writes=[et2], stream="VR", tag="l2rc")
            pbc = mk.psum[3]
            kb.mm(pbc, (slice(0, 64), slice(0, 512)), mk.ones65,
                  (slice(64, 65), slice(0, 64)),
                  et2, (slice(64, 65), slice(0, 512)), start=True, stop=True, tag="l2bc")
            kb.v(lambda e, o=ao[(slice(0, 64), slice(0, 512))],
                 i=pav[(slice(0, 64), slice(0, 512))]: e.tensor_copy(o, i),
                 reads=[pav], writes=[ao], stream="V", tag="l2ae")
            kb.v(lambda e, o=ao[(slice(0, 64), slice(0, 512))],
                 i=pbc[(slice(0, 64), slice(0, 512))]: e.tensor_mul(o, o, i),
                 reads=[pbc, ao], writes=[ao], stream="V", tag="l2an")
            kb.dma(aod, (slice(h * 64, (h + 1) * 64), slice(Qn * 512, (Qn + 1) * 512)),
                   ao, (slice(0, 64), slice(0, 512)), tag="l2ast")
    mk.rel(mtile, vtok, et_ab, et2, ao, ql, kl, vl, qc, qe, qe2, kc, ke, ke2, vh)
    mk._ps = 0

    aoch = mk.take(2)
    wsl = mk.take(1)
    zt = mk.take(2)
    for n in range(4):
        for k in range(ND):
            kb.dma(aoch[k // 4], _f((k % 4) * 512, (k % 4) * 512 + 512),
                   aod, (slice(k * 128, (k + 1) * 128), slice(n * 512, (n + 1) * 512)),
                   tag="l2aold")
        for m in range(ND):
            half = (m % 2) * 1024
            kb.dma_ap(wsl, wsl.ap0[(slice(None), slice(half, half + 1024))].rearrange(
                "p (k m) -> p k m", m=128),
                WcN, WcN.ap0[:, m * 128:(m + 1) * 128].rearrange("(k p) m -> p k m", p=128),
                tag="l2wc")
            p = mk.pb()
            for k in range(ND):
                kb.mm(p, (slice(None), slice(0, 512)),
                      wsl, _f(half + k * 128, half + (k + 1) * 128),
                      aoch[k // 4], _f((k % 4) * 512, (k % 4) * 512 + 512),
                      start=(k == 0), stop=(k == ND - 1), tag="l2cm")
            mk.evac(zt[m % 2], _f(0, 512), p, (slice(None), slice(0, 512)), tag="l2cme")
            kb.dma_acc(mk.attd, (slice(m * 128, (m + 1) * 128), slice(n * 512, (n + 1) * 512)),
                       zt[m % 2], _f(0, 512), tag="l2cacc")
    mk.rel(aoch, wsl, zt)

    xnew = mk.take(8)
    ta = mk.take(1)
    for m in range(ND):
        msl = _p(m * 128, (m + 1) * 128)
        kb.dma(xnew[m], FS, mk.xd, msl, tag="l2cx")
        kb.dma(ta, FS, mk.attd, msl, tag="l2ca")
        mk.ew2(Alu.add, xnew[m], FS, xnew[m], FS, ta, FS, stream="V", tag="l2cs")
    mk.rel(ta)
    return xnew


def emit_layer3(mk: MK, x8):
    kb = mk.kb
    mk._col_n = 0
    CS = lambda a, n=1: (slice(None), slice(a, a + n))
    WinN = kb.dram("L3_WinN", (D, 4096), kind="ExternalInput")
    cwd = [kb.dram(f"L3_cw{kk}", (2048, 1), kind="ExternalInput") for kk in range(4)]
    cbd = kb.dram("L3_cb", (2048, 1), kind="ExternalInput")
    WxN = kb.dram("L3_WxN", (2048, 96), kind="ExternalInput")
    wdtN = kb.dram("L3_wdt", (64, 2048), kind="ExternalInput")
    bdtd = kb.dram("L3_bdt", (2048, 1), kind="ExternalInput")
    alogd = kb.dram("L3_alog", (2048, 16), kind="ExternalInput")
    Dpd = kb.dram("L3_D", (2048, 1), kind="ExternalInput")
    WoutN = kb.dram("L3_WoutN", (2048, D), kind="ExternalInput")
    wqT = kb.dram("L3_wqT", (D, D), I8, kind="ExternalInput")
    wsd = kb.dram("L3_ws", (D, 1), kind="ExternalInput")
    fbd = kb.dram("L3_fb", (D, 1), kind="ExternalInput")
    xzd = kb.dram("L3_xzd", (4096, S))
    xcd = kb.dram("L3_xcd", (2048, S))
    deltad = kb.dram("L3_deltad", (2048, S))
    yzd = kb.dram("L3_yzd", (2048, S))
    bcd = kb.dram("L3_bcd", (32, S))
    acols = kb.sb("l3acol", (128, 256))
    acols.tiny = True
    hcar = kb.sb("l3hcar", (128, 256))
    hcar.tiny = True

    for k in range(ND):
        kb.dma(mk.xd, _p(k * 128, (k + 1) * 128), x8[k], FS, tag="l3x_st")
    xn8 = x8
    mk.ln(x8, xn8, 1e-5, "l3ln")
    for k in range(ND):
        kb.dma(mk.xnd, _p(k * 128, (k + 1) * 128), xn8[k], FS, tag="l3xn_st")

    ws_cs = mk.colslot(8)
    mk.load_col(wsd, 0, ws_cs, 8, tag="l3ws")
    fb_cs = mk.colslot(8)
    mk.load_col(fbd, 0, fb_cs, 8, tag="l3fb")
    am_cs = mk.colslot(16)
    xs_cs = mk.colslot(16)
    si_cs = mk.colslot(16)
    mabs = mk.take(1)
    tt = mk.take(1)
    for k in range(ND):
        if k == 0:
            mk.stt(mabs, FS, xn8[0], FS, -1.0, xn8[0], FS, Alu.mult, Alu.max,
                   stream="V", tag="l3ab")
        else:
            mk.stt(tt, FS, xn8[k], FS, -1.0, xn8[k], FS, Alu.mult, Alu.max,
                   stream="V", tag="l3ab")
            mk.ew2(Alu.max, mabs, FS, mabs, FS, tt, FS, stream="V", tag="l3am")
    for t in range(16):
        p = mk.pb()
        kb.op("T", lambda e, o=p[(slice(None), slice(0, 128))],
              i=mabs[_f(t * 128, (t + 1) * 128)]: e.transpose(o, i, mk.ident.full),
              reads=[mabs, mk.ident], writes=[p], tag="l3tt")
        kb.v(lambda e, o=mk.cols[CS(am_cs + t)], i=p[(slice(None), slice(0, 128))]:
             e.reduce_max(o, i, axis=XAX), reads=[p], writes=[mk.cols], stream="VR",
             tag="l3amr")
    mk.ts(mk.cols, CS(xs_cs, 16), mk.cols, CS(am_cs, 16), 1e-6, Alu.max,
          1.0 / 127.0, Alu.mult, tag="l3xs")
    kb.v(lambda e: e.reciprocal(mk.cols[CS(si_cs, 16)], mk.cols[CS(xs_cs, 16)]),
         reads=[mk.cols], writes=[mk.cols], stream="VR", tag="l3si")
    xsb = mk.take(1)
    sib = mk.take(1)
    mk.cols_to_row(xs_cs, 16, 0, tag="l3c2r1")
    mk.bcast_row(0, S, xsb, 0, tag="l3bx")
    mk.cols_to_row(si_cs, 16, 0, tag="l3c2r2")
    mk.bcast_row(0, S, sib, 0, tag="l3bs")
    scr = mk.take(1)
    for k in range(ND):
        mk.ew2(Alu.mult, tt, FS, xn8[k], FS, sib, FS, stream="V", tag="l3u")
        mk.fq(xn8[k], FS, tt, FS, None, scr, FS, -128.0, 127.0, tag="l3fq")
    mk.rel(tt, sib, mabs, scr)
    wr = mk.take(2)
    zt = mk.take(2)
    for m in range(ND):
        w = wr[m % 2]
        kb.dma_ap(w, w.ap0[(slice(None), slice(0, 1024))].rearrange(
            "p (k m) -> p k m", m=128),
            wqT, wqT.ap0[:, m * 128:(m + 1) * 128].rearrange("(k p) m -> p k m", p=128),
            stream="GD", tag="l3wq")
        for n in range(4):
            p = mk.pb()
            for k in range(ND):
                kb.mm(p, (slice(None), slice(0, 512)),
                      w, _f(k * 128, (k + 1) * 128),
                      xn8[k], _f(n * 512, (n + 1) * 512),
                      start=(k == 0), stop=(k == ND - 1), tag="l3qm")
            z = zt[n % 2]
            mk.act_col(z, _f(0, 512), p, (slice(None), slice(0, 512)), Act.Identity,
                       scale_cs=ws_cs + m, tag="l3ze")
            mk.ew2(Alu.mult, z, _f(0, 512), z, _f(0, 512), xsb,
                   _f(n * 512, (n + 1) * 512), stream="V", tag="l3zx")
            mk.ts_col(z, _f(0, 512), z, _f(0, 512), mk.cols, fb_cs + m, Alu.add,
                      stream="V", tag="l3zb")
            kb.dma(mk.attd, (slice(m * 128, (m + 1) * 128), slice(n * 512, (n + 1) * 512)),
                   z, _f(0, 512), tag="l3zst")
    mk.rel(wr, zt, xsb)

    for k in range(ND):
        kb.dma(xn8[k], FS, mk.xnd, _p(k * 128, (k + 1) * 128), tag="l3xnld")
    wsl = mk.take(1)
    am2 = mk.take(2)
    for m in range(32):
        half = (m % 2) * 1024
        kb.dma_ap(wsl, wsl.ap0[(slice(None), slice(half, half + 1024))].rearrange(
            "p (k m) -> p k m", m=128),
            WinN, WinN.ap0[:, m * 128:(m + 1) * 128].rearrange("(k p) m -> p k m", p=128),
            tag="l3wi")
        at = am2[m % 2]
        for n in range(4):
            p = mk.pb()
            for k in range(ND):
                kb.mm(p, (slice(None), slice(0, 512)),
                      wsl, _f(half + k * 128, half + (k + 1) * 128),
                      xn8[k], _f(n * 512, (n + 1) * 512),
                      start=(k == 0), stop=(k == ND - 1), tag="l3zm")
            mk.evac(at, _f(n * 512, (n + 1) * 512), p, (slice(None), slice(0, 512)),
                    tag="l3zme")
        kb.dma(xzd, _p(m * 128, (m + 1) * 128), at, FS, tag="l3zmst")
    mk.rel(xn8, wsl, am2)

    cw_cs = mk.colslot(64)
    for kk in range(4):
        mk.load_col(cwd[kk], 0, cw_cs + kk * 16, 16, tag="l3cw")
    cb_cs = mk.colslot(16)
    mk.load_col(cbd, 0, cb_cs, 16, tag="l3cbl")
    wx_s = mk.take(1)
    kb.dma_ap(wx_s, wx_s.ap0[(slice(None), slice(0, 1536))].rearrange(
        "p (k m) -> p k m", m=96),
        WxN, WxN.ap0[:, :].rearrange("(k p) m -> p k m", p=128), tag="l3wx")
    xdbl = mk.take(1)
    xi = mk.take(1)
    acc2 = mk.take(2)
    for g in range(16):
        kb.dma(xi, FS, xzd, _p(g * 128, (g + 1) * 128), tag="l3xild")
        a = acc2[g % 2]
        mk.ts_col(a, FS, xi, FS, mk.cols, cw_cs + 3 * 16 + g, Alu.mult, stream="V",
                  tag="l3cv0")
        for j, kk in [(1, 2), (2, 1), (3, 0)]:
            mk.stt_col(a, (slice(None), slice(j, S)), xi, (slice(None), slice(0, S - j)),
                       mk.cols, cw_cs + kk * 16 + g, a, (slice(None), slice(j, S)),
                       Alu.mult, Alu.add, stream="V", tag="l3cv")
        mk.act_col(xi, FS, a, FS, Act.Silu, bias_cs=cb_cs + g, tag="l3cs")
        kb.dma(xcd, _p(g * 128, (g + 1) * 128), xi, FS, tag="l3xcst")
        for n in range(4):
            kb.mm(mk.psum[n], (slice(0, 96), slice(0, 512)),
                  wx_s, _f(g * 96, (g + 1) * 96),
                  xi, _f(n * 512, (n + 1) * 512),
                  start=(g == 0), stop=(g == 15), tag="l3xd")
    for n in range(4):
        mk.evac(xdbl, (slice(0, 96), slice(n * 512, (n + 1) * 512)),
                mk.psum[n], (slice(0, 96), slice(0, 512)), tag="l3xde")
    kb.dma(bcd, None, xdbl, (slice(64, 96), slice(None)), tag="l3bcst")
    mk.rel(wx_s, xi, acc2)

    bdt_cs = mk.colslot(16)
    mk.load_col(bdtd, 0, bdt_cs, 16, tag="l3bdt")
    wdt_s = mk.take(1)
    kb.dma(wdt_s, (slice(0, 64), slice(None)), wdtN, FS, tag="l3wdt")
    t1 = mk.take(1)
    aa = mk.take(1)
    ee = mk.take(1)
    dl2 = mk.take(2)
    for m in range(16):
        out = dl2[m % 2]
        for n in range(4):
            p = mk.pb()
            kb.mm(p, (slice(None), slice(0, 512)),
                  wdt_s, (slice(0, 64), slice(m * 128, (m + 1) * 128)),
                  xdbl, (slice(0, 64), slice(n * 512, (n + 1) * 512)),
                  start=True, stop=True, tag="l3dm")
            mk.act_col(t1, _f(n * 512, (n + 1) * 512), p, (slice(None), slice(0, 512)),
                       Act.Identity, bias_cs=bdt_cs + m, tag="l3du")
        mk.stt(aa, FS, t1, FS, -1.0, t1, FS, Alu.mult, Alu.max, stream="V", tag="l3da")
        for n in range(4):
            mk.act(ee, _f(n * 512, (n + 1) * 512), aa, _f(n * 512, (n + 1) * 512),
                   Act.Exp, scale=-1.0, tag="l3de")
        for n in range(4):
            mk.act(aa, _f(n * 512, (n + 1) * 512), ee, _f(n * 512, (n + 1) * 512),
                   Act.Ln, bias=1.0, tag="l3dl")
        mk.ts(t1, FS, t1, FS, 0.0, Alu.max, stream="V", tag="l3dr")
        mk.ew2(Alu.add, out, FS, t1, FS, aa, FS, stream="V", tag="l3dd")
        kb.dma(deltad, _p(m * 128, (m + 1) * 128), out, FS, tag="l3dst")
    mk.rel(wdt_s, t1, aa, ee, dl2, xdbl)

    for g in range(16):
        kb.dma(acols, (slice(None), slice(g * 16, (g + 1) * 16)),
               alogd, _p(g * 128, (g + 1) * 128), tag="l3al")
    mk.act(acols, (slice(None), slice(0, 256)), acols, (slice(None), slice(0, 256)),
           Act.Exp, tag="l3ae")
    mk.ts(acols, (slice(None), slice(0, 256)), acols, (slice(None), slice(0, 256)),
          -1.0, Alu.mult, stream="V", tag="l3an")
    dp_cs = mk.colslot(16)
    mk.load_col(Dpd, 0, dp_cs, 16, tag="l3dp")
    kb.v(lambda e: e.memset(hcar.full, 0.0), reads=[], writes=[hcar], stream="V",
         tag="l3h0")

    bbt = mk.take(4)
    cbt = mk.take(4)
    dlA, dlB, xct, zgt, wg, yac, at_, ht, yp = mk.take(9)
    Q = _f(0, 512)
    for tc in range(4):
        csl = slice(tc * 512, (tc + 1) * 512)
        for n in range(16):
            kb.dma_ap(bbt[n // 4], bbt[n // 4].ap0[_f((n % 4) * 512, (n % 4) * 512 + 512)],
                      bcd, bcd.ap0[(slice(n, n + 1), csl)].to_broadcast([128, 512]),
                      tag="l3bb")
            kb.dma_ap(cbt[n // 4], cbt[n // 4].ap0[_f((n % 4) * 512, (n % 4) * 512 + 512)],
                      bcd, bcd.ap0[(slice(16 + n, 17 + n), csl)].to_broadcast([128, 512]),
                      tag="l3cb")
        for g in range(16):
            dl = dlA if g % 2 == 0 else dlB
            kb.dma(dl, Q, deltad, (slice(g * 128, (g + 1) * 128), csl), tag="l3dld")
            kb.dma(xct, Q, xcd, (slice(g * 128, (g + 1) * 128), csl), tag="l3xcld")
            kb.dma(zgt, Q, xzd, (slice(2048 + g * 128, 2048 + (g + 1) * 128), csl),
                   tag="l3zld")
            mk.ew2(Alu.mult, wg, Q, dl, Q, xct, Q, stream="V", tag="l3wg")
            mk.ts_col(yac, Q, xct, Q, mk.cols, dp_cs + g, Alu.mult, stream="V", tag="l3y0")
            for n in range(16):
                idx = g * 16 + n
                nsl = _f((n % 4) * 512, (n % 4) * 512 + 512)
                kb.op("SC", lambda e, o=at_[Q], i=dl[Q], s=acols[CS(idx)]:
                      e.activation(o, i, Act.Exp, scale=s),
                      reads=[dl, acols], writes=[at_], tag="l3sa")
                mk.ew2(Alu.mult, yp, Q, wg, Q, bbt[n // 4], nsl, stream="V", tag="l3sb")
                kb.v(lambda e, o=ht[Q], a=at_[Q], b=yp[Q], ini=hcar[CS(idx)]:
                     e.tensor_tensor_scan(o, a, b, ini, Alu.mult, Alu.add),
                     reads=[at_, yp, hcar], writes=[ht], stream="V", tag="l3sc")
                kb.v(lambda e, o=hcar[CS(idx)], i=ht[_f(511, 512)]: e.tensor_copy(o, i),
                     reads=[ht], writes=[hcar], stream="V", tag="l3hc")
                mk.ew2(Alu.mult, yp, Q, ht, Q, cbt[n // 4], nsl, stream="V", tag="l3sy")
                mk.ew2(Alu.add, yac, Q, yac, Q, yp, Q, stream="G", tag="l3sacc")
            mk.act(zgt, Q, zgt, Q, Act.Silu, tag="l3sz")
            mk.ew2(Alu.mult, yac, Q, yac, Q, zgt, Q, stream="V", tag="l3yz")
            kb.dma(yzd, (slice(g * 128, (g + 1) * 128), csl), yac, Q, tag="l3yst")
    mk.rel(bbt, cbt, dlA, dlB, xct, zgt, wg, yac, at_, ht, yp)

    yzc = mk.take(4)
    wsl = mk.take(1)
    zt = mk.take(2)
    for n in range(4):
        for ki in range(16):
            kb.dma(yzc[ki // 4], _f((ki % 4) * 512, (ki % 4) * 512 + 512),
                   yzd, (slice(ki * 128, (ki + 1) * 128), slice(n * 512, (n + 1) * 512)),
                   tag="l3yld")
        for m in range(ND):
            kb.dma_ap(wsl, wsl.ap0[FS].rearrange("p (k m) -> p k m", m=128),
                      WoutN, WoutN.ap0[:, m * 128:(m + 1) * 128].rearrange(
                          "(k p) m -> p k m", p=128), tag="l3wo")
            p = mk.pb()
            for ki in range(16):
                kb.mm(p, (slice(None), slice(0, 512)),
                      wsl, _f(ki * 128, (ki + 1) * 128),
                      yzc[ki // 4], _f((ki % 4) * 512, (ki % 4) * 512 + 512),
                      start=(ki == 0), stop=(ki == 15), tag="l3om")
            mk.evac(zt[m % 2], _f(0, 512), p, (slice(None), slice(0, 512)), tag="l3oe")
            kb.dma_acc(mk.attd, (slice(m * 128, (m + 1) * 128), slice(n * 512, (n + 1) * 512)),
                       zt[m % 2], _f(0, 512), tag="l3oacc")
    mk.rel(yzc, wsl, zt)

    xnew = mk.take(8)
    ta = mk.take(1)
    for m in range(ND):
        msl = _p(m * 128, (m + 1) * 128)
        kb.dma(xnew[m], FS, mk.xd, msl, tag="l3cx")
        kb.dma(ta, FS, mk.attd, msl, tag="l3ca")
        mk.ew2(Alu.add, xnew[m], FS, xnew[m], FS, ta, FS, stream="V", tag="l3cs2")
    mk.rel(ta)
    return xnew


def _prep(inputs, n_layers):
    f32 = np.float32
    x = np.asarray(inputs["L0_attn_x"], f32)
    com = {}
    if n_layers >= 1:
        com["L0_fc1_q"] = np.ascontiguousarray(np.asarray(inputs["L0_ffn_fc1_q"], np.int32))
        com["L0_fc1_sb"] = np.ascontiguousarray(
            np.repeat(np.asarray(inputs["L0_ffn_fc1_scales"], f32), 128, axis=0))
        com["L0_fc1_b"] = np.ascontiguousarray(np.asarray(inputs["L0_ffn_fc1_bias"], f32).reshape(-1, 1))
        com["L0_fc2_q"] = np.ascontiguousarray(np.asarray(inputs["L0_ffn_fc2_q"], np.int32))
        com["L0_fc2_sb"] = np.ascontiguousarray(
            np.repeat(np.asarray(inputs["L0_ffn_fc2_scales"], f32), 128, axis=0))
        com["L0_fc2_b"] = np.ascontiguousarray(np.asarray(inputs["L0_ffn_fc2_bias"], f32).reshape(-1, 1))
        com["L0_ln_g"] = np.ascontiguousarray(np.asarray(inputs["L0_ffn_ln_gamma"], f32).reshape(-1, 1))
        com["L0_ln_b"] = np.ascontiguousarray(np.asarray(inputs["L0_ffn_ln_beta"], f32).reshape(-1, 1))
    if n_layers >= 2:
        com["L1_WqT"] = np.ascontiguousarray(np.asarray(inputs["L1_attn_Wq"], f32).T)
        com["L1_WkT"] = np.ascontiguousarray(np.asarray(inputs["L1_attn_Wk"], f32).T)
        com["L1_WvT"] = np.ascontiguousarray(np.asarray(inputs["L1_attn_Wv"], f32).T)
        com["L1_WpT"] = np.ascontiguousarray(np.asarray(inputs["L1_attn_Wp"], f32).T)
        com["L1_bp"] = np.ascontiguousarray(np.asarray(inputs["L1_attn_bp"], f32).reshape(-1, 1))
        com["L1_bng"] = np.ascontiguousarray(np.asarray(inputs["L1_attn_bn_gamma"], f32).reshape(-1, 1))
        com["L1_bnb"] = np.ascontiguousarray(np.asarray(inputs["L1_attn_bn_beta"], f32).reshape(-1, 1))
        com["L1_bnm"] = np.ascontiguousarray(np.asarray(inputs["L1_attn_bn_mean"], f32).reshape(-1, 1))
        com["L1_bnv"] = np.ascontiguousarray(np.asarray(inputs["L1_attn_bn_var"], f32).reshape(-1, 1))
        com["L1_clip_x"] = np.ascontiguousarray(np.asarray(inputs["L1_ffn_clip_x"], f32).reshape(1, 1))
        com["L1_clip_d"] = np.ascontiguousarray(np.asarray(inputs["L1_ffn_clip_down"], f32).reshape(1, 1))
        com["L1_wgN"] = np.ascontiguousarray(np.asarray(inputs["L1_ffn_w_gate"], f32))
        com["L1_wuN"] = np.ascontiguousarray(np.asarray(inputs["L1_ffn_w_up"], f32))
        com["L1_wdN"] = np.ascontiguousarray(np.asarray(inputs["L1_ffn_w_down"], f32))
        com["L1_wgT"] = np.ascontiguousarray(np.asarray(inputs["L1_ffn_w_gate"], f32).T)
        com["L1_wuT"] = np.ascontiguousarray(np.asarray(inputs["L1_ffn_w_up"], f32).T)
        com["L1_wdT"] = np.ascontiguousarray(np.asarray(inputs["L1_ffn_w_down"], f32).T)
    if n_layers >= 3:
        com["L2_WqdN"] = np.ascontiguousarray(np.asarray(inputs["L2_attn_Wq_down"], f32))
        com["L2_WkdN"] = np.ascontiguousarray(np.asarray(inputs["L2_attn_Wk_down"], f32))
        com["L2_WvdN"] = np.ascontiguousarray(np.asarray(inputs["L2_attn_Wv_down"], f32))
        com["L2_upqc"] = np.ascontiguousarray(np.asarray(inputs["L2_attn_Wq_up_c"], f32))
        com["L2_upqe"] = np.ascontiguousarray(np.asarray(inputs["L2_attn_Wq_up_e"], f32))
        com["L2_upkc"] = np.ascontiguousarray(np.asarray(inputs["L2_attn_Wk_up_c"], f32))
        com["L2_upke"] = np.ascontiguousarray(np.asarray(inputs["L2_attn_Wk_up_e"], f32))
        com["L2_upv"] = np.ascontiguousarray(np.asarray(inputs["L2_attn_Wv_up"], f32))
        com["L2_upqeT"] = np.ascontiguousarray(np.asarray(inputs["L2_attn_Wq_up_e"], f32).T)
        com["L2_upkeT"] = np.ascontiguousarray(np.asarray(inputs["L2_attn_Wk_up_e"], f32).T)
        com["L2_WcN"] = np.ascontiguousarray(np.asarray(inputs["L2_attn_Wc"], f32))
        com["L2_gwT"] = np.ascontiguousarray(np.asarray(inputs["L2_ffn_gate_w"], f32).T)
        com["L2_bb"] = np.ascontiguousarray(np.asarray(inputs["L2_ffn_balance_bias"], f32).reshape(-1, 1))
        com["L2_w1T"] = np.ascontiguousarray(
            np.asarray(inputs["L2_ffn_w1"], f32).transpose(0, 2, 1).reshape(8 * 1024, 2048))
        com["L2_w2T"] = np.ascontiguousarray(
            np.asarray(inputs["L2_ffn_w2"], f32).transpose(0, 2, 1).reshape(8 * 2048, 1024))
    if n_layers >= 4:
        com["L3_WinN"] = np.ascontiguousarray(np.asarray(inputs["L3_attn_W_in"], f32))
        cw = np.asarray(inputs["L3_attn_conv_w"], f32)
        for kk in range(4):
            com[f"L3_cw{kk}"] = np.ascontiguousarray(cw[:, kk].reshape(-1, 1))
        com["L3_cb"] = np.ascontiguousarray(np.asarray(inputs["L3_attn_conv_b"], f32).reshape(-1, 1))
        com["L3_WxN"] = np.ascontiguousarray(np.asarray(inputs["L3_attn_W_x"], f32))
        com["L3_wdt"] = np.ascontiguousarray(np.asarray(inputs["L3_attn_W_dt"], f32))
        com["L3_bdt"] = np.ascontiguousarray(np.asarray(inputs["L3_attn_b_dt"], f32).reshape(-1, 1))
        com["L3_alog"] = np.ascontiguousarray(np.asarray(inputs["L3_attn_A_log"], f32))
        com["L3_D"] = np.ascontiguousarray(np.asarray(inputs["L3_attn_D"], f32).reshape(-1, 1))
        com["L3_WoutN"] = np.ascontiguousarray(np.asarray(inputs["L3_attn_W_out"], f32))
        com["L3_wqT"] = np.ascontiguousarray(np.asarray(inputs["L3_ffn_weight_q"]).T)
        com["L3_ws"] = np.ascontiguousarray(np.asarray(inputs["L3_ffn_weight_scale"], f32).reshape(-1, 1))
        com["L3_fb"] = np.ascontiguousarray(np.asarray(inputs["L3_ffn_bias"], f32).reshape(-1, 1))
    in_maps = []
    for b in range(8):
        m = dict(com)
        m["x_f"] = np.ascontiguousarray(x[b].T)
        in_maps.append(m)
    return in_maps


def _build(n_layers, dbg=False, layers=None):
    nc = bass.Bass("TRN2", target_bir_lowering=False, detect_race_conditions=False)
    stack = ExitStack()
    with stack:
        kb = KB(nc, stack)
        mk = MK(kb, n_layers, list(range(8)), dbg=dbg)
        x8 = mk.take(8)
        for k in range(ND):
            kb.dma(x8[k], FS, mk.x_in, _p(k * 128, (k + 1) * 128), tag="xin")
        emits = [emit_layer0, emit_layer1, emit_layer2, emit_layer3]
        if layers is None:
            layers = list(range(n_layers))
        for li in layers:
            x8 = emits[li](mk, x8)
        for k in range(ND):
            kb.dma(mk.y_out, _p(k * 128, (k + 1) * 128), x8[k], FS, tag="yout")
        block = stack.enter_context(nc.Block())
        kb.finalize(block)
    return nc


def _run(inputs, n_layers=4, dbg=False, layers=None):
    in_maps = _prep(inputs, n_layers)
    nc = _build(n_layers, dbg=dbg, layers=layers)
    res = run_bass_kernel_spmd(nc, in_maps, core_ids=list(range(8)))
    if dbg:
        return res
    return np.stack([r["y"] for r in res.results], axis=0)


def kernel(**inputs):
    yf = _run(inputs, n_layers=4)
    return np.ascontiguousarray(np.transpose(yf, (0, 2, 1)))


# version 05 (final)
# speedup vs baseline: 1.0572x; 1.0572x over previous
import bisect
import numpy as np
import concourse.bass as bass
import concourse.mybir as mybir
from concourse.bass_utils import run_bass_kernel_spmd
from contextlib import ExitStack

F32 = mybir.dt.float32
BF16 = mybir.dt.bfloat16
I32 = mybir.dt.int32
I8 = mybir.dt.int8
XAX = mybir.AxisListType.X
Alu = mybir.AluOpType
Act = mybir.ActivationFunctionType

S = 2048
D = 1024
ND = D // 128
FS = (slice(None), slice(None))


def _f(a, b):
    return (slice(None), slice(a, b))


def _p(a, b):
    return (slice(a, b), slice(None))


class TB:
    __slots__ = ("ap0", "name", "writer", "readers", "shape", "dtype", "tiny")

    def __init__(self, handle, name, shape, dtype):
        self.ap0 = handle.ap() if hasattr(handle, "ap") else handle
        self.name = name
        self.shape = tuple(shape)
        self.dtype = dtype
        self.writer = None
        self.readers = []
        self.tiny = len(shape) >= 2 and shape[1] <= 32

    def __getitem__(self, sl):
        return self.ap0[sl]

    @property
    def full(self):
        return self.ap0[tuple(slice(None) for _ in self.shape)]

    def i32(self, sl):
        return self.ap0.bitcast(I32)[sl]

    def bf(self, sl):
        return self.ap0.bitcast(BF16)[sl]


class Lane:
    _next_id = 0

    def __init__(self, step):
        self.step = step
        self.sem = None
        self.id = Lane._next_id
        Lane._next_id += 1


NDMALANES = 6


class Stream:
    def __init__(self, kb, name, engine_key, dma):
        self.kb = kb
        self.name = name
        self.engine_key = engine_key
        self.dma = dma
        self.async_ = name == "VR"
        self.step = 16 if dma else 1
        self.n = 0
        self.inc_idx = []
        self.lanes = [Lane(self.step) for _ in range(NDMALANES if dma else 1)]

    def cover(self, idx):
        if self.dma:
            ln = self.lanes[idx % NDMALANES]
            return (ln, (idx // NDMALANES + 1) * self.step)
        ln = self.lanes[0]
        j = bisect.bisect_left(self.inc_idx, idx)
        if j < len(self.inc_idx):
            return (ln, (j + 1) * self.step)
        last_op = self.kb.last_op[self]
        assert last_op.sidx >= idx
        last_op.inc = True
        self.inc_idx.append(last_op.sidx)
        return (ln, len(self.inc_idx) * self.step)


class Op:
    __slots__ = ("fn", "stream", "sidx", "waits", "inc", "tag")

    def __init__(self, fn, stream, sidx, tag):
        self.fn = fn
        self.stream = stream
        self.sidx = sidx
        self.waits = []
        self.inc = False
        self.tag = tag


class KB:
    ENGINES = ("tensor", "vector", "scalar", "gpsimd", "sync")

    def __init__(self, nc, stack):
        self.nc = nc
        self.stack = stack
        self.eops = {e: [] for e in self.ENGINES}
        self.streams = {}
        self.last_op = {}
        self.n_ops = 0
        for name, ekey, dma in [
            ("T", "tensor", False), ("V", "vector", False), ("SC", "scalar", False),
            ("G", "gpsimd", False), ("SD", "sync", True), ("GD", "gpsimd", True),
            ("CC", "gpsimd", False), ("VR", "vector", False),
        ]:
            self.streams[name] = Stream(self, name, ekey, dma)
        self._uid = 0

    def uid(self, p="t"):
        self._uid += 1
        return f"{p}{self._uid}"

    def sb(self, name, shape, dtype=F32):
        h = self.stack.enter_context(self.nc.sbuf_tensor(name, list(shape), dtype))
        return TB(h, name, shape, dtype)

    def ps(self, name):
        h = self.stack.enter_context(self.nc.psum_tensor(name, [128, 512], F32))
        return TB(h, name, (128, 512), F32)

    def dram(self, name, shape, dtype=F32, kind="Internal", addr_space="Local"):
        h = self.nc.dram_tensor(name, list(shape), dtype, kind=kind, addr_space=addr_space)
        return TB(h, name, shape, dtype)

    def const(self, name, arr):
        h = self.nc.inline_tensor(np.ascontiguousarray(arr), name=name)
        return TB(h, name, arr.shape, None)

    def op(self, stream, fn, reads=(), writes=(), tag=""):
        if isinstance(stream, str):
            stream = self.streams[stream]
        if stream.name == "V" and any(getattr(b, "tiny", False) for b in writes):
            stream = self.streams["VR"]
        o = Op(fn, stream, stream.n, tag)
        stream.n += 1
        self.n_ops += 1
        waits = {}

        def need(src, idx):
            if src is stream and not stream.dma and not stream.async_:
                return
            if src is stream and stream.dma and (stream.n - 1 - idx) < NDMALANES:
                if (stream.n - 1) % NDMALANES == idx % NDMALANES:
                    return
            ln, v = src.cover(idx)
            if waits.get(ln, 0) < v:
                waits[ln] = v

        for b in reads:
            if b.writer is not None:
                need(*b.writer)
        for b in writes:
            if b.writer is not None:
                need(*b.writer)
            for r in b.readers:
                need(*r)
        if stream.dma:
            o.inc = True
            ln = stream.lanes[o.sidx % NDMALANES]
            prev_uses = o.sidx // NDMALANES
            if prev_uses >= 1:
                if waits.get(ln, 0) < prev_uses * stream.step:
                    waits[ln] = prev_uses * stream.step
        elif stream.async_:
            o.inc = True
            stream.inc_idx.append(o.sidx)
        o.waits = list(waits.items())
        self.eops[stream.engine_key].append(o)
        self.last_op[stream] = o
        me = (stream, o.sidx)
        for b in writes:
            b.writer = me
            b.readers = []
        for b in reads:
            b.readers.append(me)
        return o

    def dma(self, dst, dsl, src, ssl, stream="SD", tag="dma"):
        d = dst[dsl] if dsl is not None else dst.full
        s = src[ssl] if ssl is not None else src.full
        return self.op(stream, lambda e: e.dma_start(out=d, in_=s),
                       reads=[src], writes=[dst], tag=tag)

    def dma_ap(self, dst, d_ap, src, s_ap, stream="SD", tag="dma", slow=False):
        if slow:
            return self.op(stream,
                           lambda e: e.dma_start(out=d_ap, in_=s_ap,
                                                 allow_slow_non_contiguous=True),
                           reads=[src], writes=[dst], tag=tag)
        return self.op(stream, lambda e: e.dma_start(out=d_ap, in_=s_ap),
                       reads=[src], writes=[dst], tag=tag)

    def dma_acc(self, dst, dsl, src, ssl, tag="dmacc"):
        d = dst[dsl] if dsl is not None else dst.full
        s = src[ssl] if ssl is not None else src.full
        return self.op("GD", lambda e: e.dma_start(out=d, in_=s, accum_op=Alu.add),
                       reads=[src, dst], writes=[dst], tag=tag)

    def v(self, fn, reads, writes, stream="V", tag="v"):
        return self.op(stream, fn, reads=reads, writes=writes, tag=tag)

    def mm(self, psum, psl, lhsT, lsl, rhs, rsl, start, stop, tag="mm"):
        o = psum[psl] if psl is not None else psum.full
        l = lhsT[lsl] if lsl is not None else lhsT.full
        r = rhs[rsl] if rsl is not None else rhs.full
        return self.op("T", lambda e: e.matmul(o, l, r, start=start, stop=stop),
                       reads=[lhsT, rhs], writes=[psum], tag=tag)

    def finalize(self, block):
        for s in self.streams.values():
            for i, ln in enumerate(s.lanes):
                ln.sem = self.stack.enter_context(self.nc.semaphore(f"sem_{s.name}{i}"))

        def mk(ekey):
            def run(eng):
                waited = {}
                for o in self.eops[ekey]:
                    for (ln, val) in o.waits:
                        if waited.get(ln.id, 0) < val:
                            eng.wait_ge(ln.sem, val)
                            waited[ln.id] = val
                    try:
                        inst = o.fn(eng)
                    except Exception:
                        print(f"[finalize] failing op: stream={o.stream.name} "
                              f"sidx={o.sidx} tag={o.tag}")
                        raise
                    if o.inc:
                        st = o.stream
                        if st.dma:
                            ln = st.lanes[o.sidx % NDMALANES]
                            inst.then_inc(ln.sem, st.step)
                        else:
                            inst.then_inc(st.lanes[0].sem, st.step)
                if ekey == "sync":
                    for s in self.streams.values():
                        if s.dma:
                            for i, ln in enumerate(s.lanes):
                                uses = (s.n - 1 - i) // NDMALANES + 1 if s.n > i else 0
                                if uses > 0:
                                    eng.wait_ge(ln.sem, uses * s.step)
                        elif s.name == "CC" and s.inc_idx:
                            eng.wait_ge(s.lanes[0].sem, len(s.inc_idx) * s.step)
            return run

        block.tensor(mk("tensor"))
        block.vector(mk("vector"))
        block.scalar(mk("scalar"))
        block.gpsimd(mk("gpsimd"))
        block.sync(mk("sync"))


class MK:
    def __init__(self, kb: KB, n_layers, core_ids, dbg=False):
        self.kb = kb
        self.n_layers = n_layers
        self.core_ids = core_ids
        self.dbg = dbg
        self.dbg_names = []
        self._vg = 0
        self.psum = [kb.ps(f"pbank{i}") for i in range(8)]
        self._ps = 0
        self.ones = kb.sb("fx_ones", (1, 128))
        kb.dma(self.ones, None, kb.const("c_ones", np.ones((1, 128), np.float32)),
               (slice(0, 1), slice(None)))
        self.onesc = kb.sb("fx_onesc", (128, 1))
        kb.dma(self.onesc, None, kb.const("c_onesc", np.ones((128, 1), np.float32)), FS)
        self.ones65 = kb.sb("fx_ones65", (65, 128))
        kb.dma(self.ones65, None, kb.const("c_ones65", np.ones((65, 128), np.float32)), FS)
        self.ident = kb.sb("fx_ident", (128, 128))
        kb.dma(self.ident, None, kb.const("c_ident", np.eye(128, dtype=np.float32)), FS)
        self.rows = kb.sb("fx_rows", (1, 2 * S))
        self.junk = kb.sb("fx_junk", (128, 4))
        self.CVALS = [1e-5, 1e-6, 1e-8, 0.5, 0.125, -1.0, 2.0, 1.0 / 127.0, -0.5, 0.17677669529663687, 0.25,
                      1.0, 0.35355339059327373]
        self.ccols = kb.sb("fx_cc", (128, len(self.CVALS)))
        kb.dma(self.ccols, None, kb.const(
            "c_cvals", np.tile(np.asarray(self.CVALS, np.float32)[None, :], (128, 1))), FS)
        self.cols = kb.sb("fx_cols", (128, 256))
        self.cols.tiny = True
        self._col_n = 0
        self.pool = [kb.sb(f"pool{i}", (128, S)) for i in range(17)]
        self._free = list(range(17))
        self.slab = [kb.sb(f"slab{i}", (128, 4096)) for i in range(2)]
        self.cmsrow = kb.sb("fx_cms", (1, 129))
        self.xd = kb.dram("xd", (D, S))
        self.xnd = kb.dram("xnd", (D, S))
        self.attd = kb.dram("attd", (D, S))
        self.x_in = kb.dram("x_f", (D, S), kind="ExternalInput")
        self.y_out = kb.dram("y", (D, S), kind="ExternalOutput")

    def take(self, n=1):
        assert len(self._free) >= n, f"pool exhausted (want {n}, have {len(self._free)})"
        out = [self.pool[self._free.pop()] for _ in range(n)]
        return out if n > 1 else out[0]

    def rel(self, *tiles):
        for ts in tiles:
            if isinstance(ts, list):
                self.rel(*ts)
            else:
                i = int(ts.name[4:])
                assert i not in self._free
                self._free.append(i)

    def pb(self):
        b = self.psum[self._ps % 8]
        self._ps += 1
        return b

    def colslot(self, n=1):
        c = self._col_n
        self._col_n += n
        assert self._col_n <= 256
        return c

    def tap(self, name, tb, sl=None, shape=None):
        if not self.dbg:
            return
        shp = shape or tb.shape
        d = self.kb.dram("dbg_" + name, shp, kind="ExternalOutput")
        if tb.ap0.tensor.__class__.__name__.startswith("DRam") if False else str(type(tb.ap0)).find("x") >= -1:
            pass
        self.kb.dma(d, None, tb, sl, tag="tap")
        self.dbg_names.append("dbg_" + name)

    def vg(self):
        self._vg += 1
        return "V" if self._vg % 2 else "G"

    def cc(self, v, pdim=128):
        i = self.CVALS.index(v)
        return self.ccols[(slice(0, pdim), slice(i, i + 1))]

    def evac(self, out_tb, osl, psum_tb, psl, tag="ev"):
        self._vg += 1
        if self._vg % 2:
            self.kb.v(lambda e, o=out_tb[osl], i=psum_tb[psl]: e.tensor_copy(o, i),
                      reads=[psum_tb], writes=[out_tb], stream="V", tag=tag)
        else:
            self.kb.op("SC", lambda e, o=out_tb[osl], i=psum_tb[psl]: e.copy(o, i),
                       reads=[psum_tb], writes=[out_tb], tag=tag)

    def ew2(self, op, out_tb, osl, a_tb, asl, b_tb, bsl, stream=None, tag="ew"):
        st = stream or self.vg()
        self.kb.v(lambda e, o=out_tb[osl], a=a_tb[asl], b=b_tb[bsl]: e.tensor_tensor(o, a, b, op),
                  reads=[a_tb, b_tb], writes=[out_tb], stream=st, tag=tag)

    def ts(self, out_tb, osl, in_tb, isl, s1, op0, s2=None, op1=None, stream=None, tag="ts"):
        st = stream or "V"
        if s2 is None:
            self.kb.v(lambda e, o=out_tb[osl], i=in_tb[isl]: e.tensor_single_scalar(o, i, s1, op0),
                      reads=[in_tb], writes=[out_tb], stream=st, tag=tag)
        else:
            self.kb.v(lambda e, o=out_tb[osl], i=in_tb[isl]: e.tensor_scalar(o, i, s1, s2, op0, op1),
                      reads=[in_tb], writes=[out_tb], stream=st, tag=tag)

    def ts_col(self, out_tb, osl, in_tb, isl, col_tb, cslot, op0, s2=None, op1=None,
               stream=None, tag="tsc"):
        st = stream or "V"
        cap = col_tb[(slice(None), slice(cslot, cslot + 1))]
        if s2 is None:
            self.kb.v(lambda e, o=out_tb[osl], i=in_tb[isl]: e.tensor_single_scalar(o, i, cap, op0),
                      reads=[in_tb, col_tb], writes=[out_tb], stream=st, tag=tag)
        else:
            self.kb.v(lambda e, o=out_tb[osl], i=in_tb[isl]: e.tensor_scalar(o, i, cap, s2, op0, op1),
                      reads=[in_tb, col_tb], writes=[out_tb], stream=st, tag=tag)

    def stt_col(self, out_tb, osl, in0_tb, i0sl, col_tb, cslot, in1_tb, i1sl, op0, op1,
                stream=None, tag="stt"):
        st = stream or "V"
        cap = col_tb[(slice(None), slice(cslot, cslot + 1))]
        self.kb.v(lambda e, o=out_tb[osl], a=in0_tb[i0sl], b=in1_tb[i1sl]:
                  e.scalar_tensor_tensor(o, a, cap, b, op0, op1),
                  reads=[in0_tb, in1_tb, col_tb], writes=[out_tb], stream=st, tag=tag)

    def stt(self, out_tb, osl, in0_tb, i0sl, scalar, in1_tb, i1sl, op0, op1,
            stream=None, tag="stt"):
        st = stream or "V"
        self.kb.v(lambda e, o=out_tb[osl], a=in0_tb[i0sl], b=in1_tb[i1sl]:
                  e.scalar_tensor_tensor(o, a, scalar, b, op0, op1),
                  reads=[in0_tb, in1_tb], writes=[out_tb], stream=st, tag=tag)

    def sc_settle(self, out_tb, osl, pdim=128, tag="stl"):
        last = (slice(0, min(pdim, 128)),
                slice(osl[1].stop - 1 if osl[1].stop is not None else out_tb.shape[1] - 1,
                      osl[1].stop if osl[1].stop is not None else out_tb.shape[1]))
        self.kb.op("SC", lambda e, i=out_tb[last]:
                   e.activation(self.junk[(slice(0, min(pdim, 128)), slice(0, 1))], i,
                                Act.Identity),
                   reads=[out_tb], writes=[self.junk], tag=tag)

    def _act_needs_settle(self, out_tb, osl, in_tb, isl):
        w = (osl[1].stop - (osl[1].start or 0)) if osl[1].stop is not None else out_tb.shape[1]
        return w <= 64 or (out_tb is in_tb and osl == isl)

    def act(self, out_tb, osl, in_tb, isl, func, bias=0.0, scale=1.0, pdim=128, tag="act"):
        reads = [in_tb]
        if isinstance(bias, float) and bias not in (0.0,):
            bias = self.cc(bias, pdim)
            reads.append(self.ccols)
        if isinstance(scale, float) and scale not in (1.0,):
            scale = self.cc(scale, pdim)
            reads.append(self.ccols)
        self.kb.op("SC", lambda e, o=out_tb[osl], i=in_tb[isl], b=bias, s=scale:
                   e.activation(o, i, func, bias=b, scale=s),
                   reads=reads, writes=[out_tb], tag=tag)
        if self._act_needs_settle(out_tb, osl, in_tb, isl):
            self.sc_settle(out_tb, osl, pdim=pdim, tag=tag + "s")

    def act_col(self, out_tb, osl, in_tb, isl, func, bias_cs=None, scale_cs=None,
                accum_cs=None, col_tb=None, tag="actc"):
        ct = col_tb if col_tb is not None else self.cols
        kw = {}
        reads = [in_tb]
        outs = [out_tb]
        bias = 0.0 if bias_cs is None else ct[(slice(None), slice(bias_cs, bias_cs + 1))]
        scale = 1.0 if scale_cs is None else ct[(slice(None), slice(scale_cs, scale_cs + 1))]
        if bias_cs is not None or scale_cs is not None:
            reads.append(ct)
        if accum_cs is not None:
            kw["accum_out"] = ct[(slice(None), slice(accum_cs, accum_cs + 1))]
            outs.append(ct)
        self.kb.op("SC", lambda e, o=out_tb[osl], i=in_tb[isl], b=bias, s=scale, kw=kw:
                   e.activation(o, i, func, bias=b, scale=s, **kw),
                   reads=reads, writes=outs, tag=tag)
        if accum_cs is not None:
            self.sc_settle(ct, (slice(None), slice(accum_cs, accum_cs + 1)), tag=tag + "s")
        elif self._act_needs_settle(out_tb, osl, in_tb, isl):
            self.sc_settle(out_tb, osl, tag=tag + "s")

    def bcast_row(self, row_off, n, out_tb, out_off, row_tb=None, tag="bc"):
        kb = self.kb
        rt = row_tb if row_tb is not None else self.rows
        for c0 in range(0, n, 512):
            w = min(512, n - c0)
            p = self.pb()
            kb.mm(p, (slice(None), slice(0, w)), self.ones, (slice(0, 1), slice(None)),
                  rt, (slice(0, 1), slice(row_off + c0, row_off + c0 + w)),
                  start=True, stop=True, tag=tag)
            self.evac(out_tb, _f(out_off + c0, out_off + c0 + w), p, _f(0, w), tag=tag + "e")

    def row_to_cols(self, row_off, n, cslot, tag="r2c"):
        kb = self.kb
        d = kb.dram(kb.uid("r2c"), (n,))
        kb.dma_ap(d, d.ap0[(slice(None),)],
                  self.rows, self.rows.ap0[(slice(0, 1), slice(row_off, row_off + n))],
                  stream="GD", tag=tag + "w")
        nj = n // 128
        kb.dma_ap(self.cols, self.cols.ap0[(slice(None), slice(cslot, cslot + nj))],
                  d, d.ap0.rearrange("(j p) -> p j", p=128),
                  stream="GD", tag=tag + "r", slow=True)

    def cols_to_row(self, cslot, nj, row_off, tag="c2r"):
        kb = self.kb
        d = kb.dram(kb.uid("c2r"), (nj * 128,))
        kb.dma_ap(d, d.ap0.rearrange("(j p) -> p j", p=128),
                  self.cols, self.cols.ap0[(slice(None), slice(cslot, cslot + nj))],
                  stream="GD", tag=tag + "w", slow=True)
        kb.dma_ap(self.rows, self.rows.ap0[(slice(0, 1), slice(row_off, row_off + nj * 128))],
                  d, d.ap0[(slice(None),)], stream="GD", tag=tag + "r")

    def transpose128(self, in_tb, isl, out_tb, osl, pr=128, pc=128, kp=128, tag="tr"):
        kb = self.kb
        p = self.pb()
        kb.op("T", lambda e: e.transpose(p[(slice(0, pc), slice(0, pr))], in_tb[isl],
                                         self.ident[(slice(0, kp), slice(0, kp))]),
              reads=[in_tb, self.ident], writes=[p], tag=tag)
        self.evac(out_tb, osl, p, (slice(0, pc), slice(0, pr)), tag=tag + "e")

    def colsum(self, tiles, out_off, scale, sq=False, tag="cs"):
        kb = self.kb
        sqt = self.take() if sq else None
        for c0 in range(0, S, 512):
            p = self.pb()
            for k, t in enumerate(tiles):
                if sq:
                    dsl = _f((k % 4) * 512, (k % 4) * 512 + 512)
                    self.act(sqt, dsl, t, _f(c0, c0 + 512), Act.Square, tag=tag + "sq")
                    rt, rsl = sqt, dsl
                else:
                    rt, rsl = t, _f(c0, c0 + 512)
                kb.mm(p, (slice(0, 1), slice(None)), self.onesc, FS,
                      rt, rsl, start=(k == 0), stop=(k == len(tiles) - 1), tag=tag)
            self.kb.v(lambda e, o=self.rows[(slice(0, 1), slice(out_off + c0, out_off + c0 + 512))],
                      i=p[(slice(0, 1), slice(0, 512))], sc=scale: e.tensor_scalar_mul(o, i, sc),
                      reads=[p], writes=[self.rows], stream="V", tag=tag + "ev")
        if sq:
            self.rel(sqt)

    def ln(self, src, dst, eps, tag, g_cs=None, b_cs=None, store=None):
        kb = self.kb
        MU, RS = 0, S
        self.colsum(src, MU, 1.0 / D, sq=False, tag=tag + "mu")
        self.colsum(src, RS, 1.0 / D, sq=True, tag=tag + "ss")
        mus = (slice(0, 1), slice(MU, MU + S))
        rss = (slice(0, 1), slice(RS, RS + S))
        mub = self.take()
        rsb = self.take()
        t2 = (slice(0, 1), slice(0, S))
        self.ew2(Alu.mult, mub, t2, self.rows, mus, self.rows, mus, stream="V",
                 tag=tag + "m2")
        self.ew2(Alu.subtract, self.rows, rss, self.rows, rss, mub, t2, stream="V",
                 tag=tag + "var")
        self.act(self.rows, rss, self.rows, rss, Act.Sqrt, bias=float(eps), pdim=1, tag=tag + "sd")
        kb.v(lambda e: e.reciprocal(self.rows[rss], self.rows[rss]),
             reads=[self.rows], writes=[self.rows], stream="VR", tag=tag + "rc")
        self.bcast_row(MU, S, mub, 0, tag=tag + "bm")
        self.bcast_row(RS, S, rsb, 0, tag=tag + "br")
        for k in range(ND):
            st = self.vg()
            d = dst[k % len(dst)] if store is not None else dst[k]
            self.ew2(Alu.subtract, d, FS, src[k], FS, mub, FS, stream=st, tag=tag + "s")
            self.ew2(Alu.mult, d, FS, d, FS, rsb, FS, stream=st, tag=tag + "m")
            if g_cs is not None:
                self.act_col(d, FS, d, FS, Act.Identity, bias_cs=b_cs + k,
                             scale_cs=g_cs + k, tag=tag + "aff")
            if store is not None:
                self.kb.dma(store, _p(k * 128, (k + 1) * 128), d, FS, tag=tag + "st")
        self.rel(mub, rsb)

    def load_col(self, dram_tb, row0, cslot, n=1, tag="lc"):
        for j in range(n):
            self.kb.dma(self.cols, (slice(None), slice(cslot + j, cslot + j + 1)),
                        dram_tb, (slice(row0 + j * 128, row0 + (j + 1) * 128), slice(0, 1)),
                        tag=tag)

def _fq_round(mk, out_tb, osl, src_tb, ssl, si_cs, scr_tb, scrsl, lo, hi, tag="fq"):
    kb = mk.kb
    if si_cs is None:
        kb.v(lambda e, o=scr_tb.i32(scrsl), i=src_tb[ssl]: e.tensor_copy(o, i),
             reads=[src_tb], writes=[scr_tb], stream="V", tag=tag + "q")
        kb.v(lambda e, o=out_tb[osl], i=scr_tb.i32(scrsl): e.tensor_copy(o, i),
             reads=[scr_tb], writes=[out_tb], stream="V", tag=tag + "r0")
        kb.v(lambda e, o=scr_tb[scrsl], a=src_tb[ssl], b=out_tb[osl]:
             e.tensor_sub(o, a, b),
             reads=[src_tb, out_tb], writes=[scr_tb], stream="V", tag=tag + "d")
    else:
        cap = mk.cols[(slice(None), slice(si_cs, si_cs + 1))]
        kb.v(lambda e, o=scr_tb.i32(scrsl), i=src_tb[ssl]: e.tensor_single_scalar(o, i, cap, Alu.mult),
             reads=[src_tb, mk.cols], writes=[scr_tb], stream="V", tag=tag + "q")
        kb.v(lambda e, o=out_tb[osl], i=scr_tb.i32(scrsl): e.tensor_copy(o, i),
             reads=[scr_tb], writes=[out_tb], stream="V", tag=tag + "r0")
        kb.v(lambda e, o=scr_tb[scrsl], a=src_tb[ssl], b=out_tb[osl]:
             e.scalar_tensor_tensor(o, a, cap, b, Alu.mult, Alu.subtract),
             reads=[src_tb, out_tb, mk.cols], writes=[scr_tb], stream="V", tag=tag + "d")
    mk.stt(out_tb, osl, scr_tb, scrsl, 0.5, out_tb, osl, Alu.is_gt, Alu.add, tag=tag + "c1")
    mk.stt(out_tb, osl, scr_tb, scrsl, -0.5, out_tb, osl, Alu.is_ge, Alu.add, tag=tag + "c2")
    mk.ts(out_tb, osl, out_tb, osl, hi + 1.0, Alu.min, lo + 1.0, Alu.max, tag=tag + "cl")
    mk.ts(out_tb, osl, out_tb, osl, -1.0, Alu.add, tag=tag + "m1")


def _colmax_to_scalar(mk, cslot, out_cslot, tag="cms"):
    kb = mk.kb
    p = mk.pb()
    kb.op("T", lambda e: e.transpose(p[(slice(0, 1), slice(0, 128))],
                                     mk.cols[(slice(None), slice(cslot, cslot + 1))],
                                     mk.ident.full),
          reads=[mk.cols, mk.ident], writes=[p], tag=tag + "t")
    row = mk.cmsrow
    kb.v(lambda e: e.tensor_copy(row[(slice(0, 1), slice(0, 128))],
                                 p[(slice(0, 1), slice(0, 128))]),
         reads=[p], writes=[row], stream="V", tag=tag + "e")
    kb.v(lambda e: e.reduce_max(row[(slice(0, 1), slice(128, 129))],
                                row[(slice(0, 1), slice(0, 128))], axis=XAX),
         reads=[row], writes=[row], stream="VR", tag=tag + "r")
    return row


def _allreduce_max(mk, row_tb, row_off, out_cslot, tag="cc"):
    kb = mk.kb
    cin = kb.dram(kb.uid("ccin"), (1, 1))
    cout = kb.dram(kb.uid("ccout"), (1, 1))
    kb.dma(cin, (slice(0, 1), slice(0, 1)), row_tb,
           (slice(0, 1), slice(row_off, row_off + 1)), stream="GD", tag=tag + "w")
    kb.op("CC", lambda e: e.collective_compute(
        "AllReduce", Alu.max, ins=[cin.full], outs=[cout.full],
        replica_groups=[mk.core_ids]), reads=[cin], writes=[cout], tag=tag)
    kb.dma_ap(mk.cols, mk.cols.ap0[(slice(None), slice(out_cslot, out_cslot + 1))],
              cout, cout.ap0[(slice(0, 1), slice(0, 1))].to_broadcast([128, 1]),
              stream="SD", tag=tag + "b")


MK.fq = _fq_round
MK.colmax_to_scalar = _colmax_to_scalar
MK.allreduce_max = _allreduce_max


def emit_layer0(mk: MK, x8):
    kb = mk.kb
    mk._col_n = 0
    fc1_q = kb.dram("L0_fc1_q", (D, 4096), I32, kind="ExternalInput")
    fc1_sb = kb.dram("L0_fc1_sb", (D, 4096), kind="ExternalInput")
    fc1_b = kb.dram("L0_fc1_b", (4096, 1), kind="ExternalInput")
    fc2_q = kb.dram("L0_fc2_q", (4096, D), I32, kind="ExternalInput")
    fc2_sb = kb.dram("L0_fc2_sb", (4096, D), kind="ExternalInput")
    fc2_b = kb.dram("L0_fc2_b", (D, 1), kind="ExternalInput")
    ln_g = kb.dram("L0_ln_g", (D, 1), kind="ExternalInput")
    ln_b = kb.dram("L0_ln_b", (D, 1), kind="ExternalInput")
    etd = kb.dram("L0_etd", (S, S))
    xtd = kb.dram("L0_xtd", (S, D))
    hd = kb.dram("L0_hd", (D, S))
    w1fd = kb.dram("L0_w1f", (D, 4096))
    w2fd = kb.dram("L0_w2f", (4096, D))

    for k in range(ND):
        kb.dma(mk.xd, _p(k * 128, (k + 1) * 128), x8[k], FS, tag="x_st")
    xn8 = x8
    mk.ln(x8, xn8, 1e-5, "l0ln")
    for k in range(ND):
        kb.dma(mk.xnd, _p(k * 128, (k + 1) * 128), xn8[k], FS, tag="xn_st")

    g_cs = mk.colslot(8)
    b_cs = mk.colslot(8)
    mk.load_col(ln_g, 0, g_cs, 8, tag="l0g")
    mk.load_col(ln_b, 0, b_cs, 8, tag="l0b")
    h2 = mk.take(2)
    mk.ln(xn8, h2, 1e-5, "l0ln2", g_cs=g_cs, b_cs=b_cs, store=hd)
    mk.rel(h2)

    NRM = 0
    mk.colsum(xn8, NRM, 1.0, sq=True, tag="a0n")
    nsl = (slice(0, 1), slice(NRM, NRM + S))
    mk.act(mk.rows, nsl, mk.rows, nsl, Act.Sqrt, pdim=1, tag="a0sq")
    kb.v(lambda e: e.reciprocal(mk.rows[nsl], mk.rows[nsl]), reads=[mk.rows],
         writes=[mk.rows], stream="VR", tag="a0rc")
    rb = mk.take()
    mk.bcast_row(NRM, S, rb, 0, tag="a0rb")
    r_cs = mk.colslot(16)
    mk.row_to_cols(NRM, S, r_cs, tag="a0r2c")
    m_cs = mk.colslot(16)
    s_cs = mk.colslot(16)

    sim = mk.take()
    et = mk.take()
    for a in range(16):
        for c0 in range(0, S, 512):
            p = mk.pb()
            for k in range(ND):
                kb.mm(p, (slice(None), slice(0, 512)),
                      xn8[k], _f(a * 128, (a + 1) * 128),
                      xn8[k], _f(c0, c0 + 512),
                      start=(k == 0), stop=(k == ND - 1), tag="a0S")
            fsl = _f(c0, c0 + 512)
            mk.ts_col(sim, fsl, rb, fsl, mk.cols, r_cs + a, Alu.mult, 1e8, Alu.min,
                      stream="V", tag="a0f")
            mk.kb.v(lambda e, o=sim[fsl], i=p[(slice(None), slice(0, 512))]:
                    e.tensor_mul(o, o, i), reads=[p, sim], writes=[sim], stream="V",
                    tag="a0sm")
        kb.v(lambda e, o=mk.cols[(slice(None), slice(m_cs + a, m_cs + a + 1))], i=sim.full:
             e.reduce_max(o, i, axis=XAX, negate=True),
             reads=[sim], writes=[mk.cols], stream="VR", tag="a0mx")
        mk.act_col(sim, FS, sim, FS, Act.Exp, bias_cs=m_cs + a, accum_cs=s_cs + a, tag="a0e")
        for b in range(16):
            mk.transpose128(sim, _f(b * 128, (b + 1) * 128), et, _f(b * 128, (b + 1) * 128),
                            tag="a0tr")
        kb.dma_ap(etd, etd.ap0[:, a * 128:(a + 1) * 128].rearrange("(b p) i -> p b i", p=128),
                  et, et.ap0.rearrange("p (b i) -> p b i", i=128), tag="a0etst")
    xtk = mk.take()
    for t in range(16):
        for k in range(ND):
            mk.transpose128(xn8[k], _f(t * 128, (t + 1) * 128), xtk,
                            _f(k * 128, (k + 1) * 128), tag="a0xt")
        kb.dma(xtd, _p(t * 128, (t + 1) * 128), xtk, _f(0, D), tag="a0xtst")
    mk.rel(sim, et, xtk, rb, xn8)

    xtok = mk.take(8)
    for u in range(8):
        kb.dma_ap(xtok[u], xtok[u].ap0.rearrange("p (c d) -> p c d", d=D),
                  xtd, xtd.ap0[2 * u * 128:(2 * u + 2) * 128, :].rearrange(
                      "(c p) d -> p c d", p=128), tag="a0xtld")
    att8 = mk.take(8)
    etb = mk.take()
    for a in range(16):
        kb.dma_ap(etb, etb.ap0.rearrange("p (b i) -> p b i", i=128),
                  etd, etd.ap0[:, a * 128:(a + 1) * 128].rearrange("(b p) i -> p b i", p=128),
                  tag="a0etld")
        for m in range(ND):
            p = mk.pb()
            for j in range(16):
                kb.mm(p, (slice(None), slice(0, 128)),
                      xtok[j // 2], _f((j % 2) * D + m * 128, (j % 2) * D + (m + 1) * 128),
                      etb, _f(j * 128, (j + 1) * 128),
                      start=(j == 0), stop=(j == 15), tag="a0AV")
            mk.evac(att8[m], _f(a * 128, (a + 1) * 128), p, (slice(None), slice(0, 128)),
                    tag="a0AVe")
    mk.rel(xtok, etb)
    rs_cs = mk.colslot(16)
    kb.v(lambda e: e.reciprocal(mk.cols[(slice(None), slice(rs_cs, rs_cs + 16))],
                                mk.cols[(slice(None), slice(s_cs, s_cs + 16))]),
         reads=[mk.cols], writes=[mk.cols], stream="VR", tag="a0rs")
    mk.cols_to_row(rs_cs, 16, 0, tag="a0c2r")
    sbb = mk.take()
    mk.bcast_row(0, S, sbb, 0, tag="a0sbb")
    for m in range(ND):
        mk.ew2(Alu.mult, att8[m], FS, att8[m], FS, sbb, FS, tag="a0nm")
        kb.dma(mk.attd, _p(m * 128, (m + 1) * 128), att8[m], FS, tag="a0st")
    mk.rel(att8, sbb)

    sa, sb_ = mk.slab
    for k in range(ND):
        kb.dma(sa, FS, fc1_q, _p(k * 128, (k + 1) * 128), stream="GD", tag="f0qld")
        kb.dma(sb_, FS, fc1_sb, _p(k * 128, (k + 1) * 128), tag="f0sld")
        mk.stt(sa, FS, sa, FS, -128.0, sb_, FS, Alu.add, Alu.mult, stream="V", tag="f0dq")
        kb.dma(w1fd, _p(k * 128, (k + 1) * 128), sa, FS, tag="f0wst")
    for k2 in range(32):
        qs = _f(0, D)
        ss = _f(D, 2 * D)
        kb.dma(sa, qs, fc2_q, _p(k2 * 128, (k2 + 1) * 128), stream="GD", tag="f0q2")
        kb.dma(sa, ss, fc2_sb, _p(k2 * 128, (k2 + 1) * 128), tag="f0s2")
        mk.stt(sa, qs, sa, qs, -128.0, sa, ss, Alu.add, Alu.mult, stream="V", tag="f0dq2")
        kb.dma(w2fd, _p(k2 * 128, (k2 + 1) * 128), sa, qs, tag="f0w2st")

    b1_cs = mk.colslot(32)
    mk.load_col(fc1_b, 0, b1_cs, 32, tag="f0b1")
    b2_cs = mk.colslot(8)
    mk.load_col(fc2_b, 0, b2_cs, 8, tag="f0b2")
    b2h_cs = mk.colslot(8)
    kb.v(lambda e: e.tensor_scalar_mul(mk.cols[(slice(None), slice(b2h_cs, b2h_cs + 8))],
                                       mk.cols[(slice(None), slice(b2_cs, b2_cs + 8))], 0.5),
         reads=[mk.cols], writes=[mk.cols], stream="V", tag="f0bh")

    hch = mk.take(2)
    y4 = mk.take(8)
    wsl = mk.take(2)
    w2s = mk.take(2)
    zt = mk.take(2)
    for n in range(4):
        nsl_ = slice(n * 512, (n + 1) * 512)
        for k in range(ND):
            kb.dma(hch[k // 4], _f((k % 4) * 512, (k % 4) * 512 + 512),
                   hd, (slice(k * 128, (k + 1) * 128), nsl_), tag="f0hld")
        for mi in range(32):
            ws = wsl[mi % 2]
            kb.dma_ap(ws, ws.ap0[:, 0:1024].rearrange("p (k m) -> p k m", m=128),
                      w1fd, w1fd.ap0[:, mi * 128:(mi + 1) * 128].rearrange(
                          "(k p) m -> p k m", p=128), tag="f0wld")
            p = mk.pb()
            for k in range(ND):
                kb.mm(p, (slice(None), slice(0, 512)),
                      ws, _f(k * 128, (k + 1) * 128),
                      hch[k // 4], _f((k % 4) * 512, (k % 4) * 512 + 512),
                      start=(k == 0), stop=(k == ND - 1), tag="f0mm1")
            mk.act_col(y4[mi // 4], _f((mi % 4) * 512, (mi % 4) * 512 + 512),
                       p, (slice(None), slice(0, 512)), Act.Silu, bias_cs=b1_cs + mi,
                       tag="f0silu")
        for mi in range(32):
            w2t = w2s[mi % 2]
            kb.dma(w2t, _f(0, D), w2fd, _p(mi * 128, (mi + 1) * 128), tag="f0w2ld")
            for o in range(8):
                kb.mm(mk.psum[o], (slice(None), slice(0, 512)),
                      w2t, _f(o * 128, (o + 1) * 128),
                      y4[mi // 4], _f((mi % 4) * 512, (mi % 4) * 512 + 512),
                      start=(mi == 0), stop=(mi == 31), tag="f0mm2")
        for o in range(8):
            bap = mk.cols[(slice(None), slice(b2h_cs + o, b2h_cs + o + 1))]
            kb.op("SC", lambda e, o_=zt[o // 4][_f((o % 4) * 512, (o % 4) * 512 + 512)],
                  i=mk.psum[o][(slice(None), slice(0, 512))], b=bap, s=mk.cc(0.5):
                  e.activation(o_, i, Act.Identity, bias=b, scale=s),
                  reads=[mk.psum[o], mk.cols, mk.ccols], writes=[zt[o // 4]], tag="f0zev")
        for o in range(8):
            kb.dma_acc(mk.attd, (slice(o * 128, (o + 1) * 128), nsl_),
                       zt[o // 4], _f((o % 4) * 512, (o % 4) * 512 + 512), tag="f0zacc")
    mk._ps = 0
    mk.rel(hch, y4, wsl, w2s, zt)

    xnew = mk.take(8)
    ta = mk.take()
    tb2 = mk.take()
    for m in range(ND):
        msl = _p(m * 128, (m + 1) * 128)
        kb.dma(xnew[m], FS, mk.xd, msl, tag="c0x")
        kb.dma(ta, FS, mk.attd, msl, tag="c0a")
        kb.dma(tb2, FS, mk.xnd, msl, tag="c0n")
        mk.ew2(Alu.add, xnew[m], FS, xnew[m], FS, ta, FS, stream="V", tag="c0s1")
        mk.stt(xnew[m], FS, tb2, FS, 2.0, xnew[m], FS, Alu.mult, Alu.add, stream="V",
               tag="c0s2")
    mk.rel(ta, tb2)
    return xnew



def emit_layer1(mk: MK, x8):
    kb = mk.kb
    mk._col_n = 0
    WqT = kb.dram("L1_WqT", (D, D), kind="ExternalInput")
    WkT = kb.dram("L1_WkT", (D, D), kind="ExternalInput")
    WvT = kb.dram("L1_WvT", (D, D), kind="ExternalInput")
    WpT = kb.dram("L1_WpT", (D, D), kind="ExternalInput")
    bp = kb.dram("L1_bp", (D, 1), kind="ExternalInput")
    bng = kb.dram("L1_bng", (D, 1), kind="ExternalInput")
    bnb = kb.dram("L1_bnb", (D, 1), kind="ExternalInput")
    bnm = kb.dram("L1_bnm", (D, 1), kind="ExternalInput")
    bnv = kb.dram("L1_bnv", (D, 1), kind="ExternalInput")
    clip_x = kb.dram("L1_clip_x", (1, 1), kind="ExternalInput")
    clip_d = kb.dram("L1_clip_d", (1, 1), kind="ExternalInput")
    I1 = 2816
    NI = I1 // 128
    wgN = kb.dram("L1_wgN", (I1, D), kind="ExternalInput")
    wuN = kb.dram("L1_wuN", (I1, D), kind="ExternalInput")
    wdN = kb.dram("L1_wdN", (D, I1), kind="ExternalInput")
    wgT = kb.dram("L1_wgT", (D, I1), kind="ExternalInput")
    wuT = kb.dram("L1_wuT", (D, I1), kind="ExternalInput")
    wdT = kb.dram("L1_wdT", (I1, D), kind="ExternalInput")
    iwg_d = kb.dram("L1_iwg", (D, I1), BF16)
    iwu_d = kb.dram("L1_iwu", (D, I1), BF16)
    iwd_d = kb.dram("L1_iwd", (I1, D), BF16)
    xrd = kb.dram("L1_xrd", (D, 512))
    hmd = kb.dram("L1_hmd", (I1, S))
    ihd = kb.dram("L1_ihd", (I1, S), BF16)

    CS = lambda a, n=1: (slice(None), slice(a, a + n))

    for k in range(ND):
        kb.dma(mk.xd, _p(k * 128, (k + 1) * 128), x8[k], FS, tag="l1x_st")
    xn8 = x8
    mk.ln(x8, xn8, 1e-5, "l1ln")
    for k in range(ND):
        kb.dma(mk.xnd, _p(k * 128, (k + 1) * 128), xn8[k], FS, tag="l1xn_st")

    gm_cs = mk.colslot(1)
    gtmp = kb.sb(kb.uid("gm"), (128, 1))
    for k in range(ND):
        if k == 0:
            kb.v(lambda e: e.tensor_reduce(mk.cols[CS(gm_cs)], xn8[0].full, axis=XAX,
                                           op=Alu.max, apply_absolute_value=True),
                 reads=[xn8[0]], writes=[mk.cols], stream="VR", tag="l1gm")
        else:
            kb.v(lambda e, i=xn8[k].full: e.tensor_reduce(gtmp.full, i, axis=XAX,
                                                          op=Alu.max, apply_absolute_value=True),
                 reads=[xn8[k]], writes=[gtmp], stream="VR", tag="l1gm")
            kb.v(lambda e: e.tensor_tensor(mk.cols[CS(gm_cs)], mk.cols[CS(gm_cs)],
                                           gtmp.full, Alu.max),
                 reads=[gtmp, mk.cols], writes=[mk.cols], stream="V", tag="l1gmx")
    grow = mk.colmax_to_scalar(gm_cs, None, tag="l1cms")
    gx_cs = mk.colslot(1)
    mk.allreduce_max(grow, 128, gx_cs, tag="l1cc1")
    sx_cs = mk.colslot(1)
    six_cs = mk.colslot(1)
    kb.dma_ap(mk.cols, mk.cols.ap0[CS(sx_cs)],
              clip_x, clip_x.ap0[(slice(0, 1), slice(0, 1))].to_broadcast([128, 1]),
              stream="SD", tag="l1clipx")
    mk.act_col(mk.cols, CS(sx_cs), mk.cols, CS(sx_cs), Act.Sigmoid, tag="l1sig")
    kb.v(lambda e: e.tensor_mul(mk.cols[CS(sx_cs)], mk.cols[CS(sx_cs)], mk.cols[CS(gx_cs)]),
         reads=[mk.cols], writes=[mk.cols], stream="V", tag="l1sx1")
    mk.ts(mk.cols, CS(sx_cs), mk.cols, CS(sx_cs), 1.0 / 127.0, Alu.mult, stream="V", tag="l1sx2")
    kb.v(lambda e: e.reciprocal(mk.cols[CS(six_cs)], mk.cols[CS(sx_cs)]),
         reads=[mk.cols], writes=[mk.cols], stream="VR", tag="l1six")

    bnr = mk.colslot(8)
    bnbi = mk.colslot(8)
    tmpc = mk.colslot(8)
    mk.load_col(bnv, 0, bnr, 8, tag="l1bnv")
    mk.load_col(bng, 0, tmpc, 8, tag="l1bng")
    mk.ts(mk.cols, CS(bnr, 8), mk.cols, CS(bnr, 8), 1e-6, Alu.add, stream="V", tag="l1bn1")
    mk.act_col(mk.cols, CS(bnr, 8), mk.cols, CS(bnr, 8), Act.Sqrt, tag="l1bn2")
    kb.v(lambda e: e.reciprocal(mk.cols[CS(bnr, 8)], mk.cols[CS(bnr, 8)]),
         reads=[mk.cols], writes=[mk.cols], stream="VR", tag="l1bn3")
    kb.v(lambda e: e.tensor_mul(mk.cols[CS(bnr, 8)], mk.cols[CS(bnr, 8)], mk.cols[CS(tmpc, 8)]),
         reads=[mk.cols], writes=[mk.cols], stream="V", tag="l1bn4")
    mk.load_col(bnm, 0, tmpc, 8, tag="l1bnm")
    kb.v(lambda e: e.tensor_mul(mk.cols[CS(bnbi, 8)], mk.cols[CS(tmpc, 8)], mk.cols[CS(bnr, 8)]),
         reads=[mk.cols], writes=[mk.cols], stream="V", tag="l1bn5")
    mk.load_col(bnb, 0, tmpc, 8, tag="l1bnb")
    kb.v(lambda e: e.tensor_sub(mk.cols[CS(bnbi, 8)], mk.cols[CS(tmpc, 8)], mk.cols[CS(bnbi, 8)]),
         reads=[mk.cols], writes=[mk.cols], stream="V", tag="l1bn6")

    mk.tap("l1cols", mk.cols)
    xr2 = mk.take(2)
    for k in range(ND):
        xt = xr2[k // 4]
        osl = (slice(None), slice((k % 4) * 512, (k % 4) * 512 + 512))
        aps = [xn8[k].ap0[(slice(None), slice(j, S, 4))] for j in range(4)]
        st = mk.vg()
        kb.v(lambda e, o=xt[osl], a=aps[0], b=aps[1]: e.tensor_add(o, a, b),
             reads=[xn8[k]], writes=[xt], stream=st, tag="l1p1")
        kb.v(lambda e, o=xt[osl], a=aps[2]: e.tensor_add(o, o, a),
             reads=[xn8[k], xt], writes=[xt], stream=st, tag="l1p2")
        kb.v(lambda e, o=xt[osl], a=aps[3]: e.tensor_add(o, o, a),
             reads=[xn8[k], xt], writes=[xt], stream=st, tag="l1p3")
        mk.ts(xt, osl, xt, osl, 0.25, Alu.mult, stream=st, tag="l1p4")
        if k == 0:
            mk.tap("l1xraw", xt, sl=osl, shape=(128, 512))
        mk.act_col(xt, osl, xt, osl, Act.Identity, bias_cs=bnbi + k, scale_cs=bnr + k,
                   tag="l1bna")
        kb.dma(xrd, _p(k * 128, (k + 1) * 128), xr2[k // 4], osl, tag="l1xr_st")
    mk.rel(xr2)

    mk.tap("l1xrd", xrd)
    q8 = mk.take(8)
    wsl = mk.take(1)
    for m in range(ND):
        half = (m % 2) * 1024
        kb.dma_ap(wsl, wsl.ap0[(slice(None), slice(half, half + 1024))].rearrange(
            "p (k m) -> p k m", m=128),
            WqT, WqT.ap0[:, m * 128:(m + 1) * 128].rearrange("(k p) m -> p k m", p=128),
            tag="l1wq")
        for n in range(4):
            p = mk.pb()
            for k in range(ND):
                kb.mm(p, (slice(None), slice(0, 512)),
                      wsl, _f(half + k * 128, half + (k + 1) * 128),
                      xn8[k], _f(n * 512, (n + 1) * 512),
                      start=(k == 0), stop=(k == ND - 1), tag="l1qmm")
            mk.evac(q8[m], _f(n * 512, (n + 1) * 512), p, (slice(None), slice(0, 512)),
                    tag="l1qev")
    mk.rel(xn8, wsl)

    for m in range(ND):
        mk.tap(f"l1q{m}", q8[m])
    xr2 = mk.take(2)
    for k in range(ND):
        kb.dma(xr2[k // 4], (slice(None), slice((k % 4) * 512, (k % 4) * 512 + 512)),
               xrd, _p(k * 128, (k + 1) * 128), tag="l1xr_ld")
    ktok = mk.take(2)
    vtok = mk.take(2)
    for wT, dst, tg in [(WkT, ktok, "l1k"), (WvT, vtok, "l1v")]:
        for c in range(ND):
            kb.dma(mk.slab[c // 4], (slice(None), slice((c % 4) * 1024, (c % 4) * 1024 + 1024)),
                   wT, _p(c * 128, (c + 1) * 128), tag=tg + "w")
        for tb in range(4):
            for j2 in range(2):
                p = mk.pb()
                for c in range(ND):
                    kb.mm(p, (slice(None), slice(0, 512)),
                          xr2[c // 4], _f((c % 4) * 512 + tb * 128,
                                          (c % 4) * 512 + (tb + 1) * 128),
                          mk.slab[c // 4], _f((c % 4) * 1024 + j2 * 512,
                                              (c % 4) * 1024 + (j2 + 1) * 512),
                          start=(c == 0), stop=(c == ND - 1), tag=tg)
                mk.evac(dst[tb // 2], _f((tb % 2) * 1024 + j2 * 512,
                                         (tb % 2) * 1024 + (j2 + 1) * 512),
                        p, (slice(None), slice(0, 512)), tag=tg + "e")
    for u in range(2):
        mk.tap(f"l1ktok{u}", ktok[u])
        mk.tap(f"l1vtok{u}", vtok[u])
    mk.rel(xr2)

    kvbd = mk.take(1)
    for g in range(ND):
        bd = (slice(None), slice((g % 2) * 1024, (g % 2) * 1024 + 128))
        kb.v(lambda e, o=kvbd[(slice(None), slice((g % 2) * 1024, (g % 2) * 1024 + 128))]:
             e.memset(o, 0.0), reads=[], writes=[kvbd], stream="V", tag="l1bz")
        for j in range(4):
            h = 4 * g + j
            p = mk.pb()
            for c in range(4):
                u, off = c // 2, (c % 2) * 1024
                kb.mm(p, (slice(0, 32), slice(0, 32)),
                      ktok[u], _f(off + h * 32, off + (h + 1) * 32),
                      vtok[u], _f(off + h * 32, off + (h + 1) * 32),
                      start=(c == 0), stop=(c == 3), tag="l1kv")
            kb.op("SC", lambda e, o=kvbd[(slice(32 * j, 32 * j + 32),
                                          slice((g % 2) * 1024 + 32 * j,
                                                (g % 2) * 1024 + 32 * j + 32))],
                  i=p[(slice(0, 32), slice(0, 32))], s=mk.cc(0.17677669529663687, 32):
                  e.activation(o, i, Act.Identity, scale=s),
                  reads=[p, mk.ccols], writes=[kvbd], tag="l1kve")
        mk.sc_settle(kvbd, (slice(None), slice((g % 2) * 1024 + 127, (g % 2) * 1024 + 128)),
                     tag="l1kvs")
        for n in range(4):
            p = mk.pb()
            kb.mm(p, (slice(None), slice(0, 512)),
                  kvbd, (slice(None), slice((g % 2) * 1024, (g % 2) * 1024 + 128)),
                  q8[g], _f(n * 512, (n + 1) * 512),
                  start=True, stop=True, tag="l1ao")
            mk.evac(q8[g], _f(n * 512, (n + 1) * 512), p, (slice(None), slice(0, 512)),
                    tag="l1aoev")
    for g in range(ND):
        mk.tap(f"l1ao{g}", q8[g])
    mk.rel(ktok, vtok, kvbd)

    bp_cs = mk.colslot(8)
    mk.load_col(bp, 0, bp_cs, 8, tag="l1bp")
    wsl = mk.take(1)
    am2 = mk.take(2)
    for m in range(ND):
        half = (m % 2) * 1024
        kb.dma_ap(wsl, wsl.ap0[(slice(None), slice(half, half + 1024))].rearrange(
            "p (k m) -> p k m", m=128),
            WpT, WpT.ap0[:, m * 128:(m + 1) * 128].rearrange("(k p) m -> p k m", p=128),
            tag="l1wp")
        at = am2[m % 2]
        for n in range(4):
            p = mk.pb()
            for k in range(ND):
                kb.mm(p, (slice(None), slice(0, 512)),
                      wsl, _f(half + k * 128, half + (k + 1) * 128),
                      q8[k], _f(n * 512, (n + 1) * 512),
                      start=(k == 0), stop=(k == ND - 1), tag="l1pmm")
            mk.act_col(at, _f(n * 512, (n + 1) * 512), p, (slice(None), slice(0, 512)),
                       Act.Identity, bias_cs=bp_cs + m, tag="l1pev")
        kb.dma(mk.attd, _p(m * 128, (m + 1) * 128), at, FS, tag="l1att_st")
    mk.tap("l1att", mk.attd)
    mk.rel(q8, wsl, am2)

    def w_scales(wN, n_i, tg):
        sc = mk.colslot(((n_i + 127) // 128))
        inv = mk.colslot(((n_i + 127) // 128))
        wt = mk.take(1) if wN.shape[1] == D else None
        for it in range(n_i // 128):
            if wN.shape[1] == D:
                buf, bsl = wt, (slice(None), slice(0, D))
                kb.dma(buf, bsl, wN, _p(it * 128, (it + 1) * 128), tag=tg + "ld")
            else:
                buf, bsl = mk.slab[0], (slice(None), slice(0, I1))
                kb.dma(buf, bsl, wN, _p(it * 128, (it + 1) * 128), tag=tg + "ld")
            kb.v(lambda e, o=mk.cols[CS(sc + it)], i=buf[bsl]:
                 e.tensor_reduce(o, i, axis=XAX, op=Alu.max, apply_absolute_value=True),
                 reads=[buf], writes=[mk.cols], stream="VR", tag=tg + "mx")
        nslots = n_i // 128
        mk.ts(mk.cols, CS(sc, nslots), mk.cols, CS(sc, nslots), 1.0 / 127.0, Alu.mult,
              stream="V", tag=tg + "sc")
        kb.v(lambda e: e.reciprocal(mk.cols[CS(inv, nslots)], mk.cols[CS(sc, nslots)]),
             reads=[mk.cols], writes=[mk.cols], stream="VR", tag=tg + "iv")
        if wt is not None:
            mk.rel(wt)
        return sc, inv

    g_sc, g_inv = w_scales(wgN, I1, "l1sg")
    u_sc, u_inv = w_scales(wuN, I1, "l1su")
    d_sc, d_inv = w_scales(wdN, D, "l1sd")

    def w_round(wT, inv_cs, n_free, iw_d, nk, tg):
        nj = n_free // 128
        mk.cols_to_row(inv_cs, nj, 0, tag=tg + "c2r")
        ib = mk.take(2)
        mk.bcast_row(0, n_free, ib[0], 0, tag=tg + "bc") if n_free <= S else None
        if n_free > S:
            mk.bcast_row(0, S, ib[0], 0, tag=tg + "bc")
            mk.bcast_row(S, n_free - S, ib[1], 0, tag=tg + "bc2")
        scr = mk.take(1)
        out = mk.take(1) if n_free <= S else mk.slab[1]
        for kkt in range(nk):
            buf = mk.slab[0]
            kb.dma(buf, (slice(None), slice(0, n_free)), wT, _p(kkt * 128, (kkt + 1) * 128),
                   tag=tg + "ld")
            for c0 in range(0, n_free, S):
                w_ = min(S, n_free - c0)
                ibt = ib[c0 // S]
                kb.v(lambda e, o=out[(slice(None), slice(c0, c0 + w_))] if n_free > S
                     else out[(slice(None), slice(0, w_))],
                     a=buf[(slice(None), slice(c0, c0 + w_))], b=ibt[(slice(None), slice(0, w_))]:
                     e.tensor_mul(o, a, b),
                     reads=[buf, ibt], writes=[out], stream="V", tag=tg + "u")
            for c0 in range(0, n_free, S):
                w_ = min(S, n_free - c0)
                osl = (slice(None), slice(c0, c0 + w_)) if n_free > S else (slice(None), slice(0, w_))
                mk.fq(buf, (slice(None), slice(c0, c0 + w_)), out, osl, None,
                      scr, (slice(None), slice(0, w_)), -128.0, 127.0, tag=tg + "fq")
            kb.v(lambda e, o=out.bf((slice(None), slice(0, n_free))),
                 i=buf[(slice(None), slice(0, n_free))]: e.tensor_copy(o, i),
                 reads=[buf], writes=[out], stream="V", tag=tg + "c16")
            kb.dma_ap(iw_d, iw_d.ap0[(slice(kkt * 128, (kkt + 1) * 128), slice(None))],
                      out, out.bf((slice(None), slice(0, n_free))), tag=tg + "st")
        mk.rel(ib, scr)
        if n_free <= S:
            mk.rel(out)

    w_round(wgT, g_inv, I1, iwg_d, ND, "l1rg")
    w_round(wuT, u_inv, I1, iwu_d, ND, "l1ru")
    w_round(wdT, d_inv, D, iwd_d, NI, "l1rd")

    gs2 = mk.colslot(NI)
    us2 = mk.colslot(NI)
    kb.v(lambda e: e.tensor_single_scalar(mk.cols[CS(gs2, NI)], mk.cols[CS(g_sc, NI)],
                                          mk.cols[CS(sx_cs)], Alu.mult),
         reads=[mk.cols], writes=[mk.cols], stream="V", tag="l1gs2")
    kb.v(lambda e: e.tensor_single_scalar(mk.cols[CS(us2, NI)], mk.cols[CS(u_sc, NI)],
                                          mk.cols[CS(sx_cs)], Alu.mult),
         reads=[mk.cols], writes=[mk.cols], stream="V", tag="l1us2")

    mk.tap("l1iwg", iwg_d)
    bxq = mk.take(4)
    ld = mk.take(1)
    tq = mk.take(1)
    scr = mk.take(1)
    for k in range(ND):
        kb.dma(ld, FS, mk.xnd, _p(k * 128, (k + 1) * 128), tag="l1xq_ld")
        mk.fq(tq, FS, ld, FS, six_cs, scr, FS, -128.0, 127.0, tag="l1xq")
        kb.v(lambda e, o=bxq[k // 2].bf((slice(None), slice((k % 2) * S, (k % 2) * S + S))),
             i=tq.full: e.tensor_copy(o, i),
             reads=[tq], writes=[bxq[k // 2]], stream="V", tag="l1xqc")
    mk.rel(ld, tq, scr)

    hmx_cs = mk.colslot(1)
    kb.v(lambda e: e.memset(mk.cols[CS(hmx_cs)], 0.0), reads=[], writes=[mk.cols],
         stream="V", tag="l1hm0")
    wsl = mk.take(1)
    g6 = mk.take(6)
    tmp = mk.take(1)
    hred = kb.sb(kb.uid("hred"), (128, 1))
    for n in range(4):
        nsl_ = slice(n * 512, (n + 1) * 512)
        for it in range(NI):
            half = (it % 2) * 1024
            slot = (g6[it // 4], (slice(None), slice((it % 4) * 512, (it % 4) * 512 + 512)))
            for wi, (iw_dr, scs, isfirst) in enumerate(
                    [(iwg_d, gs2, True), (iwu_d, us2, False)]):
                off = ((it % 2) * 2 + wi) * 1024
                kb.dma_ap(wsl, wsl.ap0.bitcast(BF16)[(slice(None), slice(off, off + 1024))]
                          .rearrange("p (k m) -> p k m", m=128),
                          iw_dr, iw_dr.ap0[:, it * 128:(it + 1) * 128].rearrange(
                              "(k p) m -> p k m", p=128), tag="l1iwld")
                p = mk.pb()
                for k in range(ND):
                    kb.op("T", lambda e, o=p[(slice(None), slice(0, 512))],
                          l=wsl.bf((slice(None), slice(off + k * 128, off + (k + 1) * 128))),
                          r=bxq[k // 2].bf((slice(None), slice((k % 2) * S + n * 512,
                                                               (k % 2) * S + (n + 1) * 512))),
                          st=(k == 0), sp=(k == ND - 1):
                          e.matmul(o, l, r, start=st, stop=sp),
                          reads=[wsl, bxq[k // 2]], writes=[p], tag="l1gumm")
                if isfirst:
                    mk.act_col(slot[0], slot[1], p, (slice(None), slice(0, 512)),
                               Act.Silu, scale_cs=scs + it, tag="l1gev")
                else:
                    mk.act_col(tmp, _f(0, 512), p, (slice(None), slice(0, 512)),
                               Act.Identity, scale_cs=scs + it, tag="l1uev")
            kb.v(lambda e, o=slot[0][slot[1]], b=tmp[_f(0, 512)]: e.tensor_mul(o, o, b),
                 reads=[tmp, slot[0]], writes=[slot[0]], stream="V", tag="l1h")
            kb.v(lambda e, o=hred.full, i=slot[0][slot[1]]:
                 e.tensor_reduce(o, i, axis=XAX, op=Alu.max, apply_absolute_value=True),
                 reads=[slot[0]], writes=[hred], stream="VR", tag="l1hmr")
            kb.v(lambda e: e.tensor_tensor(mk.cols[CS(hmx_cs)], mk.cols[CS(hmx_cs)],
                                           hred.full, Alu.max),
                 reads=[hred, mk.cols], writes=[mk.cols], stream="V", tag="l1hmx")
            kb.dma(hmd, (slice(it * 128, (it + 1) * 128), nsl_), slot[0], slot[1],
                   tag="l1h_st")
    mk.rel(bxq, wsl, g6, tmp)

    hrow = mk.colmax_to_scalar(hmx_cs, None, tag="l1cms2")
    hx_cs = mk.colslot(1)
    mk.allreduce_max(hrow, 128, hx_cs, tag="l1cc2")
    s2_cs = mk.colslot(1)
    si2_cs = mk.colslot(1)
    kb.dma_ap(mk.cols, mk.cols.ap0[CS(s2_cs)],
              clip_d, clip_d.ap0[(slice(0, 1), slice(0, 1))].to_broadcast([128, 1]),
              stream="SD", tag="l1clipd")
    mk.act_col(mk.cols, CS(s2_cs), mk.cols, CS(s2_cs), Act.Sigmoid, tag="l1sig2")
    kb.v(lambda e: e.tensor_mul(mk.cols[CS(s2_cs)], mk.cols[CS(s2_cs)], mk.cols[CS(hx_cs)]),
         reads=[mk.cols], writes=[mk.cols], stream="V", tag="l1s21")
    mk.ts(mk.cols, CS(s2_cs), mk.cols, CS(s2_cs), 1.0 / 127.0, Alu.mult, stream="V", tag="l1s22")
    kb.v(lambda e: e.reciprocal(mk.cols[CS(si2_cs)], mk.cols[CS(s2_cs)]),
         reads=[mk.cols], writes=[mk.cols], stream="VR", tag="l1si2")
    ds2 = mk.colslot(8)
    kb.v(lambda e: e.tensor_single_scalar(mk.cols[CS(ds2, 8)], mk.cols[CS(d_sc, 8)],
                                          mk.cols[CS(s2_cs)], Alu.mult),
         reads=[mk.cols], writes=[mk.cols], stream="V", tag="l1ds2")

    mk.tap("l1hmd", hmd)
    mk.tap("l1ffncols", mk.cols)
    qt = mk.take(2)
    scr = mk.take(1)
    for it in range(NI):
        b0, b1 = qt[0], qt[1]
        kb.dma(b0, FS, hmd, _p(it * 128, (it + 1) * 128), tag="l1ih_ld")
        mk.fq(b1, FS, b0, FS, si2_cs, scr, FS, -128.0, 127.0, tag="l1ih")
        kb.v(lambda e, o=b0.bf((slice(None), slice(0, S))), i=b1.full: e.tensor_copy(o, i),
             reads=[b1], writes=[b0], stream="V", tag="l1ihc")
        kb.dma_ap(ihd, ihd.ap0[(slice(it * 128, (it + 1) * 128), slice(None))],
                  b0, b0.bf((slice(None), slice(0, S))), tag="l1ih_st")
    mk.rel(qt, scr)

    mk.tap("l1ihd", ihd)
    ih6 = mk.take(3)
    zt = mk.take(2)
    for n in range(4):
        nsl_ = slice(n * 512, (n + 1) * 512)
        for it in range(NI):
            kb.dma_ap(ih6[it // 8],
                      ih6[it // 8].bf((slice(None), slice((it % 8) * 512, (it % 8) * 512 + 512))),
                      ihd, ihd.ap0[(slice(it * 128, (it + 1) * 128), nsl_)], tag="l1dh_ld")
        for o in range(8):
            half = (o % 2) * 4096
            kb.dma_ap(mk.slab[0],
                      mk.slab[0].ap0.bitcast(BF16)[(slice(None), slice(half, half + NI * 128))]
                      .rearrange("p (k m) -> p k m", m=128),
                      iwd_d, iwd_d.ap0[:, o * 128:(o + 1) * 128].rearrange("(k p) m -> p k m", p=128),
                      tag="l1dwld")
            p = mk.pb()
            for it in range(NI):
                kb.op("T", lambda e, o_=p[(slice(None), slice(0, 512))],
                      l=mk.slab[0].bf((slice(None), slice(half + it * 128, half + (it + 1) * 128))),
                      r=ih6[it // 8].bf((slice(None), slice((it % 8) * 512, (it % 8) * 512 + 512))),
                      st=(it == 0), sp=(it == NI - 1):
                      e.matmul(o_, l, r, start=st, stop=sp),
                      reads=[mk.slab[0], ih6[it // 8]], writes=[p], tag="l1dmm")
            mk.act_col(zt[o // 4], _f((o % 4) * 512, (o % 4) * 512 + 512),
                       p, (slice(None), slice(0, 512)), Act.Identity,
                       scale_cs=ds2 + o, tag="l1zev")
        for o in range(8):
            kb.dma_acc(mk.attd, (slice(o * 128, (o + 1) * 128), nsl_),
                       zt[o // 4], _f((o % 4) * 512, (o % 4) * 512 + 512), tag="l1zacc")
    mk.rel(ih6, zt)

    xnew = mk.take(8)
    ta = mk.take(1)
    for m in range(ND):
        msl = _p(m * 128, (m + 1) * 128)
        kb.dma(xnew[m], FS, mk.xd, msl, tag="l1cx")
        kb.dma(ta, FS, mk.attd, msl, tag="l1ca")
        mk.ew2(Alu.add, xnew[m], FS, xnew[m], FS, ta, FS, stream="V", tag="l1cs")
    mk.rel(ta)
    return xnew


def emit_layer2(mk: MK, x8):
    kb = mk.kb
    mk._col_n = 0
    CS = lambda a, n=1: (slice(None), slice(a, a + n))
    WqdN = kb.dram("L2_WqdN", (D, D), kind="ExternalInput")
    WkdN = kb.dram("L2_WkdN", (D, D), kind="ExternalInput")
    WvdN = kb.dram("L2_WvdN", (D, D), kind="ExternalInput")
    upqc = kb.dram("L2_upqc", (64, 32), kind="ExternalInput")
    upqe = kb.dram("L2_upqe", (64, 32), kind="ExternalInput")
    upkc = kb.dram("L2_upkc", (64, 32), kind="ExternalInput")
    upke = kb.dram("L2_upke", (64, 32), kind="ExternalInput")
    upv = kb.dram("L2_upv", (64, 64), kind="ExternalInput")
    upqeT = kb.dram("L2_upqeT", (32, 64), kind="ExternalInput")
    upkeT = kb.dram("L2_upkeT", (32, 64), kind="ExternalInput")
    WcN = kb.dram("L2_WcN", (D, D), kind="ExternalInput")
    gwT = kb.dram("L2_gwT", (D, 8), kind="ExternalInput")
    bbal = kb.dram("L2_bb", (8, 1), kind="ExternalInput")
    w1T = kb.dram("L2_w1T", (8 * D, 2048), kind="ExternalInput")
    w2T = kb.dram("L2_w2T", (8 * 2048, D), kind="ExternalInput")
    qlatd = kb.dram("L2_qlat", (D, S))
    klatd = kb.dram("L2_klat", (D, S))
    vlatd = kb.dram("L2_vlat", (D, S))
    aod = kb.dram("L2_aod", (D, S))
    combd = kb.dram("L2_combd", (8, S))
    inv = 1.0 / 10000.0 ** (np.arange(0, 32, 2, dtype=np.float64) / 32.0)
    fr = np.arange(S, dtype=np.float64)[None, :] * inv[:, None]
    cosT = (np.tile(np.cos(fr), (2, 1)) * (8.0 ** -0.5)).astype(np.float32)
    sinT = (np.tile(np.sin(fr), (2, 1)) * (8.0 ** -0.5)).astype(np.float32)
    pm = np.zeros((32, 32), np.float32)
    for i in range(16):
        pm[2 * i + 1, 2 * i] = -1.0
        pm[2 * i, 2 * i + 1] = 1.0
    cos_s = kb.sb("l2cos", (32, S))
    sin_s = kb.sb("l2sin", (32, S))
    kb.dma(cos_s, None, kb.const("c_l2cos", cosT), FS, tag="l2tc")
    kb.dma(sin_s, None, kb.const("c_l2sin", sinT), FS, tag="l2ts")
    pm_s = kb.sb("l2pm", (32, 32))
    kb.dma(pm_s, None, kb.const("c_l2pm", pm), FS, tag="l2tp")
    up_s = {}
    for nm, dr, shp in [("qc", upqc, (64, 32)), ("qe", upqe, (64, 32)),
                        ("kc", upkc, (64, 32)), ("ke", upke, (64, 32)),
                        ("v", upv, (64, 64)), ("qeT", upqeT, (32, 64)),
                        ("keT", upkeT, (32, 64))]:
        t = kb.sb("l2up_" + nm, shp)
        kb.dma(t, None, dr, FS, tag="l2up")
        up_s[nm] = t
    for nm, src in [("qpe", "qeT"), ("kpe", "keT")]:
        t = kb.sb("l2up_" + nm, (64, 32))
        p = mk.pb()
        kb.mm(p, (slice(0, 64), slice(0, 32)), up_s[src], FS, pm_s, FS,
              start=True, stop=True, tag="l2pe")
        kb.v(lambda e, o=t.full, i=p[(slice(0, 64), slice(0, 32))]: e.tensor_copy(o, i),
             reads=[p], writes=[t], stream="V", tag="l2pee")
        up_s[nm] = t

    for k in range(ND):
        kb.dma(mk.xd, _p(k * 128, (k + 1) * 128), x8[k], FS, tag="l2x_st")
    xn8 = x8
    mk.ln(x8, xn8, 1e-5, "l2ln")
    for k in range(ND):
        kb.dma(mk.xnd, _p(k * 128, (k + 1) * 128), xn8[k], FS, tag="l2xn_st")

    bb_cs = mk.colslot(1)
    kb.dma(mk.cols, (slice(0, 8), slice(bb_cs, bb_cs + 1)), bbal, (slice(0, 8), slice(0, 1)),
           tag="l2bb")
    gw_s = kb.sb("l2gw", (128, 64))
    for k in range(ND):
        kb.dma(gw_s, (slice(None), slice(k * 8, (k + 1) * 8)), gwT,
               _p(k * 128, (k + 1) * 128), tag="l2gw")
    gfm = mk.take(1)
    for n in range(4):
        p = mk.pb()
        for k in range(ND):
            kb.mm(p, (slice(0, 8), slice(0, 512)), gw_s, _f(k * 8, (k + 1) * 8),
                  xn8[k], _f(n * 512, (n + 1) * 512), start=(k == 0), stop=(k == ND - 1),
                  tag="l2gmm")
        kb.op("SC", lambda e, o=gfm[(slice(0, 8), slice(n * 512, (n + 1) * 512))],
              i=p[(slice(0, 8), slice(0, 512))],
              b=mk.cols[(slice(0, 8), slice(bb_cs, bb_cs + 1))]:
              e.activation(o, i, Act.Sigmoid, bias=b),
              reads=[p, mk.cols], writes=[gfm], tag="l2sig")
    gsc = mk.colslot(8)
    m1s = mk.colslot(1); eqs = mk.colslot(8); cnts = mk.colslot(1)
    msks = mk.colslot(8); m2as = mk.colslot(1); inds = mk.colslot(1)
    t2s = mk.colslot(1); m2s = mk.colslot(1); gms = mk.colslot(8)
    dens = mk.colslot(1); rdens = mk.colslot(1); cbs = mk.colslot(8)
    fms = mk.colslot(128)
    for t in range(16):
        tsl = slice(t * 128, (t + 1) * 128)
        p = mk.pb()
        kb.op("T", lambda e, o=p[(slice(None), slice(0, 8))], i=gfm[(slice(0, 8), tsl)]:
              e.transpose(o, i, mk.ident[(slice(0, 8), slice(0, 8))]),
              reads=[gfm, mk.ident], writes=[p], tag="l2gt")
        kb.v(lambda e, o=mk.cols[CS(gsc, 8)], i=p[(slice(None), slice(0, 8))]:
             e.tensor_copy(o, i), reads=[p], writes=[mk.cols], stream="V", tag="l2gte")
        sc_ = CS(gsc, 8)
        kb.v(lambda e: e.reduce_max(mk.cols[CS(m1s)], mk.cols[sc_], axis=XAX),
             reads=[mk.cols], writes=[mk.cols], stream="VR", tag="l2m1")
        mk.ts_col(mk.cols, CS(eqs, 8), mk.cols, sc_, mk.cols, m1s, Alu.is_equal, tag="l2eq")
        kb.v(lambda e: e.tensor_reduce(mk.cols[CS(cnts)], mk.cols[CS(eqs, 8)], axis=XAX,
                                       op=Alu.add), reads=[mk.cols], writes=[mk.cols],
             stream="VR", tag="l2cnt")
        mk.stt(mk.cols, CS(msks, 8), mk.cols, CS(eqs, 8), -1e30, mk.cols, sc_,
               Alu.mult, Alu.add, tag="l2msk")
        kb.v(lambda e: e.reduce_max(mk.cols[CS(m2as)], mk.cols[CS(msks, 8)], axis=XAX),
             reads=[mk.cols], writes=[mk.cols], stream="VR", tag="l2m2a")
        mk.ts(mk.cols, CS(inds), mk.cols, CS(cnts), 2.0, Alu.is_ge, tag="l2ind")
        mk.ew2(Alu.mult, mk.cols, CS(t2s), mk.cols, CS(inds), mk.cols, CS(m1s),
               stream="V", tag="l2t2")
        mk.ew2(Alu.max, mk.cols, CS(m2s), mk.cols, CS(m2as), mk.cols, CS(t2s),
               stream="V", tag="l2m2")
        mk.ts_col(mk.cols, CS(gms, 8), mk.cols, sc_, mk.cols, m2s, Alu.is_ge, tag="l2gm")
        mk.ew2(Alu.add, mk.cols, CS(dens), mk.cols, CS(m1s), mk.cols, CS(m2s),
               stream="V", tag="l2dn")
        mk.ts(mk.cols, CS(dens), mk.cols, CS(dens), 1e-6, Alu.add, tag="l2dn2")
        kb.v(lambda e: e.reciprocal(mk.cols[CS(rdens)], mk.cols[CS(dens)]),
             reads=[mk.cols], writes=[mk.cols], stream="VR", tag="l2rd")
        mk.ew2(Alu.mult, mk.cols, CS(cbs, 8), mk.cols, sc_, mk.cols, CS(gms, 8),
               stream="V", tag="l2cb")
        mk.ts_col(mk.cols, CS(cbs, 8), mk.cols, CS(cbs, 8), mk.cols, rdens, Alu.mult,
                  tag="l2cb2")
        p = mk.pb()
        kb.op("T", lambda e, o=p[(slice(0, 8), slice(0, 128))]:
              e.transpose(o, mk.cols[CS(cbs, 8)], mk.ident.full),
              reads=[mk.cols, mk.ident], writes=[p], tag="l2ct")
        kb.v(lambda e, o=mk.cols[(slice(0, 8), slice(fms, fms + 128))],
             i=p[(slice(0, 8), slice(0, 128))]: e.tensor_copy(o, i),
             reads=[p], writes=[mk.cols], stream="V", tag="l2cfe")
        kb.dma(combd, (slice(0, 8), tsl), mk.cols, (slice(0, 8), slice(fms, fms + 128)),
               tag="l2cst")
    mk.rel(gfm)

    wsl = mk.take(1)
    am2 = mk.take(2)
    for wN, outd, tg in [(WqdN, qlatd, "l2q"), (WkdN, klatd, "l2k"), (WvdN, vlatd, "l2v")]:
        for m in range(ND):
            half = (m % 2) * 1024
            kb.dma_ap(wsl, wsl.ap0[(slice(None), slice(half, half + 1024))].rearrange(
                "p (k m) -> p k m", m=128),
                wN, wN.ap0[:, m * 128:(m + 1) * 128].rearrange("(k p) m -> p k m", p=128),
                tag=tg + "w")
            at = am2[m % 2]
            for n in range(4):
                p = mk.pb()
                for k in range(ND):
                    kb.mm(p, (slice(None), slice(0, 512)),
                          wsl, _f(half + k * 128, half + (k + 1) * 128),
                          xn8[k], _f(n * 512, (n + 1) * 512),
                          start=(k == 0), stop=(k == ND - 1), tag=tg)
                mk.evac(at, _f(n * 512, (n + 1) * 512), p, (slice(None), slice(0, 512)),
                        tag=tg + "e")
            kb.dma(outd, _p(m * 128, (m + 1) * 128), at, FS, tag=tg + "st")
    mk.rel(wsl, am2)

    z6 = mk.take(6)
    hb = mk.take(1)
    wrot = mk.take(2)

    def zsl(mo, n):
        if mo < 6:
            return z6[mo], _f(n * 512, (n + 1) * 512)
        return mk.slab[0], _f((mo - 6) * 2048 + n * 512, (mo - 6) * 2048 + (n + 1) * 512)

    for e in range(8):
        kb.dma_ap(mk.slab[1], mk.slab[1].ap0[(slice(None), slice(0, S))],
                  combd, combd.ap0[(slice(e, e + 1), slice(None))].to_broadcast([128, S]),
                  tag="l2cbld")
        for mi in range(16):
            w = wrot[(e * 16 + mi) % 2]
            kb.dma_ap(w, w.ap0[(slice(None), slice(0, 1024))].rearrange(
                "p (k m) -> p k m", m=128),
                w1T, w1T.ap0[e * D:(e + 1) * D, mi * 128:(mi + 1) * 128].rearrange(
                    "(k p) m -> p k m", p=128), tag="l2w1")
            kb.dma(w, _f(1024, 2048), w2T,
                   _p(e * 2048 + mi * 128, e * 2048 + (mi + 1) * 128), tag="l2w2")
            for n in range(4):
                p = mk.pb()
                for k in range(ND):
                    kb.mm(p, (slice(None), slice(0, 512)),
                          w, _f(k * 128, (k + 1) * 128),
                          xn8[k], _f(n * 512, (n + 1) * 512),
                          start=(k == 0), stop=(k == ND - 1), tag="l2f1")
                mk.act(hb, _f(n * 512, (n + 1) * 512), p, (slice(None), slice(0, 512)),
                       Act.Silu, tag="l2silu")
                kb.v(lambda e_, o=hb[_f(n * 512, (n + 1) * 512)],
                     b=mk.slab[1][_f(n * 512, (n + 1) * 512)]: e_.tensor_mul(o, o, b),
                     reads=[mk.slab[1], hb], writes=[hb], stream="V", tag="l2hcb")
            for mo in range(8):
                for n in range(4):
                    p = mk.pb()
                    kb.mm(p, (slice(None), slice(0, 512)),
                          w, _f(1024 + mo * 128, 1024 + (mo + 1) * 128),
                          hb, _f(n * 512, (n + 1) * 512),
                          start=True, stop=True, tag="l2f2")
                    zt, zs = zsl(mo, n)
                    if e == 0 and mi == 0:
                        kb.v(lambda e_, o=zt[zs], i=p[(slice(None), slice(0, 512))]:
                             e_.tensor_copy(o, i), reads=[p], writes=[zt], stream="V",
                             tag="l2z0")
                    else:
                        kb.v(lambda e_, o=zt[zs], i=p[(slice(None), slice(0, 512))]:
                             e_.tensor_add(o, o, i), reads=[p, zt], writes=[zt], stream="V",
                             tag="l2za")
    for mo in range(8):
        zt, _ = zsl(mo, 0)
        src = _f(0, S) if mo < 6 else _f((mo - 6) * 2048, (mo - 6) * 2048 + S)
        kb.dma(mk.attd, _p(mo * 128, (mo + 1) * 128), zt, src, tag="l2zst")
    mk.rel(z6, hb, wrot, xn8)

    mtile = mk.take(1)
    ms = mk.take(1)
    kb.v(lambda e, o=ms[_f(0, 512)]: e.memset(o, 1.0), reads=[], writes=[ms],
         stream="V", tag="l2m1s")
    for j in range(4):
        kb.op("G", lambda e, o=mtile[_f(j * 512, (j + 1) * 512)], i=ms[_f(0, 512)],
              b=-128 * j:
              e.affine_select(o, i, pattern=[[1, 512]], compare_op=Alu.is_ge,
                              fill=0.0, base=b, channel_multiplier=-1),
              reads=[ms], writes=[mtile], tag="l2mk")
    mk.rel(ms)
    vtok = mk.take(1)
    for c in range(16):
        kb.v(lambda e, o=vtok[_f(c * 65 + 64, c * 65 + 65)]: e.memset(o, 1.0),
             reads=[], writes=[vtok], stream="VR", tag="l2vo")
    et_ab = mk.take(2)
    et2 = mk.take(1)
    ao = mk.take(1)
    ql, kl, vl = mk.take(3)
    qc, qe, qe2, kc, ke, ke2 = mk.take(6)
    vh = mk.take(1)
    P32 = (slice(0, 32), slice(None))
    for h in range(16):
        for lt, src in [(ql, qlatd), (kl, klatd), (vl, vlatd)]:
            kb.dma(lt, (slice(0, 64), slice(None)), src, _p(h * 64, (h + 1) * 64),
                   tag="l2lat")
        for dst, ups, srct, scaled in [
                (qc, "qc", ql, True), (qe, "qe", ql, False), (qe2, "qpe", ql, False),
                (kc, "kc", kl, True), (ke, "ke", kl, False), (ke2, "kpe", kl, False)]:
            for n in range(4):
                p = mk.psum[n % 4]
                kb.mm(p, (slice(0, 32), slice(0, 512)), up_s[ups], None,
                      srct, (slice(0, 64), slice(n * 512, (n + 1) * 512)),
                      start=True, stop=True, tag="l2up")
                osl = (slice(0, 32), slice(n * 512, (n + 1) * 512))
                if scaled:
                    kb.op("SC", lambda e, o=dst[osl], i=p[(slice(0, 32), slice(0, 512))],
                          s=mk.cc(0.35355339059327373, 32):
                          e.activation(o, i, Act.Identity, scale=s),
                          reads=[p, mk.ccols], writes=[dst], tag="l2upc")
                else:
                    kb.v(lambda e, o=dst[osl], i=p[(slice(0, 32), slice(0, 512))]:
                         e.tensor_copy(o, i), reads=[p], writes=[dst], stream="V",
                         tag="l2upe")
        for n in range(4):
            p = mk.psum[n % 4]
            kb.mm(p, (slice(0, 64), slice(0, 512)), up_s["v"], None,
                  vl, (slice(0, 64), slice(n * 512, (n + 1) * 512)),
                  start=True, stop=True, tag="l2vmm")
            mk.evac(vh, (slice(0, 64), slice(n * 512, (n + 1) * 512)),
                    p, (slice(0, 64), slice(0, 512)), tag="l2ve")
        for a, b, s1, s2 in [(qe, qe2, "V", "G"), (ke, ke2, "G", "V")]:
            mk.ew2(Alu.mult, a, P32, a, P32, cos_s, FS, stream=s1, tag="l2r1")
            mk.ew2(Alu.mult, b, P32, b, P32, sin_s, FS, stream=s2, tag="l2r2")
            mk.ew2(Alu.add, a, P32, a, P32, b, P32, stream=s1, tag="l2r3")
        for c in range(16):
            p = mk.psum[4 + c % 2]
            kb.op("T", lambda e, o=p[(slice(None), slice(0, 64))],
                  i=vh[(slice(0, 64), slice(c * 128, (c + 1) * 128))]:
                  e.transpose(o, i, mk.ident[(slice(0, 64), slice(0, 64))]),
                  reads=[vh, mk.ident], writes=[p], tag="l2vt")
            kb.v(lambda e, o=vtok[_f(c * 65, c * 65 + 64)],
                 i=p[(slice(None), slice(0, 64))]: e.tensor_copy(o, i),
                 reads=[p], writes=[vtok], stream="V", tag="l2vte")
        for Qn in range(4):
            pav = mk.psum[6 + Qn % 2]
            nlast = 4 * Qn + 3
            for cb in range(4 * Qn + 4):
                pet = mk.psum[4 + cb % 2]
                kb.mm(pet, (slice(None), slice(0, 512)),
                      kc, (slice(0, 32), slice(cb * 128, (cb + 1) * 128)),
                      qc, (slice(0, 32), slice(Qn * 512, (Qn + 1) * 512)),
                      start=True, stop=False, tag="l2et1")
                kb.mm(pet, (slice(None), slice(0, 512)),
                      ke, (slice(0, 32), slice(cb * 128, (cb + 1) * 128)),
                      qe, (slice(0, 32), slice(Qn * 512, (Qn + 1) * 512)),
                      start=False, stop=True, tag="l2et2")
                et = et_ab[cb % 2]
                kb.op("SC", lambda e, o=et[_f(0, 512)], i=pet[(slice(None), slice(0, 512))]:
                      e.activation(o, i, Act.Exp), reads=[pet], writes=[et], tag="l2ex")
                if cb >= 4 * Qn:
                    j = cb - 4 * Qn
                    kb.v(lambda e, o=et[_f(0, 512)], m=mtile[_f(j * 512, (j + 1) * 512)]:
                         e.tensor_mul(o, o, m), reads=[mtile, et], writes=[et],
                         stream="V", tag="l2dgm")
                kb.mm(pav, (slice(0, 65), slice(0, 512)),
                      vtok, _f(cb * 65, (cb + 1) * 65), et, _f(0, 512),
                      start=(cb == 0), stop=(cb == nlast), tag="l2av")
            kb.v(lambda e, o=et2[(slice(64, 65), slice(0, 512))],
                 i=pav[(slice(64, 65), slice(0, 512))]: e.tensor_copy(o, i),
                 reads=[pav], writes=[et2], stream="V", tag="l2sm")
            kb.v(lambda e, o=et2[(slice(64, 65), slice(0, 512))]: e.reciprocal(o, o),
                 reads=[et2], writes=[et2], stream="VR", tag="l2rc")
            pbc = mk.psum[3]
            kb.mm(pbc, (slice(0, 64), slice(0, 512)), mk.ones65,
                  (slice(64, 65), slice(0, 64)),
                  et2, (slice(64, 65), slice(0, 512)), start=True, stop=True, tag="l2bc")
            kb.v(lambda e, o=ao[(slice(0, 64), slice(0, 512))],
                 i=pav[(slice(0, 64), slice(0, 512))]: e.tensor_copy(o, i),
                 reads=[pav], writes=[ao], stream="V", tag="l2ae")
            kb.v(lambda e, o=ao[(slice(0, 64), slice(0, 512))],
                 i=pbc[(slice(0, 64), slice(0, 512))]: e.tensor_mul(o, o, i),
                 reads=[pbc, ao], writes=[ao], stream="V", tag="l2an")
            kb.dma(aod, (slice(h * 64, (h + 1) * 64), slice(Qn * 512, (Qn + 1) * 512)),
                   ao, (slice(0, 64), slice(0, 512)), tag="l2ast")
    mk.rel(mtile, vtok, et_ab, et2, ao, ql, kl, vl, qc, qe, qe2, kc, ke, ke2, vh)
    mk._ps = 0

    aoch = mk.take(2)
    wsl = mk.take(1)
    zt = mk.take(2)
    for n in range(4):
        for k in range(ND):
            kb.dma(aoch[k // 4], _f((k % 4) * 512, (k % 4) * 512 + 512),
                   aod, (slice(k * 128, (k + 1) * 128), slice(n * 512, (n + 1) * 512)),
                   tag="l2aold")
        for m in range(ND):
            half = (m % 2) * 1024
            kb.dma_ap(wsl, wsl.ap0[(slice(None), slice(half, half + 1024))].rearrange(
                "p (k m) -> p k m", m=128),
                WcN, WcN.ap0[:, m * 128:(m + 1) * 128].rearrange("(k p) m -> p k m", p=128),
                tag="l2wc")
            p = mk.pb()
            for k in range(ND):
                kb.mm(p, (slice(None), slice(0, 512)),
                      wsl, _f(half + k * 128, half + (k + 1) * 128),
                      aoch[k // 4], _f((k % 4) * 512, (k % 4) * 512 + 512),
                      start=(k == 0), stop=(k == ND - 1), tag="l2cm")
            mk.evac(zt[m % 2], _f(0, 512), p, (slice(None), slice(0, 512)), tag="l2cme")
            kb.dma_acc(mk.attd, (slice(m * 128, (m + 1) * 128), slice(n * 512, (n + 1) * 512)),
                       zt[m % 2], _f(0, 512), tag="l2cacc")
    mk.rel(aoch, wsl, zt)

    xnew = mk.take(8)
    ta = mk.take(1)
    for m in range(ND):
        msl = _p(m * 128, (m + 1) * 128)
        kb.dma(xnew[m], FS, mk.xd, msl, tag="l2cx")
        kb.dma(ta, FS, mk.attd, msl, tag="l2ca")
        mk.ew2(Alu.add, xnew[m], FS, xnew[m], FS, ta, FS, stream="V", tag="l2cs")
    mk.rel(ta)
    return xnew


def emit_layer3(mk: MK, x8):
    kb = mk.kb
    mk._col_n = 0
    CS = lambda a, n=1: (slice(None), slice(a, a + n))
    WinN = kb.dram("L3_WinN", (D, 4096), kind="ExternalInput")
    cwd = [kb.dram(f"L3_cw{kk}", (2048, 1), kind="ExternalInput") for kk in range(4)]
    cbd = kb.dram("L3_cb", (2048, 1), kind="ExternalInput")
    WxN = kb.dram("L3_WxN", (2048, 96), kind="ExternalInput")
    wdtN = kb.dram("L3_wdt", (64, 2048), kind="ExternalInput")
    bdtd = kb.dram("L3_bdt", (2048, 1), kind="ExternalInput")
    alogd = kb.dram("L3_alog", (2048, 16), kind="ExternalInput")
    Dpd = kb.dram("L3_D", (2048, 1), kind="ExternalInput")
    WoutN = kb.dram("L3_WoutN", (2048, D), kind="ExternalInput")
    wqT = kb.dram("L3_wqT", (D, D), I8, kind="ExternalInput")
    wsd = kb.dram("L3_ws", (D, 1), kind="ExternalInput")
    fbd = kb.dram("L3_fb", (D, 1), kind="ExternalInput")
    xzd = kb.dram("L3_xzd", (4096, S))
    xcd = kb.dram("L3_xcd", (2048, S))
    deltad = kb.dram("L3_deltad", (2048, S))
    yzd = kb.dram("L3_yzd", (2048, S))
    bcd = kb.dram("L3_bcd", (32, S))
    acols = kb.sb("l3acol", (128, 256))
    acols.tiny = True
    hcar = kb.sb("l3hcar", (128, 256))
    hcar.tiny = True

    for k in range(ND):
        kb.dma(mk.xd, _p(k * 128, (k + 1) * 128), x8[k], FS, tag="l3x_st")
    xn8 = x8
    mk.ln(x8, xn8, 1e-5, "l3ln")
    for k in range(ND):
        kb.dma(mk.xnd, _p(k * 128, (k + 1) * 128), xn8[k], FS, tag="l3xn_st")

    ws_cs = mk.colslot(8)
    mk.load_col(wsd, 0, ws_cs, 8, tag="l3ws")
    fb_cs = mk.colslot(8)
    mk.load_col(fbd, 0, fb_cs, 8, tag="l3fb")
    am_cs = mk.colslot(16)
    xs_cs = mk.colslot(16)
    si_cs = mk.colslot(16)
    mabs = mk.take(1)
    tt = mk.take(1)
    for k in range(ND):
        if k == 0:
            mk.stt(mabs, FS, xn8[0], FS, -1.0, xn8[0], FS, Alu.mult, Alu.max,
                   stream="V", tag="l3ab")
        else:
            mk.stt(tt, FS, xn8[k], FS, -1.0, xn8[k], FS, Alu.mult, Alu.max,
                   stream="V", tag="l3ab")
            mk.ew2(Alu.max, mabs, FS, mabs, FS, tt, FS, stream="V", tag="l3am")
    for t in range(16):
        p = mk.pb()
        kb.op("T", lambda e, o=p[(slice(None), slice(0, 128))],
              i=mabs[_f(t * 128, (t + 1) * 128)]: e.transpose(o, i, mk.ident.full),
              reads=[mabs, mk.ident], writes=[p], tag="l3tt")
        kb.v(lambda e, o=mk.cols[CS(am_cs + t)], i=p[(slice(None), slice(0, 128))]:
             e.reduce_max(o, i, axis=XAX), reads=[p], writes=[mk.cols], stream="VR",
             tag="l3amr")
    mk.ts(mk.cols, CS(xs_cs, 16), mk.cols, CS(am_cs, 16), 1e-6, Alu.max,
          1.0 / 127.0, Alu.mult, tag="l3xs")
    kb.v(lambda e: e.reciprocal(mk.cols[CS(si_cs, 16)], mk.cols[CS(xs_cs, 16)]),
         reads=[mk.cols], writes=[mk.cols], stream="VR", tag="l3si")
    xsb = mk.take(1)
    sib = mk.take(1)
    mk.cols_to_row(xs_cs, 16, 0, tag="l3c2r1")
    mk.bcast_row(0, S, xsb, 0, tag="l3bx")
    mk.cols_to_row(si_cs, 16, 0, tag="l3c2r2")
    mk.bcast_row(0, S, sib, 0, tag="l3bs")
    scr = mk.take(1)
    bxq = mk.take(4)
    for k in range(ND):
        mk.ew2(Alu.mult, tt, FS, xn8[k], FS, sib, FS, stream="V", tag="l3u")
        mk.fq(xn8[k], FS, tt, FS, None, scr, FS, -128.0, 127.0, tag="l3fq")
        kb.v(lambda e, o=bxq[k // 2].bf((slice(None), slice((k % 2) * S, (k % 2) * S + S))),
             i=xn8[k].full: e.tensor_copy(o, i),
             reads=[xn8[k]], writes=[bxq[k // 2]], stream="V", tag="l3fqc")
    mk.rel(tt, sib, mabs, scr, xn8)
    wr = mk.take(2)
    wrb = mk.take(1)
    zt = mk.take(2)
    for m in range(ND):
        w = wr[m % 2]
        kb.dma_ap(w, w.ap0[(slice(None), slice(0, 1024))].rearrange(
            "p (k m) -> p k m", m=128),
            wqT, wqT.ap0[:, m * 128:(m + 1) * 128].rearrange("(k p) m -> p k m", p=128),
            stream="GD", tag="l3wq")
        woff = (m % 4) * 1024
        kb.v(lambda e, o=wrb.bf((slice(None), slice(woff, woff + 1024))),
             i=w[(slice(None), slice(0, 1024))]: e.tensor_copy(o, i),
             reads=[w], writes=[wrb], stream="V", tag="l3wqc")
        for n in range(4):
            p = mk.pb()
            for k in range(ND):
                kb.op("T", lambda e, o=p[(slice(None), slice(0, 512))],
                      l=wrb.bf((slice(None), slice(woff + k * 128, woff + (k + 1) * 128))),
                      r=bxq[k // 2].bf((slice(None), slice((k % 2) * S + n * 512,
                                                           (k % 2) * S + (n + 1) * 512))),
                      st=(k == 0), sp=(k == ND - 1):
                      e.matmul(o, l, r, start=st, stop=sp),
                      reads=[wrb, bxq[k // 2]], writes=[p], tag="l3qm")
            z = zt[n % 2]
            mk.act_col(z, _f(0, 512), p, (slice(None), slice(0, 512)), Act.Identity,
                       scale_cs=ws_cs + m, tag="l3ze")
            mk.ew2(Alu.mult, z, _f(0, 512), z, _f(0, 512), xsb,
                   _f(n * 512, (n + 1) * 512), stream="V", tag="l3zx")
            mk.ts_col(z, _f(0, 512), z, _f(0, 512), mk.cols, fb_cs + m, Alu.add,
                      stream="V", tag="l3zb")
            kb.dma(mk.attd, (slice(m * 128, (m + 1) * 128), slice(n * 512, (n + 1) * 512)),
                   z, _f(0, 512), tag="l3zst")
    mk.rel(wr, wrb, bxq, zt, xsb)

    xn8 = mk.take(8)
    for k in range(ND):
        kb.dma(xn8[k], FS, mk.xnd, _p(k * 128, (k + 1) * 128), tag="l3xnld")
    wsl = mk.take(1)
    am2 = mk.take(2)
    for m in range(32):
        half = (m % 2) * 1024
        kb.dma_ap(wsl, wsl.ap0[(slice(None), slice(half, half + 1024))].rearrange(
            "p (k m) -> p k m", m=128),
            WinN, WinN.ap0[:, m * 128:(m + 1) * 128].rearrange("(k p) m -> p k m", p=128),
            tag="l3wi")
        at = am2[m % 2]
        for n in range(4):
            p = mk.pb()
            for k in range(ND):
                kb.mm(p, (slice(None), slice(0, 512)),
                      wsl, _f(half + k * 128, half + (k + 1) * 128),
                      xn8[k], _f(n * 512, (n + 1) * 512),
                      start=(k == 0), stop=(k == ND - 1), tag="l3zm")
            mk.evac(at, _f(n * 512, (n + 1) * 512), p, (slice(None), slice(0, 512)),
                    tag="l3zme")
        kb.dma(xzd, _p(m * 128, (m + 1) * 128), at, FS, tag="l3zmst")
    mk.rel(xn8, wsl, am2)

    cw_cs = mk.colslot(64)
    for kk in range(4):
        mk.load_col(cwd[kk], 0, cw_cs + kk * 16, 16, tag="l3cw")
    cb_cs = mk.colslot(16)
    mk.load_col(cbd, 0, cb_cs, 16, tag="l3cbl")
    wx_s = mk.take(1)
    kb.dma_ap(wx_s, wx_s.ap0[(slice(None), slice(0, 1536))].rearrange(
        "p (k m) -> p k m", m=96),
        WxN, WxN.ap0[:, :].rearrange("(k p) m -> p k m", p=128), tag="l3wx")
    xdbl = mk.take(1)
    xi = mk.take(1)
    acc2 = mk.take(2)
    for g in range(16):
        kb.dma(xi, FS, xzd, _p(g * 128, (g + 1) * 128), tag="l3xild")
        a = acc2[g % 2]
        mk.ts_col(a, FS, xi, FS, mk.cols, cw_cs + 3 * 16 + g, Alu.mult, stream="V",
                  tag="l3cv0")
        for j, kk in [(1, 2), (2, 1), (3, 0)]:
            mk.stt_col(a, (slice(None), slice(j, S)), xi, (slice(None), slice(0, S - j)),
                       mk.cols, cw_cs + kk * 16 + g, a, (slice(None), slice(j, S)),
                       Alu.mult, Alu.add, stream="V", tag="l3cv")
        mk.act_col(xi, FS, a, FS, Act.Silu, bias_cs=cb_cs + g, tag="l3cs")
        kb.dma(xcd, _p(g * 128, (g + 1) * 128), xi, FS, tag="l3xcst")
        for n in range(4):
            kb.mm(mk.psum[n], (slice(0, 96), slice(0, 512)),
                  wx_s, _f(g * 96, (g + 1) * 96),
                  xi, _f(n * 512, (n + 1) * 512),
                  start=(g == 0), stop=(g == 15), tag="l3xd")
    for n in range(4):
        mk.evac(xdbl, (slice(0, 96), slice(n * 512, (n + 1) * 512)),
                mk.psum[n], (slice(0, 96), slice(0, 512)), tag="l3xde")
    kb.dma(bcd, None, xdbl, (slice(64, 96), slice(None)), tag="l3bcst")
    mk.rel(wx_s, xi, acc2)

    bdt_cs = mk.colslot(16)
    mk.load_col(bdtd, 0, bdt_cs, 16, tag="l3bdt")
    wdt_s = mk.take(1)
    kb.dma(wdt_s, (slice(0, 64), slice(None)), wdtN, FS, tag="l3wdt")
    t1 = mk.take(1)
    aa = mk.take(1)
    ee = mk.take(1)
    dl2 = mk.take(2)
    for m in range(16):
        out = dl2[m % 2]
        for n in range(4):
            p = mk.pb()
            kb.mm(p, (slice(None), slice(0, 512)),
                  wdt_s, (slice(0, 64), slice(m * 128, (m + 1) * 128)),
                  xdbl, (slice(0, 64), slice(n * 512, (n + 1) * 512)),
                  start=True, stop=True, tag="l3dm")
            mk.act_col(t1, _f(n * 512, (n + 1) * 512), p, (slice(None), slice(0, 512)),
                       Act.Identity, bias_cs=bdt_cs + m, tag="l3du")
        mk.stt(aa, FS, t1, FS, -1.0, t1, FS, Alu.mult, Alu.max, stream="V", tag="l3da")
        for n in range(4):
            mk.act(ee, _f(n * 512, (n + 1) * 512), aa, _f(n * 512, (n + 1) * 512),
                   Act.Exp, scale=-1.0, tag="l3de")
        for n in range(4):
            mk.act(aa, _f(n * 512, (n + 1) * 512), ee, _f(n * 512, (n + 1) * 512),
                   Act.Ln, bias=1.0, tag="l3dl")
        mk.ts(t1, FS, t1, FS, 0.0, Alu.max, stream="V", tag="l3dr")
        mk.ew2(Alu.add, out, FS, t1, FS, aa, FS, stream="V", tag="l3dd")
        kb.dma(deltad, _p(m * 128, (m + 1) * 128), out, FS, tag="l3dst")
    mk.rel(wdt_s, t1, aa, ee, dl2, xdbl)

    for g in range(16):
        kb.dma(acols, (slice(None), slice(g * 16, (g + 1) * 16)),
               alogd, _p(g * 128, (g + 1) * 128), tag="l3al")
    mk.act(acols, (slice(None), slice(0, 256)), acols, (slice(None), slice(0, 256)),
           Act.Exp, tag="l3ae")
    mk.ts(acols, (slice(None), slice(0, 256)), acols, (slice(None), slice(0, 256)),
          -1.0, Alu.mult, stream="V", tag="l3an")
    dp_cs = mk.colslot(16)
    mk.load_col(Dpd, 0, dp_cs, 16, tag="l3dp")
    kb.v(lambda e: e.memset(hcar.full, 0.0), reads=[], writes=[hcar], stream="V",
         tag="l3h0")

    bbt = mk.take(4)
    cbt = mk.take(4)
    dlA, dlB, xct, zgt, wg, yac, at_, ht, yp = mk.take(9)
    Q = _f(0, 512)
    for tc in range(4):
        csl = slice(tc * 512, (tc + 1) * 512)
        for n in range(16):
            kb.dma_ap(bbt[n // 4], bbt[n // 4].ap0[_f((n % 4) * 512, (n % 4) * 512 + 512)],
                      bcd, bcd.ap0[(slice(n, n + 1), csl)].to_broadcast([128, 512]),
                      tag="l3bb")
            kb.dma_ap(cbt[n // 4], cbt[n // 4].ap0[_f((n % 4) * 512, (n % 4) * 512 + 512)],
                      bcd, bcd.ap0[(slice(16 + n, 17 + n), csl)].to_broadcast([128, 512]),
                      tag="l3cb")
        for g in range(16):
            dl = dlA if g % 2 == 0 else dlB
            kb.dma(dl, Q, deltad, (slice(g * 128, (g + 1) * 128), csl), tag="l3dld")
            kb.dma(xct, Q, xcd, (slice(g * 128, (g + 1) * 128), csl), tag="l3xcld")
            kb.dma(zgt, Q, xzd, (slice(2048 + g * 128, 2048 + (g + 1) * 128), csl),
                   tag="l3zld")
            mk.ew2(Alu.mult, wg, Q, dl, Q, xct, Q, stream="V", tag="l3wg")
            mk.ts_col(yac, Q, xct, Q, mk.cols, dp_cs + g, Alu.mult, stream="V", tag="l3y0")
            for n in range(16):
                idx = g * 16 + n
                nsl = _f((n % 4) * 512, (n % 4) * 512 + 512)
                kb.op("SC", lambda e, o=at_[Q], i=dl[Q], s=acols[CS(idx)]:
                      e.activation(o, i, Act.Exp, scale=s),
                      reads=[dl, acols], writes=[at_], tag="l3sa")
                mk.ew2(Alu.mult, yp, Q, wg, Q, bbt[n // 4], nsl, stream="V", tag="l3sb")
                kb.v(lambda e, o=ht[Q], a=at_[Q], b=yp[Q], ini=hcar[CS(idx)]:
                     e.tensor_tensor_scan(o, a, b, ini, Alu.mult, Alu.add),
                     reads=[at_, yp, hcar], writes=[ht], stream="V", tag="l3sc")
                kb.v(lambda e, o=hcar[CS(idx)], i=ht[_f(511, 512)]: e.tensor_copy(o, i),
                     reads=[ht], writes=[hcar], stream="V", tag="l3hc")
                mk.ew2(Alu.mult, yp, Q, ht, Q, cbt[n // 4], nsl, stream="V", tag="l3sy")
                mk.ew2(Alu.add, yac, Q, yac, Q, yp, Q, stream="G", tag="l3sacc")
            mk.act(zgt, Q, zgt, Q, Act.Silu, tag="l3sz")
            mk.ew2(Alu.mult, yac, Q, yac, Q, zgt, Q, stream="V", tag="l3yz")
            kb.dma(yzd, (slice(g * 128, (g + 1) * 128), csl), yac, Q, tag="l3yst")
    mk.rel(bbt, cbt, dlA, dlB, xct, zgt, wg, yac, at_, ht, yp)

    yzc = mk.take(4)
    wsl = mk.take(1)
    zt = mk.take(2)
    for n in range(4):
        for ki in range(16):
            kb.dma(yzc[ki // 4], _f((ki % 4) * 512, (ki % 4) * 512 + 512),
                   yzd, (slice(ki * 128, (ki + 1) * 128), slice(n * 512, (n + 1) * 512)),
                   tag="l3yld")
        for m in range(ND):
            kb.dma_ap(wsl, wsl.ap0[FS].rearrange("p (k m) -> p k m", m=128),
                      WoutN, WoutN.ap0[:, m * 128:(m + 1) * 128].rearrange(
                          "(k p) m -> p k m", p=128), tag="l3wo")
            p = mk.pb()
            for ki in range(16):
                kb.mm(p, (slice(None), slice(0, 512)),
                      wsl, _f(ki * 128, (ki + 1) * 128),
                      yzc[ki // 4], _f((ki % 4) * 512, (ki % 4) * 512 + 512),
                      start=(ki == 0), stop=(ki == 15), tag="l3om")
            mk.evac(zt[m % 2], _f(0, 512), p, (slice(None), slice(0, 512)), tag="l3oe")
            kb.dma_acc(mk.attd, (slice(m * 128, (m + 1) * 128), slice(n * 512, (n + 1) * 512)),
                       zt[m % 2], _f(0, 512), tag="l3oacc")
    mk.rel(yzc, wsl, zt)

    xnew = mk.take(8)
    ta = mk.take(1)
    for m in range(ND):
        msl = _p(m * 128, (m + 1) * 128)
        kb.dma(xnew[m], FS, mk.xd, msl, tag="l3cx")
        kb.dma(ta, FS, mk.attd, msl, tag="l3ca")
        mk.ew2(Alu.add, xnew[m], FS, xnew[m], FS, ta, FS, stream="V", tag="l3cs2")
    mk.rel(ta)
    return xnew


def _prep(inputs, n_layers):
    f32 = np.float32
    x = np.asarray(inputs["L0_attn_x"], f32)
    com = {}
    if n_layers >= 1:
        com["L0_fc1_q"] = np.ascontiguousarray(np.asarray(inputs["L0_ffn_fc1_q"], np.int32))
        com["L0_fc1_sb"] = np.ascontiguousarray(
            np.repeat(np.asarray(inputs["L0_ffn_fc1_scales"], f32), 128, axis=0))
        com["L0_fc1_b"] = np.ascontiguousarray(np.asarray(inputs["L0_ffn_fc1_bias"], f32).reshape(-1, 1))
        com["L0_fc2_q"] = np.ascontiguousarray(np.asarray(inputs["L0_ffn_fc2_q"], np.int32))
        com["L0_fc2_sb"] = np.ascontiguousarray(
            np.repeat(np.asarray(inputs["L0_ffn_fc2_scales"], f32), 128, axis=0))
        com["L0_fc2_b"] = np.ascontiguousarray(np.asarray(inputs["L0_ffn_fc2_bias"], f32).reshape(-1, 1))
        com["L0_ln_g"] = np.ascontiguousarray(np.asarray(inputs["L0_ffn_ln_gamma"], f32).reshape(-1, 1))
        com["L0_ln_b"] = np.ascontiguousarray(np.asarray(inputs["L0_ffn_ln_beta"], f32).reshape(-1, 1))
    if n_layers >= 2:
        com["L1_WqT"] = np.ascontiguousarray(np.asarray(inputs["L1_attn_Wq"], f32).T)
        com["L1_WkT"] = np.ascontiguousarray(np.asarray(inputs["L1_attn_Wk"], f32).T)
        com["L1_WvT"] = np.ascontiguousarray(np.asarray(inputs["L1_attn_Wv"], f32).T)
        com["L1_WpT"] = np.ascontiguousarray(np.asarray(inputs["L1_attn_Wp"], f32).T)
        com["L1_bp"] = np.ascontiguousarray(np.asarray(inputs["L1_attn_bp"], f32).reshape(-1, 1))
        com["L1_bng"] = np.ascontiguousarray(np.asarray(inputs["L1_attn_bn_gamma"], f32).reshape(-1, 1))
        com["L1_bnb"] = np.ascontiguousarray(np.asarray(inputs["L1_attn_bn_beta"], f32).reshape(-1, 1))
        com["L1_bnm"] = np.ascontiguousarray(np.asarray(inputs["L1_attn_bn_mean"], f32).reshape(-1, 1))
        com["L1_bnv"] = np.ascontiguousarray(np.asarray(inputs["L1_attn_bn_var"], f32).reshape(-1, 1))
        com["L1_clip_x"] = np.ascontiguousarray(np.asarray(inputs["L1_ffn_clip_x"], f32).reshape(1, 1))
        com["L1_clip_d"] = np.ascontiguousarray(np.asarray(inputs["L1_ffn_clip_down"], f32).reshape(1, 1))
        com["L1_wgN"] = np.ascontiguousarray(np.asarray(inputs["L1_ffn_w_gate"], f32))
        com["L1_wuN"] = np.ascontiguousarray(np.asarray(inputs["L1_ffn_w_up"], f32))
        com["L1_wdN"] = np.ascontiguousarray(np.asarray(inputs["L1_ffn_w_down"], f32))
        com["L1_wgT"] = np.ascontiguousarray(np.asarray(inputs["L1_ffn_w_gate"], f32).T)
        com["L1_wuT"] = np.ascontiguousarray(np.asarray(inputs["L1_ffn_w_up"], f32).T)
        com["L1_wdT"] = np.ascontiguousarray(np.asarray(inputs["L1_ffn_w_down"], f32).T)
    if n_layers >= 3:
        com["L2_WqdN"] = np.ascontiguousarray(np.asarray(inputs["L2_attn_Wq_down"], f32))
        com["L2_WkdN"] = np.ascontiguousarray(np.asarray(inputs["L2_attn_Wk_down"], f32))
        com["L2_WvdN"] = np.ascontiguousarray(np.asarray(inputs["L2_attn_Wv_down"], f32))
        com["L2_upqc"] = np.ascontiguousarray(np.asarray(inputs["L2_attn_Wq_up_c"], f32))
        com["L2_upqe"] = np.ascontiguousarray(np.asarray(inputs["L2_attn_Wq_up_e"], f32))
        com["L2_upkc"] = np.ascontiguousarray(np.asarray(inputs["L2_attn_Wk_up_c"], f32))
        com["L2_upke"] = np.ascontiguousarray(np.asarray(inputs["L2_attn_Wk_up_e"], f32))
        com["L2_upv"] = np.ascontiguousarray(np.asarray(inputs["L2_attn_Wv_up"], f32))
        com["L2_upqeT"] = np.ascontiguousarray(np.asarray(inputs["L2_attn_Wq_up_e"], f32).T)
        com["L2_upkeT"] = np.ascontiguousarray(np.asarray(inputs["L2_attn_Wk_up_e"], f32).T)
        com["L2_WcN"] = np.ascontiguousarray(np.asarray(inputs["L2_attn_Wc"], f32))
        com["L2_gwT"] = np.ascontiguousarray(np.asarray(inputs["L2_ffn_gate_w"], f32).T)
        com["L2_bb"] = np.ascontiguousarray(np.asarray(inputs["L2_ffn_balance_bias"], f32).reshape(-1, 1))
        com["L2_w1T"] = np.ascontiguousarray(
            np.asarray(inputs["L2_ffn_w1"], f32).transpose(0, 2, 1).reshape(8 * 1024, 2048))
        com["L2_w2T"] = np.ascontiguousarray(
            np.asarray(inputs["L2_ffn_w2"], f32).transpose(0, 2, 1).reshape(8 * 2048, 1024))
    if n_layers >= 4:
        com["L3_WinN"] = np.ascontiguousarray(np.asarray(inputs["L3_attn_W_in"], f32))
        cw = np.asarray(inputs["L3_attn_conv_w"], f32)
        for kk in range(4):
            com[f"L3_cw{kk}"] = np.ascontiguousarray(cw[:, kk].reshape(-1, 1))
        com["L3_cb"] = np.ascontiguousarray(np.asarray(inputs["L3_attn_conv_b"], f32).reshape(-1, 1))
        com["L3_WxN"] = np.ascontiguousarray(np.asarray(inputs["L3_attn_W_x"], f32))
        com["L3_wdt"] = np.ascontiguousarray(np.asarray(inputs["L3_attn_W_dt"], f32))
        com["L3_bdt"] = np.ascontiguousarray(np.asarray(inputs["L3_attn_b_dt"], f32).reshape(-1, 1))
        com["L3_alog"] = np.ascontiguousarray(np.asarray(inputs["L3_attn_A_log"], f32))
        com["L3_D"] = np.ascontiguousarray(np.asarray(inputs["L3_attn_D"], f32).reshape(-1, 1))
        com["L3_WoutN"] = np.ascontiguousarray(np.asarray(inputs["L3_attn_W_out"], f32))
        com["L3_wqT"] = np.ascontiguousarray(np.asarray(inputs["L3_ffn_weight_q"]).T)
        com["L3_ws"] = np.ascontiguousarray(np.asarray(inputs["L3_ffn_weight_scale"], f32).reshape(-1, 1))
        com["L3_fb"] = np.ascontiguousarray(np.asarray(inputs["L3_ffn_bias"], f32).reshape(-1, 1))
    in_maps = []
    for b in range(8):
        m = dict(com)
        m["x_f"] = np.ascontiguousarray(x[b].T)
        in_maps.append(m)
    return in_maps


def _build(n_layers, dbg=False, layers=None):
    nc = bass.Bass("TRN2", target_bir_lowering=False, detect_race_conditions=False)
    stack = ExitStack()
    with stack:
        kb = KB(nc, stack)
        mk = MK(kb, n_layers, list(range(8)), dbg=dbg)
        x8 = mk.take(8)
        for k in range(ND):
            kb.dma(x8[k], FS, mk.x_in, _p(k * 128, (k + 1) * 128), tag="xin")
        emits = [emit_layer0, emit_layer1, emit_layer2, emit_layer3]
        if layers is None:
            layers = list(range(n_layers))
        for li in layers:
            x8 = emits[li](mk, x8)
        for k in range(ND):
            kb.dma(mk.y_out, _p(k * 128, (k + 1) * 128), x8[k], FS, tag="yout")
        block = stack.enter_context(nc.Block())
        kb.finalize(block)
    return nc


def _run(inputs, n_layers=4, dbg=False, layers=None):
    in_maps = _prep(inputs, n_layers)
    nc = _build(n_layers, dbg=dbg, layers=layers)
    res = run_bass_kernel_spmd(nc, in_maps, core_ids=list(range(8)))
    if dbg:
        return res
    return np.stack([r["y"] for r in res.results], axis=0)


def kernel(**inputs):
    yf = _run(inputs, n_layers=4)
    return np.ascontiguousarray(np.transpose(yf, (0, 2, 1)))
